# version 01
# baseline (speedup 1.0000x reference)
import numpy as np
import ml_dtypes
import concourse.bass as bass
import concourse.mybir as mybir
from concourse.bass_utils import run_bass_kernel_spmd
from concourse.bass_core_rust import IndirectOffsetOnAxis

F32 = mybir.dt.float32
BF16 = mybir.dt.bfloat16
I32 = mybir.dt.int32
U8 = mybir.dt.uint8
AF = mybir.ActivationFunctionType
ALU = mybir.AluOpType
AX = mybir.AxisListType

B, T, M, NH, DH = 4, 2048, 2048, 16, 128
II = 5632
KT = T // 128
NJ = 8
NTOK = NJ * 128
NIC = II // 128
IBLK = 512
NIB = II // IBLK
RADD = 0x80000
RMASK = -0x100000

_DEBUG_STOP = None
_DEBUG_HEADS = NH
_DEBUG_TILES = KT
_DEBUG_SUB = 3
_DEBUG_BSUB = 9
_STRICT = [False]


def _host_tables(p):
    own = [2 * j + p for j in range(NJ)]
    other = [t for t in range(KT) if t not in own]
    tokperm = np.concatenate([np.arange(t * 128, (t + 1) * 128)
                              for t in own + other])
    pos = tokperm.astype(np.float64)
    dims = np.arange(DH // 2, dtype=np.float64)
    ang = np.power(10000.0, -dims / (DH // 2))
    rad = pos[None, :] * ang[:, None]
    cos = np.cos(rad)
    sin = np.sin(rad)
    cosF = np.concatenate([cos, cos], 0).astype(ml_dtypes.bfloat16)
    sinFs = np.concatenate([-sin, sin], 0).astype(ml_dtypes.bfloat16)
    tri = (np.arange(128)[:, None] <= np.arange(128)[None, :]).astype(np.float32)
    mk = np.empty((KT, 128, 128), np.float32)
    for m in range(KT):
        mk[m] = tri if m < NJ else (0.0 if p == 0 else 1.0)
    swap = np.zeros((128, 128), np.float32)
    swap[:64, 64:] = np.eye(64)
    swap[64:, :64] = np.eye(64)
    return (tokperm, cosF, sinFs, mk.astype(ml_dtypes.bfloat16),
            swap.astype(ml_dtypes.bfloat16),
            np.eye(128, dtype=ml_dtypes.bfloat16))


def build_program():
    nc = bass.Bass("TRN2", target_bir_lowering=False, debug=False)

    xb = nc.dram_tensor("xb", [T, M], F32, kind="ExternalInput").ap()
    w_aq = nc.dram_tensor("w_aq", [NH, M, DH], F32, kind="ExternalInput").ap()
    w_ak = nc.dram_tensor("w_ak", [NH, M, DH], F32, kind="ExternalInput").ap()
    w_av = nc.dram_tensor("w_av", [NH, M, DH], F32, kind="ExternalInput").ap()
    w_ao = nc.dram_tensor("w_ao", [NH, DH, M], F32, kind="ExternalInput").ap()
    wg_d = nc.dram_tensor("wg", [II, M], F32, kind="ExternalInput").ap()
    wu_d = nc.dram_tensor("wu", [II, M], F32, kind="ExternalInput").ap()
    wd_d = nc.dram_tensor("wd", [M, II], F32, kind="ExternalInput").ap()
    pi_d = nc.dram_tensor("pi", [128, NIC], I32, kind="ExternalInput").ap()
    cos_d = nc.dram_tensor("ropecos", [128, T], BF16, kind="ExternalInput").ap()
    sin_d = nc.dram_tensor("ropesin", [128, T], BF16, kind="ExternalInput").ap()
    mk_d = nc.dram_tensor("mk", [KT, 128, 128], BF16, kind="ExternalInput").ap()
    id_d = nc.dram_tensor("ident", [128, 128], BF16, kind="ExternalInput").ap()
    sw_d = nc.dram_tensor("swapm", [128, 128], BF16, kind="ExternalInput").ap()
    idf_d = nc.dram_tensor("identf", [128, 128], F32, kind="ExternalInput").ap()

    out_d = nc.dram_tensor("out", [NTOK, M], F32, kind="ExternalOutput").ap()

    qwg_d = nc.dram_tensor("qwg", [II, M], BF16, kind="Internal").ap()
    qwu_d = nc.dram_tensor("qwu", [II, M], BF16, kind="Internal").ap()
    qwd_d = nc.dram_tensor("qwd", [M, II], BF16, kind="Internal").ap()
    tT_d = nc.dram_tensor("tT", [II, NTOK], F32, kind="Internal").ap()
    qtT_d = nc.dram_tensor("qtT", [II, NTOK], BF16, kind="Internal").ap()
    x1_d = nc.dram_tensor("x1s", [NTOK, M], F32, kind="Internal").ap()

    dbg = {}
    if _DEBUG_STOP is not None:
        for nm, shp in [("lnxT", [16, 128, T]), ("kT", [128, T]),
                        ("vTok", [KT, 128, 128]), ("qT", [128, NTOK]),
                        ("oT", [128, NH * NTOK]), ("x1", [NTOK, M]),
                        ("qx", [NTOK, M]), ("tT", [II, NTOK]),
                        ("qtT", [II, NTOK]), ("st", [128, 32]),
                        ("lnA", [128, M])]:
            dbg[nm] = nc.dram_tensor("dbg_" + nm, shp, F32,
                                     kind="ExternalOutput").ap()
        dbg["lnraw"] = nc.dram_tensor("dbg_lnraw", [16, 128, T], BF16,
                                      kind="ExternalOutput").ap()

    def sba(name, shape, dt):
        return nc.alloc_sbuf_tensor(name, list(shape), dt).ap()

    ident = sba("identc", [128, 128], BF16)
    identf = sba("identfc", [128, 128], F32)
    swapm = sba("swapc", [128, 128], BF16)
    mk = sba("mkc", [128, KT, 128], BF16)
    cosF = sba("cosc", [128, T], BF16)
    sinFs = sba("sinc", [128, T], BF16)
    pi_sb = sba("pic", [128, NIC], I32)
    ones_col = sba("onescol", [128, 1], BF16)
    ones_row = sba("onesrow", [1, 128], BF16)
    wg_sc = sba("wgsc", [128, NIC], F32)
    wu_sc = sba("wusc", [128, NIC], F32)
    wd_sc = sba("wdsc", [128, 16], F32)
    wg_rs = sba("wgrs", [128, NIC], F32)
    wu_rs = sba("wurs", [128, NIC], F32)
    wd_rs = sba("wdrs", [128, 16], F32)
    stats = sba("stats", [128, 32], F32)
    rmx8 = sba("rmx8", [128, 8], F32)
    row16 = sba("row16", [1, 2, 128], BF16)
    negc = sba("negc", [128, 1], F32)
    cm16 = sba("cm16", [128, 2], BF16)
    kT = sba("kT", [128, T], BF16)
    kvraw = sba("kvraw", [128, T], BF16)
    kTr, vT = kvraw, kvraw
    qTr = kvraw[:, 0:NTOK]
    vTok2 = sba("vTok2", [128, KT, 256], BF16)
    qT = sba("qT", [128, NTOK], BF16)
    pt = sba("pt", [128, 2, NTOK], BF16)

    def region(name, nbytes):
        return nc.alloc_sbuf_tensor(name, [128, nbytes], U8).ap()

    def view(reg, o, shape, dt):
        esz = {F32: 4, BF16: 2, I32: 4}[dt]
        r = reg.bitcast(dt)
        flat = 1
        for s in shape[1:]:
            flat *= s
        v = r[:, o // esz:o // esz + flat]
        if len(shape) == 2:
            return v
        pat = "p (" + " ".join(f"d{i}" for i in range(1, len(shape))) + ") ->" \
              + " p " + " ".join(f"d{i}" for i in range(1, len(shape)))
        kw = {f"d{i}": shape[i] for i in range(1, len(shape) - 1)}
        return v.rearrange(pat, **kw)

    RG1 = region("RG1", 65536)
    lnxT = view(RG1, 0, [128, 16, T], BF16)
    x1_sb = view(RG1, 0, [128, NJ, M], F32)
    wgT = view(RG1, 0, [128, 2, 16, IBLK], BF16)
    wuT = view(RG1, 32768, [128, 2, 16, IBLK], BF16)
    RG2 = region("RG2", 32768)
    oT = view(RG2, 0, [128, NH, NTOK], BF16)
    qxT = view(RG2, 0, [128, 16, NTOK], BF16)
    RG3 = region("RG3", 36864)
    xst = view(RG3, 0, [128, 2, M], F32)
    lnA = view(RG3, 16384, [128, M], F32)
    wq16 = view(RG3, 0, [128, 2, 16, DH], BF16)
    wk16 = view(RG3, 8192, [128, 2, 16, DH], BF16)
    wv2 = view(RG3, 16384, [128, 16, 2, 128], BF16)
    linv = view(RG3, 24576, [128, NTOK], F32)
    rowB = view(RG3, 28672, [1, 1024], F32)
    rowA16 = view(RG3, 32768, [1, 1024], BF16)
    rowA = view(RG3, 34816, [1, 512], F32)
    wao16 = view(RG3, 0, [128, 2, 512], BF16)
    xres = view(RG3, 2048, [128, 2, 512], F32)
    sqx = view(RG3, 0, [128, M], F32)
    lnf = view(RG3, 8192, [128, M], F32)
    tstg = view(RG3, 16384, [128, 2, NTOK], F32)
    gs_t = view(RG3, 24576, [128, NTOK], F32)
    mx_t = view(RG3, 28672, [128, NTOK], F32)
    mn_t = view(RG3, 32768, [128, NTOK], F32)
    lnsq = view(RG3, 24576, [128, M], F32)
    st_b = view(RG3, 28672, [128, NTOK], F32)
    str_b = view(RG3, 32768, [128, NTOK], F32)
    csc_b = view(RG3, 16384, [128, M], F32)
    RG4 = region("RG4", 28672)
    wst = view(RG4, 0, [128, 2, 2816], F32)
    tg = view(RG4, 0, [128, 2, NTOK], F32)
    tgq = view(RG4, 8192, [128, 2, NTOK], F32)
    qt16 = view(RG4, 16384, [128, 2, NTOK], BF16)
    wdTs = view(RG4, 20480, [128, 2, 512], BF16)
    x1st = view(RG4, 22528, [128, 2, 512], F32)
    qts = view(RG4, 24576, [128, 2, NTOK], BF16)
    yev = view(RG4, 0, [128, 2, 512], F32)

    PA = nc.alloc_psum_tensor("PA", [128, 1024], F32).ap()
    PB = nc.alloc_psum_tensor("PB", [128, 1024], F32).ap()
    PC = nc.alloc_psum_tensor("PC", [128, 1024], F32).ap()
    PDh = nc.alloc_psum_tensor("PDh", [128, 512], F32).ap()
    PF = nc.alloc_psum_tensor("PF", [1, 128], F32).ap()

    _GUARD = {"tensor_copy", "tensor_tensor", "tensor_add", "tensor_sub",
              "tensor_mul", "tensor_max", "tensor_scalar",
              "tensor_single_scalar", "tensor_scalar_add",
              "tensor_scalar_mul", "tensor_scalar_max", "tensor_scalar_min",
              "tensor_reduce", "reduce_max", "reduce_sum", "reciprocal",
              "memset", "activation", "square", "sqrt", "copy", "mul",
              "add", "select"}

    class _EG:
        def __init__(self, e):
            self._e = e
            self.last = None

        def __getattr__(self, name):
            f = getattr(self._e, name)
            if name in _GUARD:
                def g(*a, **k):
                    fs = 1024
                    try:
                        shp = a[0].shape
                        fs = 1
                        for x in shp[1:]:
                            fs *= x
                    except Exception:
                        pass
                    if _STRICT[0] or fs < 512:
                        self._e.drain()
                    else:
                        self._e.nop()
                    r = f(*a, **k)
                    self.last = r
                    return r
                return g
            return f

    class _TG:
        def __init__(self, e):
            self._e = e
            self._mm = 0
            self.last = None

        def __getattr__(self, name):
            f = getattr(self._e, name)
            if name == "transpose":
                def g(*a, **k):
                    self._e.nop()
                    r = f(*a, **k)
                    self._e.nop()
                    self.last = r
                    return r
                return g
            if name == "matmul":
                def g(*a, **k):
                    if self._mm >= 6:
                        self._e.drain()
                        self._mm = 0
                    self._mm += 1
                    r = f(*a, **k)
                    self.last = r
                    return r
                return g
            if name == "drain":
                def g(*a, **k):
                    self._mm = 0
                    return f(*a, **k)
                return g
            return f

    sy = nc.sync
    gp = _EG(nc.gpsimd)
    ve = _EG(nc.vector)
    sc = _EG(nc.scalar)
    te = _TG(nc.tensor)
    sems = {k: nc.alloc_semaphore(k) for k in ("P", "V", "S")}
    n = {k: 0 for k in sems}
    sP, sV, sS = (sems[k] for k in ("P", "V", "S"))

    class DStream:
        def __init__(self, name, nsub=1, eng=None):
            self.sems = [nc.alloc_semaphore(name + str(i))
                         for i in range(nsub)]
            self.n = [0] * nsub
            self.eng = eng if eng is not None else sy

        def go(self, out, in_, sub=0, transpose=False):
            if transpose:
                r = self.eng.dma_start_transpose(out, in_)
            else:
                r = self.eng.dma_start(out=out, in_=in_)
            r.then_inc(self.sems[sub], 16)
            self.n[sub] += 16

        def gather(self, out, in_, idx, sub=0):
            self.eng.indirect_dma_start(
                out, None, in_, IndirectOffsetOnAxis(idx, 0)) \
                .then_inc(self.sems[sub], 16)
            self.n[sub] += 16

        def wait(self, eng, sub=0):
            eng.wait_ge(self.sems[sub], self.n[sub])

        def wait_all(self, eng):
            for i, s in enumerate(self.sems):
                eng.wait_ge(s, self.n[i])

    Dc = DStream("Dc")
    Dx = DStream("Dx", 2)
    Dw = DStream("Dw", 2)
    Gq = DStream("Gq", 2, gp)
    Gw = DStream("Gw", 2, gp)
    Gv = DStream("Gv", 1, gp)
    Ga = DStream("Ga", 2, gp)
    Dxr = DStream("Dxr", 2)
    Dx1 = DStream("Dx1")
    Dwg = DStream("Dwg", 2)
    Dt = DStream("Dt", 2)
    Gg = DStream("Gg", 2, gp)
    Dqw = DStream("Dqw", 2)
    Dqs = DStream("Dqs", 2)
    Dwd = DStream("Dwd", 2)
    Dxs = DStream("Dxs", 2)
    O = DStream("O", 2)
    Ddbg = DStream("Ddbg")
    streams = [Dc, Dx, Dw, Gq, Gw, Gv, Ga, Dxr, Dx1, Dwg, Dt, Gg, Dqw,
               Dqs, Dwd, Dxs, O, Ddbg]

    def sig(eng, key):
        eng.last.then_inc(sems[key], 1)
        n[key] += 1
        return n[key]

    def finish():
        for e in (sy, gp, ve, sc, te):
            O.wait_all(e)
        return nc, dbg

    for st_ in streams:
        for s in st_.sems:
            sy.sem_clear(s)
    for s in sems.values():
        sy.sem_clear(s)
    nc.all_engine_barrier()
    Dc.go(ident, id_d)
    Dc.go(swapm, sw_d)
    Dc.go(identf, idf_d)
    Dc.go(mk, mk_d.rearrange("k p q -> p k q"))
    Dc.go(cosF, cos_d)
    Dc.go(sinFs, sin_d)
    Dc.go(pi_sb, pi_d)
    Dc.wait(ve)
    if _DEBUG_STOP is not None:
        for rg_ in (RG1, RG2, RG3, RG4):
            ve.memset(rg_.bitcast(F32), 0.0)
        ve.memset(stats, 0.0)
        ve.memset(rmx8, 0.0)
    ve.memset(ones_col, 1.0)
    ve.memset(ones_row, 1.0)
    v0 = sig(ve, "V")
    for e in (te, sc, gp, sy):
        e.wait_ge(sV, v0)

    def pe_fence():
        return sig(te, "P")

    def tr_f32(src_f32, ncols, write_fn):
        slots = [(PA, 0), (PA, 512), (PB, 0), (PB, 512), (PC, 0), (PC, 512),
                 (PDh, 0)]
        nt = ncols // 128
        done = 0
        while done < nt:
            batch = min(7, nt - done)
            for k in range(batch):
                c = done + k
                P, o = slots[k]
                te.transpose(P[:, o:o + 128].bitcast(F32) if False else
                             P[:, o:o + 128],
                             src_f32[:, c * 128:(c + 1) * 128], identf)
            p1 = pe_fence()
            ve.wait_ge(sP, p1)
            for k in range(batch):
                c = done + k
                P, o = slots[k]
                write_fn(c, P[:, o:o + 128])
            v1 = sig(ve, "V")
            te.wait_ge(sV, v1)
            done += batch

    lnsq_free = [0]
    sqrt_sig = [0]

    def ln_stats(src_ap):
        ve.wait_ge(sS, sqrt_sig[0])
        ve.reduce_sum(stats[:, 0:1], src_ap, axis=AX.X)
        sc.wait_ge(sV, lnsq_free[0])
        sc.square(lnsq, src_ap)
        s1 = sig(sc, "S")
        ve.wait_ge(sS, s1)
        ve.reduce_sum(stats[:, 1:2], lnsq, axis=AX.X)
        lnsq_free[0] = n["V"] + 1
        ve.tensor_scalar_mul(stats[:, 2:3], stats[:, 0:1], 1.0 / M)
        ve.tensor_scalar_mul(stats[:, 3:4], stats[:, 1:2], 1.0 / M)
        ve.tensor_mul(stats[:, 4:5], stats[:, 2:3], stats[:, 2:3])
        ve.tensor_sub(stats[:, 5:6], stats[:, 3:4], stats[:, 4:5])
        ve.tensor_scalar_add(stats[:, 5:6], stats[:, 5:6], 1e-5)
        v1 = sig(ve, "V")
        sc.wait_ge(sV, v1)
        sc.sqrt(stats[:, 6:7], stats[:, 5:6])
        s2 = sig(sc, "S")
        sqrt_sig[0] = s2
        ve.wait_ge(sS, s2)
        ve.reciprocal(stats[:, 7:8], stats[:, 6:7])
        v2 = sig(ve, "V")
        sc.wait_ge(sV, v2)


    s_lnuse = [0] * KT
    for i in range(_DEBUG_TILES):
        par = i % 2
        if i >= 2:
            sy.wait_ge(sV, s_lnuse[i - 2])
        Dx.go(xst[:, par, :], xb[i * 128:(i + 1) * 128, :], sub=par)
        Dx.wait(ve, par)
        Dx.wait(sc, par)
        if _DEBUG_SUB >= 1:
            ln_stats(xst[:, par, :])
        if _DEBUG_SUB >= 2:
            if i >= 1:
                ve.wait_ge(sP, p_lnA[i - 1])
            ve.tensor_scalar(lnA, xst[:, par, :], stats[:, 2:3],
                             stats[:, 7:8], op0=ALU.subtract, op1=ALU.mult)
            s_lnuse[i] = sig(ve, "V")
        if _DEBUG_SUB >= 3:
            te.wait_ge(sV, s_lnuse[i])
            tr_f32(lnA, M,
                   lambda c, src, i=i: ve.tensor_copy(
                       lnxT[:, c, i * 128:(i + 1) * 128], src))
        if i == 0:
            p_lnA = [0] * KT
        p_lnA[i] = n["P"]
    v_endA = n["V"]

    if _DEBUG_STOP == "A":
        sy.wait_ge(sV, v_endA)
        O.go(dbg["st"], stats)
        O.go(dbg["lnA"], lnA)
        for c in range(16):
            O.go(dbg["lnraw"][c], lnxT[:, c, :])
        for c in range(16):
            sc.wait_ge(sV, v_endA)
            O.wait(sc, 0)
            sc.copy(lnf, lnxT[:, c, :])
            s1 = sig(sc, "S")
            sy.wait_ge(sS, s1)
            O.go(dbg["lnxT"][c], lnf)
        return finish()

    w_units = []
    for t in range(II // 128):
        w_units.append(("g", t))
        w_units.append(("u", t))
    for t in range(16):
        w_units.append(("dmax", t))
        for q in range(4):
            w_units.append(("d", t, q))
    wpar = [0]
    w_free = {0: None, 1: None}
    w_dmax_v = {}

    def w_wait_free(par):
        g = w_free[par]
        if g is None:
            return
        if g[0] == "V":
            sy.wait_ge(sV, g[1])
        else:
            Gq.wait(sy, par)

    def w_round_store(par, width, scale_col, dst):
        sc.activation(wst[:, par, 0:width], wst[:, par, 0:width], AF.Copy,
                      scale=scale_col)
        s1 = sig(sc, "S")
        ve.wait_ge(sS, s1)
        iv = wst[:, par, 0:width].bitcast(I32)
        ve.tensor_scalar(iv, iv, RADD, None, op0=ALU.add)
        ve.tensor_scalar(iv, iv, RMASK, None, op0=ALU.bitwise_and)
        v2 = sig(ve, "V")
        gp.wait_ge(sV, v2)
        Gq.go(dst, wst[:, par, 0:width], sub=par)
        w_free[par] = ("Gq", par)

    def emit_w_unit(u):
        if u[0] in ("g", "u"):
            _, t = u
            src, dst = (wg_d, qwg_d) if u[0] == "g" else (wu_d, qwu_d)
            scl = wg_sc if u[0] == "g" else wu_sc
            rsl = wg_rs if u[0] == "g" else wu_rs
            par = wpar[0] % 2
            wpar[0] += 1
            w_wait_free(par)
            Dw.go(wst[:, par, 0:2048], src[t * 128:(t + 1) * 128, :], sub=par)
            Dw.wait(ve, par)
            ve.reduce_max(stats[:, 10:11], wst[:, par, 0:2048], axis=AX.X,
                          apply_absolute_value=True)
            ve.reciprocal(stats[:, 11:12], stats[:, 10:11])
            ve.tensor_scalar_mul(scl[:, t:t + 1], stats[:, 11:12], 448.0)
            ve.tensor_scalar_mul(rsl[:, t:t + 1], stats[:, 10:11], 1 / 448.0)
            v1 = sig(ve, "V")
            sc.wait_ge(sV, v1)
            w_round_store(par, 2048, scl[:, t:t + 1],
                          dst[t * 128:(t + 1) * 128, :])
        elif u[0] == "dmax":
            _, t = u
            for qq in range(4):
                par = wpar[0] % 2
                wpar[0] += 1
                w_wait_free(par)
                Dw.go(wst[:, par, 0:1408],
                      wd_d[t * 128:(t + 1) * 128, qq * 1408:(qq + 1) * 1408],
                      sub=par)
                Dw.wait(ve, par)
                ve.reduce_max(stats[:, 12 + qq:13 + qq], wst[:, par, 0:1408],
                              axis=AX.X, apply_absolute_value=True)
                w_free[par] = ("V", sig(ve, "V"))
            ve.tensor_max(stats[:, 12:13], stats[:, 12:13], stats[:, 13:14])
            ve.tensor_max(stats[:, 14:15], stats[:, 14:15], stats[:, 15:16])
            ve.tensor_max(stats[:, 12:13], stats[:, 12:13], stats[:, 14:15])
            ve.reciprocal(stats[:, 13:14], stats[:, 12:13])
            ve.tensor_scalar_mul(wd_sc[:, t:t + 1], stats[:, 13:14], 448.0)
            ve.tensor_scalar_mul(wd_rs[:, t:t + 1], stats[:, 12:13], 1 / 448.0)
            w_dmax_v[t] = sig(ve, "V")
        else:
            _, t, q = u
            par = wpar[0] % 2
            wpar[0] += 1
            w_wait_free(par)
            Dw.go(wst[:, par, 0:1408],
                  wd_d[t * 128:(t + 1) * 128, q * 1408:(q + 1) * 1408],
                  sub=par)
            Dw.wait(sc, par)
            sc.wait_ge(sV, w_dmax_v[t])
            w_round_store(par, 1408, wd_sc[:, t:t + 1],
                          qwd_d[t * 128:(t + 1) * 128,
                                q * 1408:(q + 1) * 1408])

    p_qproj_prev = [0]

    def fetch_head_w(h):
        hp = h % 2
        if h >= 2:
            gp.wait_ge(sP, p_qproj_prev[0])
        Gw.go(wq16[:, hp, :, :],
              w_aq[h].rearrange("(c p) d -> p c d", p=128), sub=hp)
        Gw.go(wk16[:, hp, :, :],
              w_ak[h].rearrange("(c p) d -> p c d", p=128), sub=hp)

    def fetch_wv2(h):
        Gv.go(wv2[:, :, 0, :], w_av[h].rearrange("(c p) d -> p c d", p=128))
        Gv.go(wv2[:, :, 1, :],
              w_av[h + 1].rearrange("(c p) d -> p c d", p=128))

    gp.wait_ge(sV, v_endA)
    fetch_head_w(0)
    WPH = (len(w_units) + NH - 1) // NH

    def proj(dst16, wsb, hp, psum, ofs):
        for mc in range(16):
            first, last = mc == 0, mc == 15
            for c0 in (0, 512):
                te.matmul(psum[:, c0:c0 + 512], wsb[:, hp, mc, :],
                          lnxT[:, mc, ofs + c0:ofs + c0 + 512],
                          start=first, stop=last)
        p1 = sig(te, "P")
        sc.wait_ge(sP, p1)
        sc.copy(dst16[:, ofs:ofs + 1024], psum[:, 0:1024])
        s1 = sig(sc, "S")
        te.wait_ge(sS, s1)
        return s1

    s_v2 = 0
    p_v2mm = 0
    pv_done_prev = 0
    for h in range(_DEBUG_HEADS):
        hp = h % 2
        if h + 1 < NH:
            fetch_head_w(h + 1)
        Gw.wait(te, hp)
        if hp == 0 and _DEBUG_BSUB >= 1:
            if h >= 2:
                gp.wait_ge(sP, p_v2mm)
            fetch_wv2(h)
            Gv.wait(te)
            te.wait_ge(sS, s_v2)
            if h >= 2:
                te.wait_ge(sP, pv_done_prev)
            sv2_evac = [0] * KT
            for tt in range(KT):
                VP = PDh if tt % 2 == 0 else PC
                if tt >= 2:
                    te.wait_ge(sS, sv2_evac[tt - 2])
                for hh in range(2):
                    for mc in range(16):
                        te.matmul(VP[:, hh * 128:hh * 128 + 128],
                                  lnxT[:, mc, tt * 128:(tt + 1) * 128],
                                  wv2[:, mc, hh, :],
                                  start=(mc == 0), stop=(mc == 15))
                p1 = sig(te, "P")
                sc.wait_ge(sP, p1)
                sc.copy(vTok2[:, tt, :], VP[:, 0:256])
                sv2_evac[tt] = sig(sc, "S")
            s_v2 = sv2_evac[KT - 1]
            p_v2mm = n["P"]
            te.wait_ge(sS, s_v2)
        if _DEBUG_BSUB < 2:
            continue
        proj(kTr, wk16, hp, PA, 0)
        sk1 = proj(kTr, wk16, hp, PB, 1024)
        for c0 in range(0, T, 512):
            te.matmul((PA if c0 < 1024 else PB)[:, c0 % 1024:c0 % 1024 + 512],
                      swapm, kTr[:, c0:c0 + 512], start=True, stop=True)
        pswap = sig(te, "P")
        ve.wait_ge(sS, sk1)
        ve.tensor_mul(kT, kTr, cosF)
        ve.wait_ge(sP, pswap)
        ve.tensor_mul(kTr[:, 0:1024], PA, sinFs[:, 0:1024])
        ve.tensor_mul(kTr[:, 1024:2048], PB, sinFs[:, 1024:2048])
        ve.tensor_add(kT, kT, kTr)
        ve.reduce_max(stats[:, 16:17], kT, axis=AX.X,
                      apply_absolute_value=True)
        vk = sig(ve, "V")
        te.wait_ge(sV, vk)
        if _DEBUG_BSUB < 3:
            continue
        sc.wait_ge(sV, vk)
        sq0 = proj(qTr, wq16, hp, PA, 0)
        p_qproj_prev[0] = n["P"]
        for c0 in (0, 512):
            te.matmul(PA[:, c0:c0 + 512], swapm, qTr[:, c0:c0 + 512],
                      start=True, stop=True)
        pswq = sig(te, "P")
        ve.wait_ge(sS, sq0)
        ve.tensor_mul(qT, qTr, cosF[:, 0:NTOK])
        ve.wait_ge(sP, pswq)
        ve.tensor_mul(qTr, PA, sinFs[:, 0:NTOK])
        ve.tensor_add(qT, qT, qTr)
        ve.reduce_max(stats[:, 17:18], qT, axis=AX.X,
                      apply_absolute_value=True)
        vq = sig(ve, "V")
        te.wait_ge(sV, vq)
        sc.wait_ge(sV, vq)
        sc.copy(cm16[:, 0:1], stats[:, 16:17])
        sc.copy(cm16[:, 1:2], stats[:, 17:18])
        scm = sig(sc, "S")
        te.wait_ge(sS, scm)
        te.matmul(PA[0:1, 0:1], cm16[:, 0:1], cm16[:, 1:2],
                  start=True, stop=True)
        pc1 = sig(te, "P")
        sc.wait_ge(sP, pc1)
        sc.copy(rowA16[0:1, 0:1], PA[0:1, 0:1])
        sc1 = sig(sc, "S")
        te.wait_ge(sS, sc1)
        te.matmul(PA[:, 0:1], ones_row, rowA16[0:1, 0:1],
                  start=True, stop=True)
        pc2 = sig(te, "P")
        sc.wait_ge(sP, pc2)
        sc.activation(negc, PA[:, 0:1], AF.Copy, scale=-1.0 / 128.0)
        snegc = sig(sc, "S")
        te.wait_ge(sS, snegc)

        if _DEBUG_BSUB < 4:
            continue
        exp_done = [0] * KT
        pv_done = [0] * KT
        for m in range(KT):
            j0 = m if m < NJ else m - NJ
            q0 = j0 * 128
            SP = PC
            if m >= 1:
                te.wait_ge(sS, exp_done[m - 1])
            c0 = q0
            while c0 < NTOK:
                ce = min((c0 // 512 + 1) * 512, NTOK)
                te.matmul(SP[:, c0:ce], kT[:, m * 128:(m + 1) * 128],
                          qT[:, c0:ce], start=True, stop=True)
                c0 = ce
            ps = sig(te, "P")
            sc.wait_ge(sP, ps)
            if m >= 2:
                sc.wait_ge(sP, pv_done[m - 2])
            sc.activation(pt[:, m % 2, q0:NTOK], SP[:, q0:NTOK], AF.Exp,
                          bias=negc, scale=1.0 / 128.0)
            exp_done[m] = sig(sc, "S")
            ve.wait_ge(sS, exp_done[m])
            ve.tensor_mul(pt[:, m % 2, q0:q0 + 128],
                          pt[:, m % 2, q0:q0 + 128], mk[:, m, :])
            vm1 = sig(ve, "V")
            te.wait_ge(sV, vm1)
            c0 = q0
            while c0 < NTOK:
                ce = min((c0 // 512 + 1) * 512, NTOK)
                stop = (m == 11) if c0 < 512 else (m == 15)
                te.matmul(PA[0:1, c0:ce], ones_col, pt[:, m % 2, c0:ce],
                          start=(m == 0), stop=stop)
                te.matmul(PB[:, c0:ce],
                          vTok2[:, m, hp * 128:(hp + 1) * 128],
                          pt[:, m % 2, c0:ce],
                          start=(m == 0), stop=stop)
                c0 = ce
            pv_done[m] = sig(te, "P")
        pv_done_prev = pv_done[KT - 1]
        if _DEBUG_BSUB < 5:
            continue
        sc.wait_ge(sP, pv_done[KT - 1])
        sc.copy(rowA[0:1, 0:512], PA[0:1, 0:512])
        sc.copy(rowB[0:1, 0:512], PA[0:1, 512:1024])
        sl1 = sig(sc, "S")
        ve.wait_ge(sS, sl1)
        ve.reciprocal(rowB[0:1, 512:1024], rowB[0:1, 0:512])
        ve.reciprocal(rowA[0:1, 0:512], rowA[0:1, 0:512])
        ve.tensor_copy(rowA16[0:1, 0:512], rowA[0:1, 0:512])
        ve.tensor_copy(rowA16[0:1, 512:1024], rowB[0:1, 512:1024])
        vl1 = sig(ve, "V")
        te.wait_ge(sV, vl1)
        te.matmul(PC[:, 0:512], ones_row, rowA16[0:1, 0:512],
                  start=True, stop=True)
        te.matmul(PC[:, 512:1024], ones_row, rowA16[0:1, 512:1024],
                  start=True, stop=True)
        pl1 = sig(te, "P")
        sc.wait_ge(sP, pl1)
        sc.copy(linv, PC)
        sl2 = sig(sc, "S")
        ve.wait_ge(sS, sl2)
        ve.tensor_mul(oT[:, h, :], PB, linv)
        voT = sig(ve, "V")
        te.wait_ge(sV, voT)
        pv_done_prev = pv_done[KT - 1]
        for u in w_units[h * WPH:(h + 1) * WPH]:
            emit_w_unit(u)

    if _DEBUG_STOP == "B":
        v = n["V"]
        sc.wait_ge(sV, v)
        Gw.wait_all(sc)
        Gv.wait_all(sc)
        sc.copy(sqx[:, 0:T], kT)
        s1 = sig(sc, "S")
        sy.wait_ge(sS, s1)
        O.go(dbg["kT"], sqx[:, 0:T])
        for c in range(KT):
            O.wait(sc, 0)
            sc.copy(lnf[:, 0:128], vTok2[:, c, 0:128] if _DEBUG_HEADS % 2
                    else vTok2[:, c, 128:256])
            s1 = sig(sc, "S")
            sy.wait_ge(sS, s1)
            O.go(dbg["vTok"][c], lnf[:, 0:128])
        O.wait(sc, 0)
        sc.copy(sqx[:, 0:NTOK], qT)
        s1 = sig(sc, "S")
        sy.wait_ge(sS, s1)
        O.go(dbg["qT"], sqx[:, 0:NTOK])
        for hh in range(_DEBUG_HEADS):
            O.wait(sc, 0)
            sc.copy(lnf[:, 0:NTOK], oT[:, hh, :])
            s1 = sig(sc, "S")
            sy.wait_ge(sS, s1)
            O.go(dbg["oT"][:, hh * NTOK:(hh + 1) * NTOK], lnf[:, 0:NTOK])
        return finish()

    def jslice4(jj):
        return (PA, PB)[jj // 2][:, (jj % 2) * 512:(jj % 2) * 512 + 512]

    p_b2pre = n["P"]
    p_cch = {}
    v_ev = {0: 0, 1: 0}
    k = 0
    for jh in range(2):
        for mcol in range(4):
            for cch in range(16):
                par = cch % 2
                if k >= 2:
                    gp.wait_ge(sP, p_cch[k - 2])
                Ga.go(wao16[:, par, :],
                      w_ao[cch, :, mcol * 512:(mcol + 1) * 512], sub=par)
                Ga.wait(te, par)
                if k == 0:
                    te.wait_ge(sP, p_b2pre)
                for jj in range(4):
                    j = jh * 4 + jj
                    te.matmul(jslice4(jj), oT[:, cch, j * 128:(j + 1) * 128],
                              wao16[:, par, :],
                              start=(cch == 0), stop=(cch == 15))
                p_cch[k] = sig(te, "P")
                k += 1
            pj = p_cch[k - 1]
            for jj in range(4):
                j = jh * 4 + jj
                par = jj % 2
                sy.wait_ge(sV, v_ev[par])
                Dxr.go(xres[:, par, :],
                       xb[j * 128:(j + 1) * 128,
                          mcol * 512:(mcol + 1) * 512], sub=par)
                Dxr.wait(ve, par)
                ve.wait_ge(sP, pj)
                ve.tensor_add(x1_sb[:, j, mcol * 512:(mcol + 1) * 512],
                              jslice4(jj), xres[:, par, :])
                v_ev[par] = sig(ve, "V")
            te.wait_ge(sV, v_ev[par])
    v_x1 = n["V"]

    if _DEBUG_STOP == "B2":
        sy.wait_ge(sV, v_x1)
        for j in range(NJ):
            O.go(dbg["x1"][j * 128:(j + 1) * 128, :], x1_sb[:, j, :])
        return finish()

    s_qx = [0] * NJ
    for j in range(NJ):
        ve.wait_ge(sV, v_x1)
        sc.wait_ge(sV, v_x1)
        ln_stats(x1_sb[:, j, :])
        ve.tensor_scalar(lnf, x1_sb[:, j, :], stats[:, 2:3],
                         stats[:, 7:8], op0=ALU.subtract, op1=ALU.mult)
        ve.reduce_max(stats[:, 10:11], lnf, axis=AX.X,
                      apply_absolute_value=True)
        ve.reciprocal(stats[:, 11:12], stats[:, 10:11])
        ve.tensor_scalar_mul(stats[:, 12:13], stats[:, 11:12], 448.0)
        ve.tensor_scalar_mul(stats[:, 13:14], stats[:, 10:11], 1.0 / 448.0)
        v3 = sig(ve, "V")
        sc.wait_ge(sV, v3)
        if _DEBUG_STOP == "C0":
            O.wait(sc, 0)
        sc.activation(sqx, lnf, AF.Copy, scale=stats[:, 12:13])
        s4 = sig(sc, "S")
        ve.wait_ge(sS, s4)
        iv = sqx.bitcast(I32)
        ve.tensor_scalar(iv, iv, RADD, None, op0=ALU.add)
        ve.tensor_scalar(iv, iv, RMASK, None, op0=ALU.bitwise_and)
        ve.tensor_scalar_mul(sqx, sqx, stats[:, 13:14])
        v4 = sig(ve, "V")
        if _DEBUG_STOP == "C0":
            ve.wait_ge(sV, v4)
            sy.wait_ge(sV, v4)
            O.go(dbg["qx"][j * 128:(j + 1) * 128, :], sqx)
        te.wait_ge(sV, v4)
        tr_f32(sqx, M,
               lambda c, src, j=j: ve.tensor_copy(
                   qxT[:, c, j * 128:(j + 1) * 128], src))
        s_qx[j] = n["V"]
        sy.wait_ge(sV, s_qx[j])
        Dx1.go(x1_d[j * 128:(j + 1) * 128, :], x1_sb[:, j, :])

    if _DEBUG_STOP == "C0":
        return finish()

    tr_done = [0] * NIB
    st_done = [0] * NIC
    sil_done = [0] * NIC
    psum_free = {}
    ve.memset(mx_t, -3e38)
    ve.memset(mn_t, 3e38)
    Dx1.wait(te)
    Dx1.wait(sy)
    for ib in range(NIB):
        par = ib % 2
        if ib >= 2:
            sy.wait_ge(sP, tr_done[ib - 2])
        for hc in range(16):
            Dwg.go(wgT[:, par, hc, :],
                   qwg_d[ib * IBLK:(ib + 1) * IBLK, hc * 128:(hc + 1) * 128],
                   sub=par, transpose=True)
            Dwg.go(wuT[:, par, hc, :],
                   qwu_d[ib * IBLK:(ib + 1) * IBLK, hc * 128:(hc + 1) * 128],
                   sub=par, transpose=True)
        Dwg.wait(te, par)
        for q4 in range(4):
            ic = ib * 4 + q4
            bufs = (PA, PB, PC)
            gP, uP = bufs[(2 * ic) % 3], bufs[(2 * ic + 1) % 3]
            for X in (gP, uP):
                fr = psum_free.get(id(X))
                if fr is not None:
                    te.wait_ge(sems[fr[0]], fr[1])
            for hc in range(16):
                first, last = hc == 0, hc == 15
                for c0 in (0, 512):
                    te.matmul(gP[:, c0:c0 + 512],
                              wgT[:, par, hc, q4 * 128:(q4 + 1) * 128],
                              qxT[:, hc, c0:c0 + 512],
                              start=first, stop=last)
                    te.matmul(uP[:, c0:c0 + 512],
                              wuT[:, par, hc, q4 * 128:(q4 + 1) * 128],
                              qxT[:, hc, c0:c0 + 512],
                              start=first, stop=last)
            pg = sig(te, "P")
            if q4 == 3:
                tr_done[ib] = pg
            tpar = ic % 2
            sc.wait_ge(sP, pg)
            if ic >= 1:
                sc.wait_ge(sV, sil_done[ic - 1])
            sc.activation(gs_t, gP, AF.Silu, scale=wg_rs[:, ic:ic + 1])
            sg1 = sig(sc, "S")
            psum_free[id(gP)] = ("S", sg1)
            ve.wait_ge(sS, sg1)
            if ic >= 2:
                Dt.wait(ve, tpar)
            ve.tensor_mul(tstg[:, tpar, :], gs_t, uP)
            sil_done[ic] = n["V"] + 1
            ve.tensor_scalar_mul(tstg[:, tpar, :], tstg[:, tpar, :],
                                 wu_rs[:, ic:ic + 1])
            ve.tensor_max(mx_t, mx_t, tstg[:, tpar, :])
            ve.tensor_tensor(mn_t, mn_t, tstg[:, tpar, :], op=ALU.min)
            st_done[ic] = sig(ve, "V")
            psum_free[id(uP)] = ("V", st_done[ic])
            sy.wait_ge(sV, st_done[ic])
            Dt.go(tT_d[ic * 128:(ic + 1) * 128, :], tstg[:, tpar, :],
                  sub=tpar)

    if _DEBUG_STOP == "C1":
        Gq.wait_all(sy)
        Dw.wait_all(sy)
        for ic in range(NIC):
            O.wait(sy, 0)
            Dt.wait_all(sy)
            Ddbg.go(tg[:, 0, :], tT_d[ic * 128:(ic + 1) * 128, :])
            Ddbg.wait(sy)
            O.go(dbg["tT"][ic * 128:(ic + 1) * 128, :], tg[:, 0, :])
        return finish()

    ve.tensor_scalar_mul(mn_t, mn_t, -1.0)
    ve.tensor_max(mx_t, mx_t, mn_t)
    v_mx = sig(ve, "V")
    te.wait_ge(sV, v_mx)
    for jt in range(8):
        te.transpose(PDh[:, 0:128], mx_t[:, jt * 128:(jt + 1) * 128], identf)
        p_mx = pe_fence()
        ve.wait_ge(sP, p_mx)
        ve.reduce_max(rmx8[:, jt:jt + 1], PDh[:, 0:128], axis=AX.X)
        vr = sig(ve, "V")
        te.wait_ge(sV, vr)
    ve.reciprocal(stats[:, 20:28], rmx8)
    ve.nop()
    ve.tensor_scalar_mul(stats[:, 20:28], stats[:, 20:28], 448.0)
    ve.tensor_scalar_mul(rmx8, rmx8, 1.0 / 448.0)
    v_st = sig(ve, "V")
    te.wait_ge(sV, v_st)
    for jt in range(8):
        for which, dstP in ((0, PA), (1, PB)):
            src1 = stats[:, 20 + jt:21 + jt] if which == 0                 else rmx8[:, jt:jt + 1]
            te.transpose(PDh[0:1, 0:128], src1, identf)
            p1 = pe_fence()
            sc.wait_ge(sP, p1)
            sc.copy(row16[0:1, which, :], PDh[0:1, 0:128])
            s1 = sig(sc, "S")
            te.wait_ge(sS, s1)
            te.matmul(dstP[:, jt * 128:(jt + 1) * 128], ones_row,
                      row16[0:1, which, :], start=True, stop=True)
    p_bb = sig(te, "P")
    sc.wait_ge(sP, p_bb)
    sc.copy(st_b, PA)
    sc.copy(str_b, PB)
    s_b = sig(sc, "S")
    te.wait_ge(sS, s_b)

    Dt.wait_all(gp)
    Gq.wait_all(gp)
    Dw.wait_all(gp)
    vq_done = [0] * NIC
    for ic in range(NIC):
        par = ic % 2
        if ic >= 2:
            gp.wait_ge(sV, vq_done[ic - 2])
        Gg.gather(tg[:, par, :], tT_d, pi_sb[:, ic:ic + 1], sub=par)
        Gg.wait(ve, par)
        ve.wait_ge(sS, s_b)
        if ic >= 2:
            Dqw.wait(ve, par)
        ve.tensor_mul(tgq[:, par, :], tg[:, par, :], st_b)
        vq_done[ic] = n["V"] + 1
        iv = tgq[:, par, :].bitcast(I32)
        ve.tensor_scalar(iv, iv, RADD, None, op0=ALU.add)
        ve.tensor_scalar(iv, iv, RMASK, None, op0=ALU.bitwise_and)
        ve.tensor_mul(tgq[:, par, :], tgq[:, par, :], str_b)
        v1 = sig(ve, "V")
        sc.wait_ge(sV, v1)
        sc.copy(qt16[:, par, :], tgq[:, par, :])
        s1 = sig(sc, "S")
        sy.wait_ge(sS, s1)
        Dqw.go(qtT_d[ic * 128:(ic + 1) * 128, :], qt16[:, par, :], sub=par)

    if _DEBUG_STOP == "C3":
        for ic in range(NIC):
            O.wait(sc, 0)
            Dqw.wait_all(sy)
            Ddbg.go(qt16[:, 0, :], qtT_d[ic * 128:(ic + 1) * 128, :])
            Ddbg.wait(sc)
            sc.copy(tgq[:, 0, :], qt16[:, 0, :])
            s1 = sig(sc, "S")
            sy.wait_ge(sS, s1)
            O.go(dbg["qtT"][ic * 128:(ic + 1) * 128, :], tgq[:, 0, :])
        return finish()

    for tt in range(16):
        te.transpose(PDh[0:1, 0:128], wd_rs[:, tt:tt + 1], identf)
        p1 = pe_fence()
        sc.wait_ge(sP, p1)
        sc.copy(row16[0:1, 0, :], PDh[0:1, 0:128])
        s1 = sig(sc, "S")
        te.wait_ge(sS, s1)
        te.matmul((PA, PB)[tt // 8][:, (tt % 8) * 128:(tt % 8 + 1) * 128],
                  ones_row, row16[0:1, 0, :], start=True, stop=True)
    p_cb = sig(te, "P")
    sc.wait_ge(sP, p_cb)
    sc.copy(csc_b[:, 0:1024], PA)
    sc.copy(csc_b[:, 1024:2048], PB)
    s_cb = sig(sc, "S")
    te.wait_ge(sS, s_cb)
    Dqw.wait_all(sy)
    sy.wait_ge(sS, s_cb)

    for jh in range(2):
        for mcol in range(4):
            dn_done = [0] * NIC
            v_evprev = n["V"]
            for ic in range(NIC):
                par = ic % 2
                if ic >= 2:
                    sy.wait_ge(sP, dn_done[ic - 2])
                Dqs.go(qts[:, par, 0:512],
                       qtT_d[ic * 128:(ic + 1) * 128,
                             jh * 512:(jh + 1) * 512], sub=par)
                Dwd.go(wdTs[:, par, :],
                       qwd_d[mcol * 512:(mcol + 1) * 512,
                             ic * 128:(ic + 1) * 128],
                       sub=par, transpose=True)
                Dqs.wait(te, par)
                Dwd.wait(te, par)
                if ic == 0:
                    te.wait_ge(sV, v_evprev)
                for jj in range(4):
                    te.matmul(jslice4(jj),
                              qts[:, par, jj * 128:(jj + 1) * 128],
                              wdTs[:, par, :],
                              start=(ic == 0), stop=(ic == NIC - 1))
                dn_done[ic] = sig(te, "P")
            p_dn = dn_done[NIC - 1]
            for jj in range(4):
                j = jh * 4 + jj
                par = jj % 2
                Dxs.go(x1st[:, par, :],
                       x1_d[j * 128:(j + 1) * 128,
                            mcol * 512:(mcol + 1) * 512], sub=par)
                Dxs.wait(ve, par)
                ve.wait_ge(sP, p_dn)
                O.wait(ve, par)
                ve.tensor_mul(yev[:, par, :], jslice4(jj),
                              csc_b[:, mcol * 512:(mcol + 1) * 512])
                ve.tensor_add(yev[:, par, :], yev[:, par, :], x1st[:, par, :])
                v1 = sig(ve, "V")
                sy.wait_ge(sV, v1)
                O.go(out_d[j * 128:(j + 1) * 128,
                           mcol * 512:(mcol + 1) * 512],
                     yev[:, par, :], sub=par)
    return finish()


def kernel(L0_attn_x, L0_attn_w_aq, L0_attn_w_ak, L0_attn_w_av, L0_attn_w_ao,
           L0_ffn_w_gate, L0_ffn_w_up, L0_ffn_w_down,
           L0_ffn_up_reorder_index, L0_ffn_down_reorder_index):
    x = np.ascontiguousarray(np.asarray(L0_attn_x, dtype=np.float32))
    waq = np.ascontiguousarray(np.asarray(L0_attn_w_aq, dtype=np.float32))
    wak = np.ascontiguousarray(np.asarray(L0_attn_w_ak, dtype=np.float32))
    wav = np.ascontiguousarray(np.asarray(L0_attn_w_av, dtype=np.float32))
    wao = np.ascontiguousarray(np.asarray(L0_attn_w_ao, dtype=np.float32))
    wg = np.ascontiguousarray(np.asarray(L0_ffn_w_gate, dtype=np.float32))
    wu = np.ascontiguousarray(np.asarray(L0_ffn_w_up, dtype=np.float32))
    wd = np.ascontiguousarray(np.asarray(L0_ffn_w_down, dtype=np.float32))
    pi = np.asarray(L0_ffn_down_reorder_index).astype(np.int32)
    pi_dev = np.ascontiguousarray(pi.reshape(NIC, 128).T)

    nc, _ = build_program()
    in_maps = []
    perms = {}
    for c in range(8):
        b, p = c // 2, c % 2
        tokperm, cosF, sinFs, mkt, swap, identm = _host_tables(p)
        perms[c] = tokperm
        in_maps.append({
            "xb": np.ascontiguousarray(x[b][tokperm]),
            "w_aq": waq, "w_ak": wak, "w_av": wav, "w_ao": wao,
            "wg": wg, "wu": wu, "wd": wd, "pi": pi_dev,
            "ropecos": np.ascontiguousarray(cosF),
            "ropesin": np.ascontiguousarray(sinFs),
            "mk": np.ascontiguousarray(mkt),
            "ident": identm, "identf": np.eye(128, dtype=np.float32),
            "swapm": swap,
        })
    res = run_bass_kernel_spmd(nc, in_maps, core_ids=list(range(8)))
    out = np.empty((B, T, M), np.float32)
    for c in range(8):
        out[c // 2][perms[c][:NTOK]] = res.results[c]["out"]
    return out


# version 02
# speedup vs baseline: 1.0579x; 1.0579x over previous
import numpy as np
import ml_dtypes
import concourse.bass as bass
import concourse.mybir as mybir
from concourse.bass_utils import run_bass_kernel_spmd
from concourse.bass_core_rust import IndirectOffsetOnAxis

F32 = mybir.dt.float32
BF16 = mybir.dt.bfloat16
I32 = mybir.dt.int32
U8 = mybir.dt.uint8
AF = mybir.ActivationFunctionType
ALU = mybir.AluOpType
AX = mybir.AxisListType

B, T, M, NH, DH = 4, 2048, 2048, 16, 128
II = 5632
KT = T // 128
NJ = 8
NTOK = NJ * 128
NIC = II // 128
IBLK = 512
NIB = II // IBLK
RADD = 0x80000
RMASK = -0x100000

_DEBUG_STOP = None
_DEBUG_HEADS = NH
_DEBUG_TILES = KT
_DEBUG_SUB = 3
_DEBUG_BSUB = 9
_STRICT = [False]


def _host_tables(p):
    own = [2 * j + p for j in range(NJ)]
    other = [t for t in range(KT) if t not in own]
    tokperm = np.concatenate([np.arange(t * 128, (t + 1) * 128)
                              for t in own + other])
    pos = tokperm.astype(np.float64)
    dims = np.arange(DH // 2, dtype=np.float64)
    ang = np.power(10000.0, -dims / (DH // 2))
    rad = pos[None, :] * ang[:, None]
    cos = np.cos(rad)
    sin = np.sin(rad)
    cosF = np.concatenate([cos, cos], 0).astype(ml_dtypes.bfloat16)
    sinFs = np.concatenate([-sin, sin], 0).astype(ml_dtypes.bfloat16)
    tri = (np.arange(128)[:, None] <= np.arange(128)[None, :]).astype(np.float32)
    mk = np.empty((KT, 128, 128), np.float32)
    for m in range(KT):
        mk[m] = tri if m < NJ else (0.0 if p == 0 else 1.0)
    swap = np.zeros((128, 128), np.float32)
    swap[:64, 64:] = np.eye(64)
    swap[64:, :64] = np.eye(64)
    return (tokperm, cosF, sinFs, mk.astype(ml_dtypes.bfloat16),
            swap.astype(ml_dtypes.bfloat16),
            np.eye(128, dtype=ml_dtypes.bfloat16))


def build_program():
    nc = bass.Bass("TRN2", target_bir_lowering=False, debug=False)

    xb = nc.dram_tensor("xb", [T, M], F32, kind="ExternalInput").ap()
    w_aq = nc.dram_tensor("w_aq", [NH, M, DH], F32, kind="ExternalInput").ap()
    w_ak = nc.dram_tensor("w_ak", [NH, M, DH], F32, kind="ExternalInput").ap()
    w_av = nc.dram_tensor("w_av", [NH, M, DH], F32, kind="ExternalInput").ap()
    w_ao = nc.dram_tensor("w_ao", [NH, DH, M], F32, kind="ExternalInput").ap()
    wg_d = nc.dram_tensor("wg", [II, M], F32, kind="ExternalInput").ap()
    wu_d = nc.dram_tensor("wu", [II, M], F32, kind="ExternalInput").ap()
    wd_d = nc.dram_tensor("wd", [M, II], F32, kind="ExternalInput").ap()
    pi_d = nc.dram_tensor("pi", [128, NIC], I32, kind="ExternalInput").ap()
    cos_d = nc.dram_tensor("ropecos", [128, T], BF16, kind="ExternalInput").ap()
    sin_d = nc.dram_tensor("ropesin", [128, T], BF16, kind="ExternalInput").ap()
    mk_d = nc.dram_tensor("mk", [KT, 128, 128], BF16, kind="ExternalInput").ap()
    id_d = nc.dram_tensor("ident", [128, 128], BF16, kind="ExternalInput").ap()
    sw_d = nc.dram_tensor("swapm", [128, 128], BF16, kind="ExternalInput").ap()
    idf_d = nc.dram_tensor("identf", [128, 128], F32, kind="ExternalInput").ap()

    out_d = nc.dram_tensor("out", [NTOK, M], F32, kind="ExternalOutput").ap()

    qwg_d = nc.dram_tensor("qwg", [II, M], BF16, kind="Internal").ap()
    qwu_d = nc.dram_tensor("qwu", [II, M], BF16, kind="Internal").ap()
    qwd_d = nc.dram_tensor("qwd", [M, II], BF16, kind="Internal").ap()
    tT_d = nc.dram_tensor("tT", [II, NTOK], F32, kind="Internal").ap()
    qtT_d = nc.dram_tensor("qtT", [II, NTOK], BF16, kind="Internal").ap()
    x1_d = nc.dram_tensor("x1s", [NTOK, M], F32, kind="Internal").ap()

    dbg = {}
    if _DEBUG_STOP is not None:
        for nm, shp in [("lnxT", [16, 128, T]), ("kT", [128, T]),
                        ("vTok", [KT, 128, 128]), ("qT", [128, NTOK]),
                        ("oT", [128, NH * NTOK]), ("x1", [NTOK, M]),
                        ("qx", [NTOK, M]), ("tT", [II, NTOK]),
                        ("qtT", [II, NTOK]), ("st", [128, 32]),
                        ("lnA", [128, M])]:
            dbg[nm] = nc.dram_tensor("dbg_" + nm, shp, F32,
                                     kind="ExternalOutput").ap()
        dbg["lnraw"] = nc.dram_tensor("dbg_lnraw", [16, 128, T], BF16,
                                      kind="ExternalOutput").ap()

    def sba(name, shape, dt):
        return nc.alloc_sbuf_tensor(name, list(shape), dt).ap()

    ident = sba("identc", [128, 128], BF16)
    identf = sba("identfc", [128, 128], F32)
    swapm = sba("swapc", [128, 128], BF16)
    mk = sba("mkc", [128, KT, 128], BF16)
    cosF = sba("cosc", [128, T], BF16)
    sinFs = sba("sinc", [128, T], BF16)
    pi_sb = sba("pic", [128, NIC], I32)
    ones_col = sba("onescol", [128, 1], BF16)
    ones_row = sba("onesrow", [1, 128], BF16)
    wg_sc = sba("wgsc", [128, NIC], F32)
    wu_sc = sba("wusc", [128, NIC], F32)
    wd_sc = sba("wdsc", [128, 16], F32)
    wg_rs = sba("wgrs", [128, NIC], F32)
    wu_rs = sba("wurs", [128, NIC], F32)
    wd_rs = sba("wdrs", [128, 16], F32)
    stats = sba("stats", [128, 32], F32)
    rmx8 = sba("rmx8", [128, 8], F32)
    row16 = sba("row16", [1, 2, 128], BF16)
    negc = sba("negc", [128, 1], F32)
    cm16 = sba("cm16", [128, 2], BF16)
    kT = sba("kT", [128, T], BF16)
    kvraw = sba("kvraw", [128, T], BF16)
    kTr, vT = kvraw, kvraw
    qTr = kvraw[:, 0:NTOK]
    vTok2 = sba("vTok2", [128, KT, 256], BF16)
    qT = sba("qT", [128, NTOK], BF16)
    pt = sba("pt", [128, 2, NTOK], BF16)

    def region(name, nbytes):
        return nc.alloc_sbuf_tensor(name, [128, nbytes], U8).ap()

    def view(reg, o, shape, dt):
        esz = {F32: 4, BF16: 2, I32: 4}[dt]
        r = reg.bitcast(dt)
        flat = 1
        for s in shape[1:]:
            flat *= s
        v = r[:, o // esz:o // esz + flat]
        if len(shape) == 2:
            return v
        pat = "p (" + " ".join(f"d{i}" for i in range(1, len(shape))) + ") ->" \
              + " p " + " ".join(f"d{i}" for i in range(1, len(shape)))
        kw = {f"d{i}": shape[i] for i in range(1, len(shape) - 1)}
        return v.rearrange(pat, **kw)

    RG1 = region("RG1", 65536)
    lnxT = view(RG1, 0, [128, 16, T], BF16)
    x1_sb = view(RG1, 0, [128, NJ, M], F32)
    wgT = view(RG1, 0, [128, 2, 16, IBLK], BF16)
    wuT = view(RG1, 32768, [128, 2, 16, IBLK], BF16)
    RG2 = region("RG2", 32768)
    oT = view(RG2, 0, [128, NH, NTOK], BF16)
    qxT = view(RG2, 0, [128, 16, NTOK], BF16)
    RG3 = region("RG3", 36864)
    xst = view(RG3, 0, [128, 2, M], F32)
    lnA = view(RG3, 16384, [128, M], F32)
    wq16 = view(RG3, 0, [128, 2, 16, DH], BF16)
    wk16 = view(RG3, 8192, [128, 2, 16, DH], BF16)
    wv2 = view(RG3, 16384, [128, 16, 2, 128], BF16)
    wv2f = view(RG3, 16384, [128, 16, 256], BF16)
    linv = view(RG3, 24576, [128, NTOK], F32)
    rowB = view(RG3, 28672, [1, 1024], F32)
    rowA16 = view(RG3, 32768, [1, 1024], BF16)
    rowA = view(RG3, 34816, [1, 512], F32)
    wao16 = view(RG3, 0, [128, 2, 512], BF16)
    xres = view(RG3, 2048, [128, 2, 512], F32)
    sqx = view(RG3, 0, [128, M], F32)
    lnf = view(RG3, 8192, [128, M], F32)
    tstg = view(RG3, 16384, [128, 2, NTOK], F32)
    gs_t = view(RG3, 24576, [128, NTOK], F32)
    mx_t = view(RG3, 28672, [128, NTOK], F32)
    mn_t = view(RG3, 32768, [128, NTOK], F32)
    lnsq = view(RG3, 24576, [128, M], F32)
    st_b = view(RG3, 28672, [128, NTOK], F32)
    str_b = view(RG3, 32768, [128, NTOK], F32)
    csc_b = view(RG3, 16384, [128, M], F32)
    RG4 = region("RG4", 28672)
    wst = view(RG4, 0, [128, 2, 2816], F32)
    tg = view(RG4, 0, [128, 2, NTOK], F32)
    tgq = view(RG4, 8192, [128, 2, NTOK], F32)
    qt16 = view(RG4, 16384, [128, 2, NTOK], BF16)
    wdTs = view(RG4, 20480, [128, 2, 512], BF16)
    x1st = view(RG4, 22528, [128, 2, 512], F32)
    qts = view(RG4, 24576, [128, 2, NTOK], BF16)
    yev = view(RG4, 0, [128, 2, 512], F32)

    PA = nc.alloc_psum_tensor("PA", [128, 1024], F32).ap()
    PB = nc.alloc_psum_tensor("PB", [128, 1024], F32).ap()
    PC = nc.alloc_psum_tensor("PC", [128, 1024], F32).ap()
    PDh = nc.alloc_psum_tensor("PDh", [128, 512], F32).ap()
    PF = nc.alloc_psum_tensor("PF", [1, 128], F32).ap()

    _GUARD = {"tensor_copy", "tensor_tensor", "tensor_add", "tensor_sub",
              "tensor_mul", "tensor_max", "tensor_scalar",
              "tensor_single_scalar", "tensor_scalar_add",
              "tensor_scalar_mul", "tensor_scalar_max", "tensor_scalar_min",
              "tensor_reduce", "reduce_max", "reduce_sum", "reciprocal",
              "memset", "activation", "square", "sqrt", "copy", "mul",
              "add", "select"}

    class _EG:
        def __init__(self, e):
            self._e = e
            self.last = None

        def __getattr__(self, name):
            f = getattr(self._e, name)
            if name in _GUARD:
                def g(*a, **k):
                    fs = 1024
                    try:
                        shp = a[0].shape
                        fs = 1
                        for x in shp[1:]:
                            fs *= x
                    except Exception:
                        pass
                    if _STRICT[0] or fs < 512:
                        self._e.drain()
                    else:
                        self._e.nop()
                    r = f(*a, **k)
                    self.last = r
                    return r
                return g
            return f

    class _TG:
        def __init__(self, e):
            self._e = e
            self.last = None

        def __getattr__(self, name):
            f = getattr(self._e, name)
            if name in ("transpose", "matmul"):
                def g(*a, **k):
                    r = f(*a, **k)
                    self.last = r
                    return r
                return g
            return f

    sy = nc.sync
    gp = _EG(nc.gpsimd)
    ve = _EG(nc.vector)
    sc = _EG(nc.scalar)
    te = _TG(nc.tensor)
    sems = {k: nc.alloc_semaphore(k) for k in ("P", "V", "S")}
    n = {k: 0 for k in sems}
    sP, sV, sS = (sems[k] for k in ("P", "V", "S"))

    class DStream:
        def __init__(self, name, nsub=1, eng=None):
            self.sems = [nc.alloc_semaphore(name + str(i))
                         for i in range(nsub)]
            self.n = [0] * nsub
            self.eng = eng if eng is not None else sy

        def go(self, out, in_, sub=0, transpose=False):
            if transpose:
                r = self.eng.dma_start_transpose(out, in_)
            else:
                r = self.eng.dma_start(out=out, in_=in_)
            r.then_inc(self.sems[sub], 16)
            self.n[sub] += 16

        def gather(self, out, in_, idx, sub=0):
            self.eng.indirect_dma_start(
                out, None, in_, IndirectOffsetOnAxis(idx, 0)) \
                .then_inc(self.sems[sub], 16)
            self.n[sub] += 16

        def wait(self, eng, sub=0):
            eng.wait_ge(self.sems[sub], self.n[sub])

        def wait_all(self, eng):
            for i, s in enumerate(self.sems):
                eng.wait_ge(s, self.n[i])

    Dc = DStream("Dc")
    Dx = DStream("Dx", 2)
    Dw = DStream("Dw", 2)
    Gq = DStream("Gq", 2, gp)
    Gw = DStream("Gw", 2, gp)
    Gv = DStream("Gv", 1, gp)
    Ga = DStream("Ga", 2, gp)
    Dxr = DStream("Dxr", 2)
    Dx1 = DStream("Dx1")
    Dwg = DStream("Dwg", 2)
    Dt = DStream("Dt", 2)
    Gg = DStream("Gg", 2, gp)
    Dqw = DStream("Dqw", 2)
    Dqs = DStream("Dqs", 2)
    Dwd = DStream("Dwd", 2)
    Dxs = DStream("Dxs", 2)
    O = DStream("O", 2)
    Ddbg = DStream("Ddbg")
    streams = [Dc, Dx, Dw, Gq, Gw, Gv, Ga, Dxr, Dx1, Dwg, Dt, Gg, Dqw,
               Dqs, Dwd, Dxs, O, Ddbg]

    def sig(eng, key):
        eng.last.then_inc(sems[key], 1)
        n[key] += 1
        return n[key]

    def finish():
        for e in (sy, gp, ve, sc, te):
            O.wait_all(e)
        return nc, dbg

    for st_ in streams:
        for s in st_.sems:
            sy.sem_clear(s)
    for s in sems.values():
        sy.sem_clear(s)
    nc.all_engine_barrier()
    Dc.go(ident, id_d)
    Dc.go(swapm, sw_d)
    Dc.go(identf, idf_d)
    Dc.go(mk, mk_d.rearrange("k p q -> p k q"))
    Dc.go(cosF, cos_d)
    Dc.go(sinFs, sin_d)
    Dc.go(pi_sb, pi_d)
    Dc.wait(ve)
    if _DEBUG_STOP is not None:
        for rg_ in (RG1, RG2, RG3, RG4):
            ve.memset(rg_.bitcast(F32), 0.0)
        ve.memset(stats, 0.0)
        ve.memset(rmx8, 0.0)
    ve.memset(ones_col, 1.0)
    ve.memset(ones_row, 1.0)
    v0 = sig(ve, "V")
    for e in (te, sc, gp, sy):
        e.wait_ge(sV, v0)

    def pe_fence():
        return sig(te, "P")

    def tr_f32(src_f32, ncols, write_fn):
        nt = ncols // 128
        done = 0
        while done < nt:
            batch = min(16, nt - done)
            for k in range(batch):
                c = done + k
                P, o = (PA, PB)[k // 8], (k % 8) * 128
                te.transpose(P[:, o:o + 128],
                             src_f32[:, c * 128:(c + 1) * 128], identf)
            p1 = pe_fence()
            ve.wait_ge(sP, p1)
            for k in range(batch):
                c = done + k
                P, o = (PA, PB)[k // 8], (k % 8) * 128
                write_fn(c, P[:, o:o + 128])
            v1 = sig(ve, "V")
            te.wait_ge(sV, v1)
            done += batch

    lnsq_free = [0]
    sqrt_sig = [0]

    def ln_stats(src_ap):
        ve.wait_ge(sS, sqrt_sig[0])
        ve.reduce_sum(stats[:, 0:1], src_ap, axis=AX.X)
        sc.wait_ge(sV, lnsq_free[0])
        sc.square(lnsq, src_ap)
        s1 = sig(sc, "S")
        ve.wait_ge(sS, s1)
        ve.reduce_sum(stats[:, 1:2], lnsq, axis=AX.X)
        lnsq_free[0] = n["V"] + 1
        ve.tensor_scalar_mul(stats[:, 2:3], stats[:, 0:1], 1.0 / M)
        ve.tensor_scalar_mul(stats[:, 3:4], stats[:, 1:2], 1.0 / M)
        ve.tensor_mul(stats[:, 4:5], stats[:, 2:3], stats[:, 2:3])
        ve.tensor_sub(stats[:, 5:6], stats[:, 3:4], stats[:, 4:5])
        ve.tensor_scalar_add(stats[:, 5:6], stats[:, 5:6], 1e-5)
        v1 = sig(ve, "V")
        sc.wait_ge(sV, v1)
        sc.sqrt(stats[:, 6:7], stats[:, 5:6])
        s2 = sig(sc, "S")
        sqrt_sig[0] = s2
        ve.wait_ge(sS, s2)
        ve.reciprocal(stats[:, 7:8], stats[:, 6:7])
        v2 = sig(ve, "V")
        sc.wait_ge(sV, v2)


    s_lnuse = [0] * KT
    for i in range(_DEBUG_TILES):
        par = i % 2
        if i >= 2:
            sy.wait_ge(sV, s_lnuse[i - 2])
        Dx.go(xst[:, par, :], xb[i * 128:(i + 1) * 128, :], sub=par)
        Dx.wait(ve, par)
        Dx.wait(sc, par)
        if _DEBUG_SUB >= 1:
            ln_stats(xst[:, par, :])
        if _DEBUG_SUB >= 2:
            if i >= 1:
                ve.wait_ge(sP, p_lnA[i - 1])
            ve.tensor_scalar(lnA, xst[:, par, :], stats[:, 2:3],
                             stats[:, 7:8], op0=ALU.subtract, op1=ALU.mult)
            s_lnuse[i] = sig(ve, "V")
        if _DEBUG_SUB >= 3:
            te.wait_ge(sV, s_lnuse[i])
            tr_f32(lnA, M,
                   lambda c, src, i=i: ve.tensor_copy(
                       lnxT[:, c, i * 128:(i + 1) * 128], src))
        if i == 0:
            p_lnA = [0] * KT
        p_lnA[i] = n["P"]
    v_endA = n["V"]

    if _DEBUG_STOP == "A":
        sy.wait_ge(sV, v_endA)
        O.go(dbg["st"], stats)
        O.go(dbg["lnA"], lnA)
        for c in range(16):
            O.go(dbg["lnraw"][c], lnxT[:, c, :])
        for c in range(16):
            sc.wait_ge(sV, v_endA)
            O.wait(sc, 0)
            sc.copy(lnf, lnxT[:, c, :])
            s1 = sig(sc, "S")
            sy.wait_ge(sS, s1)
            O.go(dbg["lnxT"][c], lnf)
        return finish()

    w_units = []
    for t in range(II // 128):
        w_units.append(("g", t))
        w_units.append(("u", t))
    for t in range(16):
        w_units.append(("dmax", t))
        for q in range(4):
            w_units.append(("d", t, q))
    wpar = [0]
    w_free = {0: None, 1: None}
    w_dmax_v = {}

    def w_wait_free(par):
        g = w_free[par]
        if g is None:
            return
        if g[0] == "V":
            sy.wait_ge(sV, g[1])
        else:
            Gq.wait(sy, par)

    def w_round_store(par, width, scale_col, dst):
        sc.activation(wst[:, par, 0:width], wst[:, par, 0:width], AF.Copy,
                      scale=scale_col)
        s1 = sig(sc, "S")
        ve.wait_ge(sS, s1)
        iv = wst[:, par, 0:width].bitcast(I32)
        ve.tensor_scalar(iv, iv, RADD, None, op0=ALU.add)
        ve.tensor_scalar(iv, iv, RMASK, None, op0=ALU.bitwise_and)
        v2 = sig(ve, "V")
        gp.wait_ge(sV, v2)
        Gq.go(dst, wst[:, par, 0:width], sub=par)
        w_free[par] = ("Gq", par)

    def emit_w_unit(u):
        if u[0] in ("g", "u"):
            _, t = u
            src, dst = (wg_d, qwg_d) if u[0] == "g" else (wu_d, qwu_d)
            scl = wg_sc if u[0] == "g" else wu_sc
            rsl = wg_rs if u[0] == "g" else wu_rs
            par = wpar[0] % 2
            wpar[0] += 1
            w_wait_free(par)
            Dw.go(wst[:, par, 0:2048], src[t * 128:(t + 1) * 128, :], sub=par)
            Dw.wait(ve, par)
            ve.reduce_max(stats[:, 10:11], wst[:, par, 0:2048], axis=AX.X,
                          apply_absolute_value=True)
            ve.reciprocal(stats[:, 11:12], stats[:, 10:11])
            ve.tensor_scalar_mul(scl[:, t:t + 1], stats[:, 11:12], 448.0)
            ve.tensor_scalar_mul(rsl[:, t:t + 1], stats[:, 10:11], 1 / 448.0)
            v1 = sig(ve, "V")
            sc.wait_ge(sV, v1)
            w_round_store(par, 2048, scl[:, t:t + 1],
                          dst[t * 128:(t + 1) * 128, :])
        elif u[0] == "dmax":
            _, t = u
            for qq in range(4):
                par = wpar[0] % 2
                wpar[0] += 1
                w_wait_free(par)
                Dw.go(wst[:, par, 0:1408],
                      wd_d[t * 128:(t + 1) * 128, qq * 1408:(qq + 1) * 1408],
                      sub=par)
                Dw.wait(ve, par)
                ve.reduce_max(stats[:, 12 + qq:13 + qq], wst[:, par, 0:1408],
                              axis=AX.X, apply_absolute_value=True)
                w_free[par] = ("V", sig(ve, "V"))
            ve.tensor_max(stats[:, 12:13], stats[:, 12:13], stats[:, 13:14])
            ve.tensor_max(stats[:, 14:15], stats[:, 14:15], stats[:, 15:16])
            ve.tensor_max(stats[:, 12:13], stats[:, 12:13], stats[:, 14:15])
            ve.reciprocal(stats[:, 13:14], stats[:, 12:13])
            ve.tensor_scalar_mul(wd_sc[:, t:t + 1], stats[:, 13:14], 448.0)
            ve.tensor_scalar_mul(wd_rs[:, t:t + 1], stats[:, 12:13], 1 / 448.0)
            w_dmax_v[t] = sig(ve, "V")
        else:
            _, t, q = u
            par = wpar[0] % 2
            wpar[0] += 1
            w_wait_free(par)
            Dw.go(wst[:, par, 0:1408],
                  wd_d[t * 128:(t + 1) * 128, q * 1408:(q + 1) * 1408],
                  sub=par)
            Dw.wait(sc, par)
            sc.wait_ge(sV, w_dmax_v[t])
            w_round_store(par, 1408, wd_sc[:, t:t + 1],
                          qwd_d[t * 128:(t + 1) * 128,
                                q * 1408:(q + 1) * 1408])

    p_qproj_prev = [0]

    def fetch_head_w(h):
        hp = h % 2
        if h >= 2:
            gp.wait_ge(sP, p_qproj_prev[0])
        Gw.go(wq16[:, hp, :, :],
              w_aq[h].rearrange("(c p) d -> p c d", p=128), sub=hp)
        Gw.go(wk16[:, hp, :, :],
              w_ak[h].rearrange("(c p) d -> p c d", p=128), sub=hp)

    def fetch_wv2(h):
        Gv.go(wv2[:, :, 0, :], w_av[h].rearrange("(c p) d -> p c d", p=128))
        Gv.go(wv2[:, :, 1, :],
              w_av[h + 1].rearrange("(c p) d -> p c d", p=128))

    gp.wait_ge(sV, v_endA)
    fetch_head_w(0)
    WPH = (len(w_units) + NH - 1) // NH

    def proj(dst16, wsb, hp, psum, ofs):
        for mc in range(16):
            first, last = mc == 0, mc == 15
            for c0 in (0, 512):
                te.matmul(psum[:, c0:c0 + 512], wsb[:, hp, mc, :],
                          lnxT[:, mc, ofs + c0:ofs + c0 + 512],
                          start=first, stop=last)
        p1 = sig(te, "P")
        sc.wait_ge(sP, p1)
        sc.copy(dst16[:, ofs:ofs + 1024], psum[:, 0:1024])
        s1 = sig(sc, "S")
        te.wait_ge(sS, s1)
        return s1

    s_v2 = 0
    p_v2mm = 0
    pv_done_prev = 0
    for h in range(_DEBUG_HEADS):
        hp = h % 2
        if h + 1 < NH:
            fetch_head_w(h + 1)
        Gw.wait(te, hp)
        if hp == 0 and _DEBUG_BSUB >= 1:
            if h >= 2:
                gp.wait_ge(sP, p_v2mm)
            fetch_wv2(h)
            Gv.wait(te)
            te.wait_ge(sS, s_v2)
            if h >= 2:
                te.wait_ge(sP, pv_done_prev)
            sv2_evac = [0] * KT
            for tt in range(KT):
                VP = PDh if tt % 2 == 0 else PC
                if tt >= 2:
                    te.wait_ge(sS, sv2_evac[tt - 2])
                for mc in range(16):
                    te.matmul(VP[:, 0:256],
                              lnxT[:, mc, tt * 128:(tt + 1) * 128],
                              wv2f[:, mc, :],
                              start=(mc == 0), stop=(mc == 15))
                p1 = sig(te, "P")
                sc.wait_ge(sP, p1)
                sc.copy(vTok2[:, tt, :], VP[:, 0:256])
                sv2_evac[tt] = sig(sc, "S")
            s_v2 = sv2_evac[KT - 1]
            p_v2mm = n["P"]
            te.wait_ge(sS, s_v2)
        if _DEBUG_BSUB < 2:
            continue
        proj(kTr, wk16, hp, PA, 0)
        sk1 = proj(kTr, wk16, hp, PB, 1024)
        for c0 in range(0, T, 512):
            te.matmul((PA if c0 < 1024 else PB)[:, c0 % 1024:c0 % 1024 + 512],
                      swapm, kTr[:, c0:c0 + 512], start=True, stop=True)
        pswap = sig(te, "P")
        ve.wait_ge(sS, sk1)
        ve.tensor_mul(kT, kTr, cosF)
        ve.wait_ge(sP, pswap)
        ve.tensor_mul(kTr[:, 0:1024], PA, sinFs[:, 0:1024])
        ve.tensor_mul(kTr[:, 1024:2048], PB, sinFs[:, 1024:2048])
        ve.tensor_add(kT, kT, kTr)
        ve.reduce_max(stats[:, 16:17], kT, axis=AX.X,
                      apply_absolute_value=True)
        vk = sig(ve, "V")
        te.wait_ge(sV, vk)
        if _DEBUG_BSUB < 3:
            continue
        sc.wait_ge(sV, vk)
        sq0 = proj(qTr, wq16, hp, PA, 0)
        p_qproj_prev[0] = n["P"]
        for c0 in (0, 512):
            te.matmul(PA[:, c0:c0 + 512], swapm, qTr[:, c0:c0 + 512],
                      start=True, stop=True)
        pswq = sig(te, "P")
        ve.wait_ge(sS, sq0)
        ve.tensor_mul(qT, qTr, cosF[:, 0:NTOK])
        ve.wait_ge(sP, pswq)
        ve.tensor_mul(qTr, PA, sinFs[:, 0:NTOK])
        ve.tensor_add(qT, qT, qTr)
        ve.reduce_max(stats[:, 17:18], qT, axis=AX.X,
                      apply_absolute_value=True)
        vq = sig(ve, "V")
        te.wait_ge(sV, vq)
        sc.wait_ge(sV, vq)
        sc.copy(cm16[:, 0:1], stats[:, 16:17])
        sc.copy(cm16[:, 1:2], stats[:, 17:18])
        scm = sig(sc, "S")
        te.wait_ge(sS, scm)
        te.matmul(PA[0:1, 0:1], cm16[:, 0:1], cm16[:, 1:2],
                  start=True, stop=True)
        pc1 = sig(te, "P")
        sc.wait_ge(sP, pc1)
        sc.copy(rowA16[0:1, 0:1], PA[0:1, 0:1])
        sc1 = sig(sc, "S")
        te.wait_ge(sS, sc1)
        te.matmul(PA[:, 0:1], ones_row, rowA16[0:1, 0:1],
                  start=True, stop=True)
        pc2 = sig(te, "P")
        sc.wait_ge(sP, pc2)
        sc.activation(negc, PA[:, 0:1], AF.Copy, scale=-1.0 / 128.0)
        snegc = sig(sc, "S")
        te.wait_ge(sS, snegc)

        if _DEBUG_BSUB < 4:
            continue
        exp_done = [0] * KT
        pv_done = [0] * KT
        for m in range(KT):
            j0 = m if m < NJ else m - NJ
            q0 = j0 * 128
            SP = PC
            if m >= 1:
                te.wait_ge(sS, exp_done[m - 1])
            c0 = q0
            while c0 < NTOK:
                ce = min((c0 // 512 + 1) * 512, NTOK)
                te.matmul(SP[:, c0:ce], kT[:, m * 128:(m + 1) * 128],
                          qT[:, c0:ce], start=True, stop=True)
                c0 = ce
            ps = sig(te, "P")
            sc.wait_ge(sP, ps)
            if m >= 2:
                sc.wait_ge(sP, pv_done[m - 2])
            sc.activation(pt[:, m % 2, q0:NTOK], SP[:, q0:NTOK], AF.Exp,
                          bias=negc, scale=1.0 / 128.0)
            exp_done[m] = sig(sc, "S")
            ve.wait_ge(sS, exp_done[m])
            ve.tensor_mul(pt[:, m % 2, q0:q0 + 128],
                          pt[:, m % 2, q0:q0 + 128], mk[:, m, :])
            vm1 = sig(ve, "V")
            te.wait_ge(sV, vm1)
            c0 = q0
            while c0 < NTOK:
                ce = min((c0 // 512 + 1) * 512, NTOK)
                stop = (m == 11) if c0 < 512 else (m == 15)
                te.matmul(PA[0:1, c0:ce], ones_col, pt[:, m % 2, c0:ce],
                          start=(m == 0), stop=stop)
                te.matmul(PB[:, c0:ce],
                          vTok2[:, m, hp * 128:(hp + 1) * 128],
                          pt[:, m % 2, c0:ce],
                          start=(m == 0), stop=stop)
                c0 = ce
            pv_done[m] = sig(te, "P")
        pv_done_prev = pv_done[KT - 1]
        if _DEBUG_BSUB < 5:
            continue
        sc.wait_ge(sP, pv_done[KT - 1])
        sc.copy(rowA[0:1, 0:512], PA[0:1, 0:512])
        sc.copy(rowB[0:1, 0:512], PA[0:1, 512:1024])
        sl1 = sig(sc, "S")
        ve.wait_ge(sS, sl1)
        ve.reciprocal(rowB[0:1, 512:1024], rowB[0:1, 0:512])
        ve.reciprocal(rowA[0:1, 0:512], rowA[0:1, 0:512])
        ve.tensor_copy(rowA16[0:1, 0:512], rowA[0:1, 0:512])
        ve.tensor_copy(rowA16[0:1, 512:1024], rowB[0:1, 512:1024])
        vl1 = sig(ve, "V")
        te.wait_ge(sV, vl1)
        te.matmul(PC[:, 0:512], ones_row, rowA16[0:1, 0:512],
                  start=True, stop=True)
        te.matmul(PC[:, 512:1024], ones_row, rowA16[0:1, 512:1024],
                  start=True, stop=True)
        pl1 = sig(te, "P")
        sc.wait_ge(sP, pl1)
        sc.copy(linv, PC)
        sl2 = sig(sc, "S")
        ve.wait_ge(sS, sl2)
        ve.tensor_mul(oT[:, h, :], PB, linv)
        voT = sig(ve, "V")
        te.wait_ge(sV, voT)
        pv_done_prev = pv_done[KT - 1]
        for u in w_units[h * WPH:(h + 1) * WPH]:
            emit_w_unit(u)

    if _DEBUG_STOP == "B":
        v = n["V"]
        sc.wait_ge(sV, v)
        Gw.wait_all(sc)
        Gv.wait_all(sc)
        sc.copy(sqx[:, 0:T], kT)
        s1 = sig(sc, "S")
        sy.wait_ge(sS, s1)
        O.go(dbg["kT"], sqx[:, 0:T])
        for c in range(KT):
            O.wait(sc, 0)
            sc.copy(lnf[:, 0:128], vTok2[:, c, 0:128] if _DEBUG_HEADS % 2
                    else vTok2[:, c, 128:256])
            s1 = sig(sc, "S")
            sy.wait_ge(sS, s1)
            O.go(dbg["vTok"][c], lnf[:, 0:128])
        O.wait(sc, 0)
        sc.copy(sqx[:, 0:NTOK], qT)
        s1 = sig(sc, "S")
        sy.wait_ge(sS, s1)
        O.go(dbg["qT"], sqx[:, 0:NTOK])
        for hh in range(_DEBUG_HEADS):
            O.wait(sc, 0)
            sc.copy(lnf[:, 0:NTOK], oT[:, hh, :])
            s1 = sig(sc, "S")
            sy.wait_ge(sS, s1)
            O.go(dbg["oT"][:, hh * NTOK:(hh + 1) * NTOK], lnf[:, 0:NTOK])
        return finish()

    def jslice4(jj):
        return (PA, PB)[jj // 2][:, (jj % 2) * 512:(jj % 2) * 512 + 512]

    p_b2pre = n["P"]
    p_cch = {}
    v_ev = {0: 0, 1: 0}
    k = 0
    for jh in range(2):
        for mcol in range(4):
            for cch in range(16):
                par = cch % 2
                if k >= 2:
                    gp.wait_ge(sP, p_cch[k - 2])
                Ga.go(wao16[:, par, :],
                      w_ao[cch, :, mcol * 512:(mcol + 1) * 512], sub=par)
                Ga.wait(te, par)
                if k == 0:
                    te.wait_ge(sP, p_b2pre)
                for jj in range(4):
                    j = jh * 4 + jj
                    te.matmul(jslice4(jj), oT[:, cch, j * 128:(j + 1) * 128],
                              wao16[:, par, :],
                              start=(cch == 0), stop=(cch == 15))
                p_cch[k] = sig(te, "P")
                k += 1
            pj = p_cch[k - 1]
            for jj in range(4):
                j = jh * 4 + jj
                par = jj % 2
                sy.wait_ge(sV, v_ev[par])
                Dxr.go(xres[:, par, :],
                       xb[j * 128:(j + 1) * 128,
                          mcol * 512:(mcol + 1) * 512], sub=par)
                Dxr.wait(ve, par)
                ve.wait_ge(sP, pj)
                ve.tensor_add(x1_sb[:, j, mcol * 512:(mcol + 1) * 512],
                              jslice4(jj), xres[:, par, :])
                v_ev[par] = sig(ve, "V")
            te.wait_ge(sV, v_ev[par])
    v_x1 = n["V"]

    if _DEBUG_STOP == "B2":
        sy.wait_ge(sV, v_x1)
        for j in range(NJ):
            O.go(dbg["x1"][j * 128:(j + 1) * 128, :], x1_sb[:, j, :])
        return finish()

    s_qx = [0] * NJ
    for j in range(NJ):
        ve.wait_ge(sV, v_x1)
        sc.wait_ge(sV, v_x1)
        ln_stats(x1_sb[:, j, :])
        ve.tensor_scalar(lnf, x1_sb[:, j, :], stats[:, 2:3],
                         stats[:, 7:8], op0=ALU.subtract, op1=ALU.mult)
        ve.reduce_max(stats[:, 10:11], lnf, axis=AX.X,
                      apply_absolute_value=True)
        ve.reciprocal(stats[:, 11:12], stats[:, 10:11])
        ve.tensor_scalar_mul(stats[:, 12:13], stats[:, 11:12], 448.0)
        ve.tensor_scalar_mul(stats[:, 13:14], stats[:, 10:11], 1.0 / 448.0)
        v3 = sig(ve, "V")
        sc.wait_ge(sV, v3)
        if _DEBUG_STOP == "C0":
            O.wait(sc, 0)
        sc.activation(sqx, lnf, AF.Copy, scale=stats[:, 12:13])
        s4 = sig(sc, "S")
        ve.wait_ge(sS, s4)
        iv = sqx.bitcast(I32)
        ve.tensor_scalar(iv, iv, RADD, None, op0=ALU.add)
        ve.tensor_scalar(iv, iv, RMASK, None, op0=ALU.bitwise_and)
        ve.tensor_scalar_mul(sqx, sqx, stats[:, 13:14])
        v4 = sig(ve, "V")
        if _DEBUG_STOP == "C0":
            ve.wait_ge(sV, v4)
            sy.wait_ge(sV, v4)
            O.go(dbg["qx"][j * 128:(j + 1) * 128, :], sqx)
        te.wait_ge(sV, v4)
        tr_f32(sqx, M,
               lambda c, src, j=j: ve.tensor_copy(
                   qxT[:, c, j * 128:(j + 1) * 128], src))
        s_qx[j] = n["V"]
        sy.wait_ge(sV, s_qx[j])
        Dx1.go(x1_d[j * 128:(j + 1) * 128, :], x1_sb[:, j, :])

    if _DEBUG_STOP == "C0":
        return finish()

    tr_done = [0] * NIB
    st_done = [0] * NIC
    sil_done = [0] * NIC
    psum_free = {}
    ve.memset(mx_t, -3e38)
    ve.memset(mn_t, 3e38)
    Dx1.wait(te)
    Dx1.wait(sy)
    for ib in range(NIB):
        par = ib % 2
        if ib >= 2:
            sy.wait_ge(sP, tr_done[ib - 2])
        for hc in range(16):
            Dwg.go(wgT[:, par, hc, :],
                   qwg_d[ib * IBLK:(ib + 1) * IBLK, hc * 128:(hc + 1) * 128],
                   sub=par, transpose=True)
            Dwg.go(wuT[:, par, hc, :],
                   qwu_d[ib * IBLK:(ib + 1) * IBLK, hc * 128:(hc + 1) * 128],
                   sub=par, transpose=True)
        Dwg.wait(te, par)
        for q4 in range(4):
            ic = ib * 4 + q4
            bufs = (PA, PB, PC)
            gP, uP = bufs[(2 * ic) % 3], bufs[(2 * ic + 1) % 3]
            for X in (gP, uP):
                fr = psum_free.get(id(X))
                if fr is not None:
                    te.wait_ge(sems[fr[0]], fr[1])
            for hc in range(16):
                first, last = hc == 0, hc == 15
                for c0 in (0, 512):
                    te.matmul(gP[:, c0:c0 + 512],
                              wgT[:, par, hc, q4 * 128:(q4 + 1) * 128],
                              qxT[:, hc, c0:c0 + 512],
                              start=first, stop=last)
                    te.matmul(uP[:, c0:c0 + 512],
                              wuT[:, par, hc, q4 * 128:(q4 + 1) * 128],
                              qxT[:, hc, c0:c0 + 512],
                              start=first, stop=last)
            pg = sig(te, "P")
            if q4 == 3:
                tr_done[ib] = pg
            tpar = ic % 2
            sc.wait_ge(sP, pg)
            if ic >= 1:
                sc.wait_ge(sV, sil_done[ic - 1])
            sc.activation(gs_t, gP, AF.Silu, scale=wg_rs[:, ic:ic + 1])
            sg1 = sig(sc, "S")
            psum_free[id(gP)] = ("S", sg1)
            ve.wait_ge(sS, sg1)
            if ic >= 2:
                Dt.wait(ve, tpar)
            ve.tensor_mul(tstg[:, tpar, :], gs_t, uP)
            sil_done[ic] = n["V"] + 1
            ve.tensor_scalar_mul(tstg[:, tpar, :], tstg[:, tpar, :],
                                 wu_rs[:, ic:ic + 1])
            ve.tensor_max(mx_t, mx_t, tstg[:, tpar, :])
            ve.tensor_tensor(mn_t, mn_t, tstg[:, tpar, :], op=ALU.min)
            st_done[ic] = sig(ve, "V")
            psum_free[id(uP)] = ("V", st_done[ic])
            sy.wait_ge(sV, st_done[ic])
            Dt.go(tT_d[ic * 128:(ic + 1) * 128, :], tstg[:, tpar, :],
                  sub=tpar)

    if _DEBUG_STOP == "C1":
        Gq.wait_all(sy)
        Dw.wait_all(sy)
        for ic in range(NIC):
            O.wait(sy, 0)
            Dt.wait_all(sy)
            Ddbg.go(tg[:, 0, :], tT_d[ic * 128:(ic + 1) * 128, :])
            Ddbg.wait(sy)
            O.go(dbg["tT"][ic * 128:(ic + 1) * 128, :], tg[:, 0, :])
        return finish()

    ve.tensor_scalar_mul(mn_t, mn_t, -1.0)
    ve.tensor_max(mx_t, mx_t, mn_t)
    v_mx = sig(ve, "V")
    te.wait_ge(sV, v_mx)
    for jt in range(8):
        te.transpose(PDh[:, 0:128], mx_t[:, jt * 128:(jt + 1) * 128], identf)
        p_mx = pe_fence()
        ve.wait_ge(sP, p_mx)
        ve.reduce_max(rmx8[:, jt:jt + 1], PDh[:, 0:128], axis=AX.X)
        vr = sig(ve, "V")
        te.wait_ge(sV, vr)
    ve.reciprocal(stats[:, 20:28], rmx8)
    ve.nop()
    ve.tensor_scalar_mul(stats[:, 20:28], stats[:, 20:28], 448.0)
    ve.tensor_scalar_mul(rmx8, rmx8, 1.0 / 448.0)
    v_st = sig(ve, "V")
    te.wait_ge(sV, v_st)
    for jt in range(8):
        for which, dstP in ((0, PA), (1, PB)):
            src1 = stats[:, 20 + jt:21 + jt] if which == 0                 else rmx8[:, jt:jt + 1]
            te.transpose(PDh[0:1, 0:128], src1, identf)
            p1 = pe_fence()
            sc.wait_ge(sP, p1)
            sc.copy(row16[0:1, which, :], PDh[0:1, 0:128])
            s1 = sig(sc, "S")
            te.wait_ge(sS, s1)
            te.matmul(dstP[:, jt * 128:(jt + 1) * 128], ones_row,
                      row16[0:1, which, :], start=True, stop=True)
    p_bb = sig(te, "P")
    sc.wait_ge(sP, p_bb)
    sc.copy(st_b, PA)
    sc.copy(str_b, PB)
    s_b = sig(sc, "S")
    te.wait_ge(sS, s_b)

    Dt.wait_all(gp)
    Gq.wait_all(gp)
    Dw.wait_all(gp)
    vq_done = [0] * NIC
    for ic in range(NIC):
        par = ic % 2
        if ic >= 2:
            gp.wait_ge(sV, vq_done[ic - 2])
        Gg.gather(tg[:, par, :], tT_d, pi_sb[:, ic:ic + 1], sub=par)
        Gg.wait(ve, par)
        ve.wait_ge(sS, s_b)
        if ic >= 2:
            Dqw.wait(ve, par)
        ve.tensor_mul(tgq[:, par, :], tg[:, par, :], st_b)
        vq_done[ic] = n["V"] + 1
        iv = tgq[:, par, :].bitcast(I32)
        ve.tensor_scalar(iv, iv, RADD, None, op0=ALU.add)
        ve.tensor_scalar(iv, iv, RMASK, None, op0=ALU.bitwise_and)
        ve.tensor_mul(tgq[:, par, :], tgq[:, par, :], str_b)
        v1 = sig(ve, "V")
        sc.wait_ge(sV, v1)
        sc.copy(qt16[:, par, :], tgq[:, par, :])
        s1 = sig(sc, "S")
        sy.wait_ge(sS, s1)
        Dqw.go(qtT_d[ic * 128:(ic + 1) * 128, :], qt16[:, par, :], sub=par)

    if _DEBUG_STOP == "C3":
        for ic in range(NIC):
            O.wait(sc, 0)
            Dqw.wait_all(sy)
            Ddbg.go(qt16[:, 0, :], qtT_d[ic * 128:(ic + 1) * 128, :])
            Ddbg.wait(sc)
            sc.copy(tgq[:, 0, :], qt16[:, 0, :])
            s1 = sig(sc, "S")
            sy.wait_ge(sS, s1)
            O.go(dbg["qtT"][ic * 128:(ic + 1) * 128, :], tgq[:, 0, :])
        return finish()

    for tt in range(16):
        te.transpose(PDh[0:1, 0:128], wd_rs[:, tt:tt + 1], identf)
        p1 = pe_fence()
        sc.wait_ge(sP, p1)
        sc.copy(row16[0:1, 0, :], PDh[0:1, 0:128])
        s1 = sig(sc, "S")
        te.wait_ge(sS, s1)
        te.matmul((PA, PB)[tt // 8][:, (tt % 8) * 128:(tt % 8 + 1) * 128],
                  ones_row, row16[0:1, 0, :], start=True, stop=True)
    p_cb = sig(te, "P")
    sc.wait_ge(sP, p_cb)
    sc.copy(csc_b[:, 0:1024], PA)
    sc.copy(csc_b[:, 1024:2048], PB)
    s_cb = sig(sc, "S")
    te.wait_ge(sS, s_cb)
    Dqw.wait_all(sy)
    sy.wait_ge(sS, s_cb)

    for jh in range(2):
        for mcol in range(4):
            dn_done = [0] * NIC
            v_evprev = n["V"]
            for ic in range(NIC):
                par = ic % 2
                if ic >= 2:
                    sy.wait_ge(sP, dn_done[ic - 2])
                Dqs.go(qts[:, par, 0:512],
                       qtT_d[ic * 128:(ic + 1) * 128,
                             jh * 512:(jh + 1) * 512], sub=par)
                Dwd.go(wdTs[:, par, :],
                       qwd_d[mcol * 512:(mcol + 1) * 512,
                             ic * 128:(ic + 1) * 128],
                       sub=par, transpose=True)
                Dqs.wait(te, par)
                Dwd.wait(te, par)
                if ic == 0:
                    te.wait_ge(sV, v_evprev)
                for jj in range(4):
                    te.matmul(jslice4(jj),
                              qts[:, par, jj * 128:(jj + 1) * 128],
                              wdTs[:, par, :],
                              start=(ic == 0), stop=(ic == NIC - 1))
                dn_done[ic] = sig(te, "P")
            p_dn = dn_done[NIC - 1]
            for jj in range(4):
                j = jh * 4 + jj
                par = jj % 2
                Dxs.go(x1st[:, par, :],
                       x1_d[j * 128:(j + 1) * 128,
                            mcol * 512:(mcol + 1) * 512], sub=par)
                Dxs.wait(ve, par)
                ve.wait_ge(sP, p_dn)
                O.wait(ve, par)
                ve.tensor_mul(yev[:, par, :], jslice4(jj),
                              csc_b[:, mcol * 512:(mcol + 1) * 512])
                ve.tensor_add(yev[:, par, :], yev[:, par, :], x1st[:, par, :])
                v1 = sig(ve, "V")
                sy.wait_ge(sV, v1)
                O.go(out_d[j * 128:(j + 1) * 128,
                           mcol * 512:(mcol + 1) * 512],
                     yev[:, par, :], sub=par)
    return finish()


def kernel(L0_attn_x, L0_attn_w_aq, L0_attn_w_ak, L0_attn_w_av, L0_attn_w_ao,
           L0_ffn_w_gate, L0_ffn_w_up, L0_ffn_w_down,
           L0_ffn_up_reorder_index, L0_ffn_down_reorder_index):
    x = np.ascontiguousarray(np.asarray(L0_attn_x, dtype=np.float32))
    waq = np.ascontiguousarray(np.asarray(L0_attn_w_aq, dtype=np.float32))
    wak = np.ascontiguousarray(np.asarray(L0_attn_w_ak, dtype=np.float32))
    wav = np.ascontiguousarray(np.asarray(L0_attn_w_av, dtype=np.float32))
    wao = np.ascontiguousarray(np.asarray(L0_attn_w_ao, dtype=np.float32))
    wg = np.ascontiguousarray(np.asarray(L0_ffn_w_gate, dtype=np.float32))
    wu = np.ascontiguousarray(np.asarray(L0_ffn_w_up, dtype=np.float32))
    wd = np.ascontiguousarray(np.asarray(L0_ffn_w_down, dtype=np.float32))
    pi = np.asarray(L0_ffn_down_reorder_index).astype(np.int32)
    pi_dev = np.ascontiguousarray(pi.reshape(NIC, 128).T)

    nc, _ = build_program()
    in_maps = []
    perms = {}
    for c in range(8):
        b, p = c // 2, c % 2
        tokperm, cosF, sinFs, mkt, swap, identm = _host_tables(p)
        perms[c] = tokperm
        in_maps.append({
            "xb": np.ascontiguousarray(x[b][tokperm]),
            "w_aq": waq, "w_ak": wak, "w_av": wav, "w_ao": wao,
            "wg": wg, "wu": wu, "wd": wd, "pi": pi_dev,
            "ropecos": np.ascontiguousarray(cosF),
            "ropesin": np.ascontiguousarray(sinFs),
            "mk": np.ascontiguousarray(mkt),
            "ident": identm, "identf": np.eye(128, dtype=np.float32),
            "swapm": swap,
        })
    res = run_bass_kernel_spmd(nc, in_maps, core_ids=list(range(8)))
    out = np.empty((B, T, M), np.float32)
    for c in range(8):
        out[c // 2][perms[c][:NTOK]] = res.results[c]["out"]
    return out


# version 04
# speedup vs baseline: 1.0984x; 1.0384x over previous
import numpy as np
import ml_dtypes
import concourse.bass as bass
import concourse.mybir as mybir
from concourse.bass_utils import run_bass_kernel_spmd
from concourse.bass_core_rust import IndirectOffsetOnAxis

F32 = mybir.dt.float32
BF16 = mybir.dt.bfloat16
I32 = mybir.dt.int32
U8 = mybir.dt.uint8
AF = mybir.ActivationFunctionType
ALU = mybir.AluOpType
AX = mybir.AxisListType

B, T, M, NH, DH = 4, 2048, 2048, 16, 128
II = 5632
KT = T // 128
NJ = 8
NTOK = NJ * 128
NIC = II // 128
IBLK = 512
NIB = II // IBLK
RADD = 0x80000
RMASK = -0x100000

_DEBUG_STOP = None
_DEBUG_HEADS = NH
_DEBUG_TILES = KT
_DEBUG_SUB = 3
_DEBUG_BSUB = 9
_STRICT = [False]
_DEPGUARD = [True]


def _host_tables(p):
    own = [2 * j + p for j in range(NJ)]
    other = [t for t in range(KT) if t not in own]
    tokperm = np.concatenate([np.arange(t * 128, (t + 1) * 128)
                              for t in own + other])
    pos = tokperm.astype(np.float64)
    dims = np.arange(DH // 2, dtype=np.float64)
    ang = np.power(10000.0, -dims / (DH // 2))
    rad = pos[None, :] * ang[:, None]
    cos = np.cos(rad)
    sin = np.sin(rad)
    cosF = np.concatenate([cos, cos], 0).astype(ml_dtypes.bfloat16)
    sinFs = np.concatenate([-sin, sin], 0).astype(ml_dtypes.bfloat16)
    tri = (np.arange(128)[:, None] <= np.arange(128)[None, :]).astype(np.float32)
    mk = np.empty((KT, 128, 128), np.float32)
    for m in range(KT):
        mk[m] = tri if m < NJ else (0.0 if p == 0 else 1.0)
    swap = np.zeros((128, 128), np.float32)
    swap[:64, 64:] = np.eye(64)
    swap[64:, :64] = np.eye(64)
    return (tokperm, cosF, sinFs, mk.astype(ml_dtypes.bfloat16),
            swap.astype(ml_dtypes.bfloat16),
            np.eye(128, dtype=ml_dtypes.bfloat16))


def build_program():
    nc = bass.Bass("TRN2", target_bir_lowering=False, debug=False)

    xb = nc.dram_tensor("xb", [T, M], F32, kind="ExternalInput").ap()
    w_aq = nc.dram_tensor("w_aq", [NH, M, DH], F32, kind="ExternalInput").ap()
    w_ak = nc.dram_tensor("w_ak", [NH, M, DH], F32, kind="ExternalInput").ap()
    w_av = nc.dram_tensor("w_av", [NH, M, DH], F32, kind="ExternalInput").ap()
    w_ao = nc.dram_tensor("w_ao", [NH, DH, M], F32, kind="ExternalInput").ap()
    wg_d = nc.dram_tensor("wg", [II, M], F32, kind="ExternalInput").ap()
    wu_d = nc.dram_tensor("wu", [II, M], F32, kind="ExternalInput").ap()
    wd_d = nc.dram_tensor("wd", [M, II], F32, kind="ExternalInput").ap()
    pi_d = nc.dram_tensor("pi", [128, NIC], I32, kind="ExternalInput").ap()
    cos_d = nc.dram_tensor("ropecos", [128, T], BF16, kind="ExternalInput").ap()
    sin_d = nc.dram_tensor("ropesin", [128, T], BF16, kind="ExternalInput").ap()
    mk_d = nc.dram_tensor("mk", [KT, 128, 128], BF16, kind="ExternalInput").ap()
    id_d = nc.dram_tensor("ident", [128, 128], BF16, kind="ExternalInput").ap()
    sw_d = nc.dram_tensor("swapm", [128, 128], BF16, kind="ExternalInput").ap()
    idf_d = nc.dram_tensor("identf", [128, 128], F32, kind="ExternalInput").ap()

    out_d = nc.dram_tensor("out", [NTOK, M], F32, kind="ExternalOutput").ap()

    qwg_d = nc.dram_tensor("qwg", [II, M], BF16, kind="Internal").ap()
    qwu_d = nc.dram_tensor("qwu", [II, M], BF16, kind="Internal").ap()
    qwd_d = nc.dram_tensor("qwd", [M, II], BF16, kind="Internal").ap()
    tT_d = nc.dram_tensor("tT", [II, NTOK], F32, kind="Internal").ap()
    qtT_d = nc.dram_tensor("qtT", [II, NTOK], BF16, kind="Internal").ap()
    x1_d = nc.dram_tensor("x1s", [NTOK, M], F32, kind="Internal").ap()

    dbg = {}
    if _DEBUG_STOP is not None:
        for nm, shp in [("lnxT", [16, 128, T]), ("kT", [128, T]),
                        ("vTok", [KT, 128, 128]), ("qT", [128, NTOK]),
                        ("oT", [128, NH * NTOK]), ("x1", [NTOK, M]),
                        ("qx", [NTOK, M]), ("tT", [II, NTOK]),
                        ("qtT", [II, NTOK]), ("st", [128, 32]),
                        ("lnA", [128, M])]:
            dbg[nm] = nc.dram_tensor("dbg_" + nm, shp, F32,
                                     kind="ExternalOutput").ap()
        dbg["lnraw"] = nc.dram_tensor("dbg_lnraw", [16, 128, T], BF16,
                                      kind="ExternalOutput").ap()

    def sba(name, shape, dt):
        return nc.alloc_sbuf_tensor(name, list(shape), dt).ap()

    ident = sba("identc", [128, 128], BF16)
    identf = sba("identfc", [128, 128], F32)
    swapm = sba("swapc", [128, 128], BF16)
    mk = sba("mkc", [128, KT, 128], BF16)
    cosF = sba("cosc", [128, T], BF16)
    sinFs = sba("sinc", [128, T], BF16)
    pi_sb = sba("pic", [128, NIC], I32)
    ones_col = sba("onescol", [128, 1], BF16)
    ones_row = sba("onesrow", [1, 128], BF16)
    wg_sc = sba("wgsc", [128, NIC], F32)
    wu_sc = sba("wusc", [128, NIC], F32)
    wd_sc = sba("wdsc", [128, 16], F32)
    wg_rs = sba("wgrs", [128, NIC], F32)
    wu_rs = sba("wurs", [128, NIC], F32)
    wd_rs = sba("wdrs", [128, 16], F32)
    stats = sba("stats", [128, 32], F32)
    rmx8 = sba("rmx8", [128, 8], F32)
    row16 = sba("row16", [1, 2, 128], BF16)
    negc = sba("negc", [128, 1], F32)
    cm16 = sba("cm16", [128, 2], BF16)
    kT = sba("kT", [128, T], BF16)
    kvraw = sba("kvraw", [128, T], BF16)
    kTr, vT = kvraw, kvraw
    qTr = kvraw[:, 0:NTOK]
    vTok2 = sba("vTok2", [128, KT, 256], BF16)
    qT = sba("qT", [128, NTOK], BF16)
    pt = sba("pt", [128, 2, NTOK], BF16)

    def region(name, nbytes):
        return nc.alloc_sbuf_tensor(name, [128, nbytes], U8).ap()

    def view(reg, o, shape, dt):
        esz = {F32: 4, BF16: 2, I32: 4}[dt]
        r = reg.bitcast(dt)
        flat = 1
        for s in shape[1:]:
            flat *= s
        v = r[:, o // esz:o // esz + flat]
        if len(shape) == 2:
            return v
        pat = "p (" + " ".join(f"d{i}" for i in range(1, len(shape))) + ") ->" \
              + " p " + " ".join(f"d{i}" for i in range(1, len(shape)))
        kw = {f"d{i}": shape[i] for i in range(1, len(shape) - 1)}
        return v.rearrange(pat, **kw)

    RG1 = region("RG1", 65536)
    lnxT = view(RG1, 0, [128, 16, T], BF16)
    x1_sb = view(RG1, 0, [128, NJ, M], F32)
    wgT = view(RG1, 0, [128, 2, 16, IBLK], BF16)
    wuT = view(RG1, 32768, [128, 2, 16, IBLK], BF16)
    RG2 = region("RG2", 32768)
    oT = view(RG2, 0, [128, NH, NTOK], BF16)
    qxT = view(RG2, 0, [128, 16, NTOK], BF16)
    RG3 = region("RG3", 36864)
    xst = view(RG3, 0, [128, 2, M], F32)
    lnA = view(RG3, 16384, [128, M], F32)
    wq16 = view(RG3, 0, [128, 2, 16, DH], BF16)
    wk16 = view(RG3, 8192, [128, 2, 16, DH], BF16)
    wv2 = view(RG3, 16384, [128, 16, 2, 128], BF16)
    wv2f = view(RG3, 16384, [128, 16, 256], BF16)
    linv = view(RG3, 24576, [128, NTOK], F32)
    rowB = view(RG3, 28672, [1, 1024], F32)
    rowA16 = view(RG3, 32768, [1, 1024], BF16)
    rowA = view(RG3, 34816, [1, 512], F32)
    wao16 = view(RG3, 0, [128, 2, 512], BF16)
    xres = view(RG3, 2048, [128, 2, 512], F32)
    sqx = view(RG3, 0, [128, M], F32)
    lnf = view(RG3, 8192, [128, M], F32)
    tstg = view(RG3, 16384, [128, 2, NTOK], F32)
    gs_t = view(RG3, 24576, [128, NTOK], F32)
    mx_t = view(RG3, 28672, [128, NTOK], F32)
    mn_t = view(RG3, 32768, [128, NTOK], F32)
    lnsq = view(RG3, 24576, [128, M], F32)
    st_b = view(RG3, 28672, [128, NTOK], F32)
    str_b = view(RG3, 32768, [128, NTOK], F32)
    csc_b = view(RG3, 16384, [128, M], F32)
    RG4 = region("RG4", 28672)
    wst = view(RG4, 0, [128, 2, 2816], F32)
    tg = view(RG4, 0, [128, 2, NTOK], F32)
    tgq = view(RG4, 8192, [128, 2, NTOK], F32)
    qt16 = view(RG4, 16384, [128, 2, NTOK], BF16)
    wdTs = view(RG4, 20480, [128, 2, 512], BF16)
    x1st = view(RG4, 22528, [128, 2, 512], F32)
    qts = view(RG4, 24576, [128, 2, NTOK], BF16)
    yev = view(RG4, 0, [128, 2, 512], F32)

    PA = nc.alloc_psum_tensor("PA", [128, 1024], F32).ap()
    PB = nc.alloc_psum_tensor("PB", [128, 1024], F32).ap()
    PC = nc.alloc_psum_tensor("PC", [128, 1024], F32).ap()
    PDh = nc.alloc_psum_tensor("PDh", [128, 512], F32).ap()
    PF = nc.alloc_psum_tensor("PF", [1, 128], F32).ap()

    _GUARD = {"tensor_copy", "tensor_tensor", "tensor_add", "tensor_sub",
              "tensor_mul", "tensor_max", "tensor_scalar",
              "tensor_single_scalar", "tensor_scalar_add",
              "tensor_scalar_mul", "tensor_scalar_max", "tensor_scalar_min",
              "tensor_reduce", "reduce_max", "reduce_sum", "reciprocal",
              "memset", "activation", "square", "sqrt", "copy", "mul",
              "add", "select"}

    _ESZ = {F32: 4, BF16: 2, I32: 4, U8: 1}

    def _rng(ap):
        try:
            esz = _ESZ.get(ap.dtype, 4)
            span = 1
            for i, (st_, ct) in enumerate(ap.ap):
                if i == 0:
                    continue
                span += (ct - 1) * abs(st_)
            lo = ap.offset * esz
            return (ap.tensor.name, lo, lo + span * esz)
        except Exception:
            try:
                return (ap.tensor.name, 0, 1 << 60)
            except Exception:
                return None

    class _EG:
        def __init__(self, e):
            self._e = e
            self.last = None
            self._hist = []

        def __getattr__(self, name):
            f = getattr(self._e, name)
            if name in _GUARD:
                def g(*a, **k):
                    aps = [x for x in list(a) + list(k.values())
                           if hasattr(x, "tensor") and hasattr(x, "offset")]
                    outr = _rng(aps[0]) if aps else None
                    fs = aps[0].free_size() if aps else 1024
                    need_drain = need_nop = False
                    for idx, (pw, pfs) in enumerate(self._hist):
                        for x in aps[1:]:
                            rr = _rng(x)
                            if rr is None or (pw is not None
                                              and rr[0] == pw[0]
                                              and rr[1] < pw[2]
                                              and pw[1] < rr[2]):
                                if fs < 512 or pfs < 512:
                                    need_drain = True
                                elif idx == 0:
                                    need_nop = True
                    if not _DEPGUARD[0]:
                        need_drain = fs < 512
                        need_nop = True
                    if _STRICT[0] or need_drain:
                        self._e.drain()
                        self._hist.clear()
                    elif need_nop:
                        self._e.nop()
                    r = f(*a, **k)
                    self.last = r
                    self._hist.insert(0, (outr, fs))
                    del self._hist[3:]
                    return r
                return g
            return f

    class _TG:
        def __init__(self, e):
            self._e = e
            self.last = None

        def __getattr__(self, name):
            f = getattr(self._e, name)
            if name in ("transpose", "matmul"):
                def g(*a, **k):
                    r = f(*a, **k)
                    self.last = r
                    return r
                return g
            return f

    sy = nc.sync
    gp = _EG(nc.gpsimd)
    ve = _EG(nc.vector)
    sc = _EG(nc.scalar)
    te = _TG(nc.tensor)
    sems = {k: nc.alloc_semaphore(k) for k in ("P", "V", "S")}
    n = {k: 0 for k in sems}
    sP, sV, sS = (sems[k] for k in ("P", "V", "S"))

    class DStream:
        def __init__(self, name, nsub=1, eng=None):
            self.sems = [nc.alloc_semaphore(name + str(i))
                         for i in range(nsub)]
            self.n = [0] * nsub
            self.eng = eng if eng is not None else sy

        def go(self, out, in_, sub=0, transpose=False):
            if transpose:
                r = self.eng.dma_start_transpose(out, in_)
            else:
                r = self.eng.dma_start(out=out, in_=in_)
            r.then_inc(self.sems[sub], 16)
            self.n[sub] += 16

        def gather(self, out, in_, idx, sub=0):
            self.eng.indirect_dma_start(
                out, None, in_, IndirectOffsetOnAxis(idx, 0)) \
                .then_inc(self.sems[sub], 16)
            self.n[sub] += 16

        def wait(self, eng, sub=0):
            eng.wait_ge(self.sems[sub], self.n[sub])

        def wait_all(self, eng):
            for i, s in enumerate(self.sems):
                eng.wait_ge(s, self.n[i])

    Dc = DStream("Dc")
    Dx = DStream("Dx", 2)
    Dw = DStream("Dw", 2)
    Gq = DStream("Gq", 2, gp)
    Gw = DStream("Gw", 2, gp)
    Gv = DStream("Gv", 1, gp)
    Ga = DStream("Ga", 2, gp)
    Dxr = DStream("Dxr", 2)
    Dx1 = DStream("Dx1")
    Dwg = DStream("Dwg", 2)
    Dwu = DStream("Dwu", 2)
    Dt = DStream("Dt", 2)
    Gg = DStream("Gg", 2, gp)
    Dqw = DStream("Dqw", 2)
    Dqs = DStream("Dqs", 2)
    Dwd = DStream("Dwd", 2)
    Dxs = DStream("Dxs", 2)
    O = DStream("O", 2)
    Ddbg = DStream("Ddbg")
    streams = [Dc, Dx, Dw, Gq, Gw, Gv, Ga, Dxr, Dx1, Dwg, Dwu, Dt, Gg,
               Dqw, Dqs, Dwd, Dxs, O, Ddbg]

    def sig(eng, key):
        eng.last.then_inc(sems[key], 1)
        n[key] += 1
        return n[key]

    def finish():
        for e in (sy, gp, ve, sc, te):
            O.wait_all(e)
        return nc, dbg

    for st_ in streams:
        for s in st_.sems:
            sy.sem_clear(s)
    for s in sems.values():
        sy.sem_clear(s)
    nc.all_engine_barrier()
    Dc.go(ident, id_d)
    Dc.go(swapm, sw_d)
    Dc.go(identf, idf_d)
    Dc.go(mk, mk_d.rearrange("k p q -> p k q"))
    Dc.go(cosF, cos_d)
    Dc.go(sinFs, sin_d)
    Dc.go(pi_sb, pi_d)
    Dc.wait(ve)
    if _DEBUG_STOP is not None:
        for rg_ in (RG1, RG2, RG3, RG4):
            ve.memset(rg_.bitcast(F32), 0.0)
        ve.memset(stats, 0.0)
        ve.memset(rmx8, 0.0)
    ve.memset(ones_col, 1.0)
    ve.memset(ones_row, 1.0)
    v0 = sig(ve, "V")
    for e in (te, sc, gp, sy):
        e.wait_ge(sV, v0)

    def pe_fence():
        return sig(te, "P")

    def tr_f32(src_f32, ncols, write_fn):
        nt = ncols // 128
        done = 0
        while done < nt:
            batch = min(16, nt - done)
            for k in range(batch):
                c = done + k
                P, o = (PA, PB)[k // 8], (k % 8) * 128
                te.transpose(P[:, o:o + 128],
                             src_f32[:, c * 128:(c + 1) * 128], identf)
            p1 = pe_fence()
            ve.wait_ge(sP, p1)
            for k in range(batch):
                c = done + k
                P, o = (PA, PB)[k // 8], (k % 8) * 128
                write_fn(c, P[:, o:o + 128])
            v1 = sig(ve, "V")
            te.wait_ge(sV, v1)
            done += batch

    lnsq_free = [0]
    sqrt_sig = [0]

    def ln_stats(src_ap):
        ve.wait_ge(sS, sqrt_sig[0])
        ve.reduce_sum(stats[:, 0:1], src_ap, axis=AX.X)
        sc.wait_ge(sV, lnsq_free[0])
        sc.square(lnsq, src_ap)
        s1 = sig(sc, "S")
        ve.wait_ge(sS, s1)
        ve.reduce_sum(stats[:, 1:2], lnsq, axis=AX.X)
        lnsq_free[0] = n["V"] + 1
        ve.tensor_scalar_mul(stats[:, 2:3], stats[:, 0:1], 1.0 / M)
        ve.tensor_scalar_mul(stats[:, 3:4], stats[:, 1:2], 1.0 / M)
        ve.tensor_mul(stats[:, 4:5], stats[:, 2:3], stats[:, 2:3])
        ve.tensor_sub(stats[:, 5:6], stats[:, 3:4], stats[:, 4:5])
        ve.tensor_scalar_add(stats[:, 5:6], stats[:, 5:6], 1e-5)
        v1 = sig(ve, "V")
        sc.wait_ge(sV, v1)
        sc.sqrt(stats[:, 6:7], stats[:, 5:6])
        s2 = sig(sc, "S")
        sqrt_sig[0] = s2
        ve.wait_ge(sS, s2)
        ve.reciprocal(stats[:, 7:8], stats[:, 6:7])
        v2 = sig(ve, "V")
        sc.wait_ge(sV, v2)


    s_lnuse = [0] * KT
    for i in range(_DEBUG_TILES):
        par = i % 2
        if i >= 2:
            sy.wait_ge(sV, s_lnuse[i - 2])
        Dx.go(xst[:, par, :], xb[i * 128:(i + 1) * 128, :], sub=par)
        Dx.wait(ve, par)
        Dx.wait(sc, par)
        if _DEBUG_SUB >= 1:
            ln_stats(xst[:, par, :])
        if _DEBUG_SUB >= 2:
            if i >= 1:
                ve.wait_ge(sP, p_lnA[i - 1])
            ve.tensor_scalar(lnA, xst[:, par, :], stats[:, 2:3],
                             stats[:, 7:8], op0=ALU.subtract, op1=ALU.mult)
            s_lnuse[i] = sig(ve, "V")
        if _DEBUG_SUB >= 3:
            te.wait_ge(sV, s_lnuse[i])
            tr_f32(lnA, M,
                   lambda c, src, i=i: ve.tensor_copy(
                       lnxT[:, c, i * 128:(i + 1) * 128], src))
        if i == 0:
            p_lnA = [0] * KT
        p_lnA[i] = n["P"]
    v_endA = n["V"]

    if _DEBUG_STOP == "A":
        sy.wait_ge(sV, v_endA)
        O.go(dbg["st"], stats)
        O.go(dbg["lnA"], lnA)
        for c in range(16):
            O.go(dbg["lnraw"][c], lnxT[:, c, :])
        for c in range(16):
            sc.wait_ge(sV, v_endA)
            O.wait(sc, 0)
            sc.copy(lnf, lnxT[:, c, :])
            s1 = sig(sc, "S")
            sy.wait_ge(sS, s1)
            O.go(dbg["lnxT"][c], lnf)
        return finish()

    w_units = []
    for t in range(II // 128):
        w_units.append(("g", t))
        w_units.append(("u", t))
    for t in range(16):
        w_units.append(("dmax", t))
        for q in range(4):
            w_units.append(("d", t, q))
    wpar = [0]
    w_free = {0: None, 1: None}
    w_dmax_v = {}

    def w_wait_free(par):
        g = w_free[par]
        if g is None:
            return
        if g[0] == "V":
            sy.wait_ge(sV, g[1])
        else:
            Gq.wait(sy, par)

    def w_round_store(par, width, scale_col, dst):
        sc.activation(wst[:, par, 0:width], wst[:, par, 0:width], AF.Copy,
                      scale=scale_col)
        s1 = sig(sc, "S")
        ve.wait_ge(sS, s1)
        iv = wst[:, par, 0:width].bitcast(I32)
        ve.tensor_scalar(iv, iv, RADD, None, op0=ALU.add)
        ve.tensor_scalar(iv, iv, RMASK, None, op0=ALU.bitwise_and)
        v2 = sig(ve, "V")
        gp.wait_ge(sV, v2)
        Gq.go(dst, wst[:, par, 0:width], sub=par)
        w_free[par] = ("Gq", par)

    def emit_w_unit(u):
        if u[0] in ("g", "u"):
            _, t = u
            src, dst = (wg_d, qwg_d) if u[0] == "g" else (wu_d, qwu_d)
            scl = wg_sc if u[0] == "g" else wu_sc
            rsl = wg_rs if u[0] == "g" else wu_rs
            par = wpar[0] % 2
            wpar[0] += 1
            w_wait_free(par)
            Dw.go(wst[:, par, 0:2048], src[t * 128:(t + 1) * 128, :], sub=par)
            Dw.wait(ve, par)
            ve.reduce_max(stats[:, 10:11], wst[:, par, 0:2048], axis=AX.X,
                          apply_absolute_value=True)
            ve.reciprocal(stats[:, 11:12], stats[:, 10:11])
            ve.tensor_scalar_mul(scl[:, t:t + 1], stats[:, 11:12], 448.0)
            ve.tensor_scalar_mul(rsl[:, t:t + 1], stats[:, 10:11], 1 / 448.0)
            v1 = sig(ve, "V")
            sc.wait_ge(sV, v1)
            w_round_store(par, 2048, scl[:, t:t + 1],
                          dst[t * 128:(t + 1) * 128, :])
        elif u[0] == "dmax":
            _, t = u
            for qq in range(4):
                par = wpar[0] % 2
                wpar[0] += 1
                w_wait_free(par)
                Dw.go(wst[:, par, 0:1408],
                      wd_d[t * 128:(t + 1) * 128, qq * 1408:(qq + 1) * 1408],
                      sub=par)
                Dw.wait(ve, par)
                ve.reduce_max(stats[:, 12 + qq:13 + qq], wst[:, par, 0:1408],
                              axis=AX.X, apply_absolute_value=True)
                w_free[par] = ("V", sig(ve, "V"))
            ve.tensor_max(stats[:, 12:13], stats[:, 12:13], stats[:, 13:14])
            ve.tensor_max(stats[:, 14:15], stats[:, 14:15], stats[:, 15:16])
            ve.tensor_max(stats[:, 12:13], stats[:, 12:13], stats[:, 14:15])
            ve.reciprocal(stats[:, 13:14], stats[:, 12:13])
            ve.tensor_scalar_mul(wd_sc[:, t:t + 1], stats[:, 13:14], 448.0)
            ve.tensor_scalar_mul(wd_rs[:, t:t + 1], stats[:, 12:13], 1 / 448.0)
            w_dmax_v[t] = sig(ve, "V")
        else:
            _, t, q = u
            par = wpar[0] % 2
            wpar[0] += 1
            w_wait_free(par)
            Dw.go(wst[:, par, 0:1408],
                  wd_d[t * 128:(t + 1) * 128, q * 1408:(q + 1) * 1408],
                  sub=par)
            Dw.wait(sc, par)
            sc.wait_ge(sV, w_dmax_v[t])
            w_round_store(par, 1408, wd_sc[:, t:t + 1],
                          qwd_d[t * 128:(t + 1) * 128,
                                q * 1408:(q + 1) * 1408])

    p_qproj_prev = [0]

    def fetch_head_w(h):
        hp = h % 2
        if h >= 2:
            gp.wait_ge(sP, p_qproj_prev[0])
        Gw.go(wq16[:, hp, :, :],
              w_aq[h].rearrange("(c p) d -> p c d", p=128), sub=hp)
        Gw.go(wk16[:, hp, :, :],
              w_ak[h].rearrange("(c p) d -> p c d", p=128), sub=hp)

    def fetch_wv2(h):
        Gv.go(wv2[:, :, 0, :], w_av[h].rearrange("(c p) d -> p c d", p=128))
        Gv.go(wv2[:, :, 1, :],
              w_av[h + 1].rearrange("(c p) d -> p c d", p=128))

    gp.wait_ge(sV, v_endA)
    fetch_head_w(0)
    WPH = (len(w_units) + NH - 1) // NH

    def proj(dst16, wsb, hp, psum, ofs):
        for mc in range(16):
            first, last = mc == 0, mc == 15
            for c0 in (0, 512):
                te.matmul(psum[:, c0:c0 + 512], wsb[:, hp, mc, :],
                          lnxT[:, mc, ofs + c0:ofs + c0 + 512],
                          start=first, stop=last)
        p1 = sig(te, "P")
        sc.wait_ge(sP, p1)
        sc.copy(dst16[:, ofs:ofs + 1024], psum[:, 0:1024])
        s1 = sig(sc, "S")
        te.wait_ge(sS, s1)
        return s1

    s_v2 = 0
    p_v2mm = 0
    pv_done_prev = 0
    for h in range(_DEBUG_HEADS):
        hp = h % 2
        if h + 1 < NH:
            fetch_head_w(h + 1)
        Gw.wait(te, hp)
        if hp == 0 and _DEBUG_BSUB >= 1:
            if h >= 2:
                gp.wait_ge(sP, p_v2mm)
            fetch_wv2(h)
            Gv.wait(te)
            te.wait_ge(sS, s_v2)
            if h >= 2:
                te.wait_ge(sP, pv_done_prev)
            sv2_evac = [0] * KT
            for tt in range(KT):
                VP = PDh if tt % 2 == 0 else PC
                if tt >= 2:
                    te.wait_ge(sS, sv2_evac[tt - 2])
                for mc in range(16):
                    te.matmul(VP[:, 0:256],
                              lnxT[:, mc, tt * 128:(tt + 1) * 128],
                              wv2f[:, mc, :],
                              start=(mc == 0), stop=(mc == 15))
                p1 = sig(te, "P")
                sc.wait_ge(sP, p1)
                sc.copy(vTok2[:, tt, :], VP[:, 0:256])
                sv2_evac[tt] = sig(sc, "S")
            s_v2 = sv2_evac[KT - 1]
            p_v2mm = n["P"]
            te.wait_ge(sS, s_v2)
        if _DEBUG_BSUB < 2:
            continue
        proj(kTr, wk16, hp, PA, 0)
        sk1 = proj(kTr, wk16, hp, PB, 1024)
        for c0 in range(0, T, 512):
            te.matmul((PA if c0 < 1024 else PB)[:, c0 % 1024:c0 % 1024 + 512],
                      swapm, kTr[:, c0:c0 + 512], start=True, stop=True)
        pswap = sig(te, "P")
        ve.wait_ge(sS, sk1)
        ve.tensor_mul(kT, kTr, cosF)
        ve.wait_ge(sP, pswap)
        ve.tensor_mul(kTr[:, 0:1024], PA, sinFs[:, 0:1024])
        ve.tensor_mul(kTr[:, 1024:2048], PB, sinFs[:, 1024:2048])
        ve.tensor_add(kT, kT, kTr)
        ve.reduce_max(stats[:, 16:17], kT, axis=AX.X,
                      apply_absolute_value=True)
        vk = sig(ve, "V")
        te.wait_ge(sV, vk)
        if _DEBUG_BSUB < 3:
            continue
        sc.wait_ge(sV, vk)
        sq0 = proj(qTr, wq16, hp, PA, 0)
        p_qproj_prev[0] = n["P"]
        for c0 in (0, 512):
            te.matmul(PA[:, c0:c0 + 512], swapm, qTr[:, c0:c0 + 512],
                      start=True, stop=True)
        pswq = sig(te, "P")
        ve.wait_ge(sS, sq0)
        ve.tensor_mul(qT, qTr, cosF[:, 0:NTOK])
        ve.wait_ge(sP, pswq)
        ve.tensor_mul(qTr, PA, sinFs[:, 0:NTOK])
        ve.tensor_add(qT, qT, qTr)
        ve.reduce_max(stats[:, 17:18], qT, axis=AX.X,
                      apply_absolute_value=True)
        vq = sig(ve, "V")
        te.wait_ge(sV, vq)
        sc.wait_ge(sV, vq)
        sc.copy(cm16[:, 0:1], stats[:, 16:17])
        sc.copy(cm16[:, 1:2], stats[:, 17:18])
        scm = sig(sc, "S")
        te.wait_ge(sS, scm)
        te.matmul(PA[0:1, 0:1], cm16[:, 0:1], cm16[:, 1:2],
                  start=True, stop=True)
        pc1 = sig(te, "P")
        sc.wait_ge(sP, pc1)
        sc.copy(rowA16[0:1, 0:1], PA[0:1, 0:1])
        sc1 = sig(sc, "S")
        te.wait_ge(sS, sc1)
        te.matmul(PA[:, 0:1], ones_row, rowA16[0:1, 0:1],
                  start=True, stop=True)
        pc2 = sig(te, "P")
        sc.wait_ge(sP, pc2)
        sc.activation(negc, PA[:, 0:1], AF.Copy, scale=-1.0 / 128.0)
        snegc = sig(sc, "S")
        te.wait_ge(sS, snegc)

        if _DEBUG_BSUB < 4:
            continue
        exp_done = [0] * KT
        pv_done = [0] * KT
        for m in range(KT):
            j0 = m if m < NJ else m - NJ
            q0 = j0 * 128
            SP = PC
            if m >= 1:
                te.wait_ge(sS, exp_done[m - 1])
            c0 = q0
            while c0 < NTOK:
                ce = min((c0 // 512 + 1) * 512, NTOK)
                te.matmul(SP[:, c0:ce], kT[:, m * 128:(m + 1) * 128],
                          qT[:, c0:ce], start=True, stop=True)
                c0 = ce
            ps = sig(te, "P")
            sc.wait_ge(sP, ps)
            if m >= 2:
                sc.wait_ge(sP, pv_done[m - 2])
            sc.activation(pt[:, m % 2, q0:NTOK], SP[:, q0:NTOK], AF.Exp,
                          bias=negc, scale=1.0 / 128.0)
            exp_done[m] = sig(sc, "S")
            ve.wait_ge(sS, exp_done[m])
            ve.tensor_mul(pt[:, m % 2, q0:q0 + 128],
                          pt[:, m % 2, q0:q0 + 128], mk[:, m, :])
            vm1 = sig(ve, "V")
            te.wait_ge(sV, vm1)
            c0 = q0
            while c0 < NTOK:
                ce = min((c0 // 512 + 1) * 512, NTOK)
                stop = (m == 11) if c0 < 512 else (m == 15)
                te.matmul(PA[0:1, c0:ce], ones_col, pt[:, m % 2, c0:ce],
                          start=(m == 0), stop=stop)
                te.matmul(PB[:, c0:ce],
                          vTok2[:, m, hp * 128:(hp + 1) * 128],
                          pt[:, m % 2, c0:ce],
                          start=(m == 0), stop=stop)
                c0 = ce
            pv_done[m] = sig(te, "P")
        pv_done_prev = pv_done[KT - 1]
        if _DEBUG_BSUB < 5:
            continue
        sc.wait_ge(sP, pv_done[KT - 1])
        sc.copy(rowA[0:1, 0:512], PA[0:1, 0:512])
        sc.copy(rowB[0:1, 0:512], PA[0:1, 512:1024])
        sl1 = sig(sc, "S")
        ve.wait_ge(sS, sl1)
        ve.reciprocal(rowB[0:1, 512:1024], rowB[0:1, 0:512])
        ve.reciprocal(rowA[0:1, 0:512], rowA[0:1, 0:512])
        ve.tensor_copy(rowA16[0:1, 0:512], rowA[0:1, 0:512])
        ve.tensor_copy(rowA16[0:1, 512:1024], rowB[0:1, 512:1024])
        vl1 = sig(ve, "V")
        te.wait_ge(sV, vl1)
        te.matmul(PC[:, 0:512], ones_row, rowA16[0:1, 0:512],
                  start=True, stop=True)
        te.matmul(PC[:, 512:1024], ones_row, rowA16[0:1, 512:1024],
                  start=True, stop=True)
        pl1 = sig(te, "P")
        sc.wait_ge(sP, pl1)
        sc.copy(linv, PC)
        sl2 = sig(sc, "S")
        ve.wait_ge(sS, sl2)
        ve.tensor_mul(oT[:, h, :], PB, linv)
        voT = sig(ve, "V")
        te.wait_ge(sV, voT)
        pv_done_prev = pv_done[KT - 1]
        for u in w_units[h * WPH:(h + 1) * WPH]:
            emit_w_unit(u)

    if _DEBUG_STOP == "B":
        v = n["V"]
        sc.wait_ge(sV, v)
        Gw.wait_all(sc)
        Gv.wait_all(sc)
        sc.copy(sqx[:, 0:T], kT)
        s1 = sig(sc, "S")
        sy.wait_ge(sS, s1)
        O.go(dbg["kT"], sqx[:, 0:T])
        for c in range(KT):
            O.wait(sc, 0)
            sc.copy(lnf[:, 0:128], vTok2[:, c, 0:128] if _DEBUG_HEADS % 2
                    else vTok2[:, c, 128:256])
            s1 = sig(sc, "S")
            sy.wait_ge(sS, s1)
            O.go(dbg["vTok"][c], lnf[:, 0:128])
        O.wait(sc, 0)
        sc.copy(sqx[:, 0:NTOK], qT)
        s1 = sig(sc, "S")
        sy.wait_ge(sS, s1)
        O.go(dbg["qT"], sqx[:, 0:NTOK])
        for hh in range(_DEBUG_HEADS):
            O.wait(sc, 0)
            sc.copy(lnf[:, 0:NTOK], oT[:, hh, :])
            s1 = sig(sc, "S")
            sy.wait_ge(sS, s1)
            O.go(dbg["oT"][:, hh * NTOK:(hh + 1) * NTOK], lnf[:, 0:NTOK])
        return finish()

    def jslice4(jj):
        return (PA, PB)[jj // 2][:, (jj % 2) * 512:(jj % 2) * 512 + 512]

    p_b2pre = n["P"]
    p_cch = {}
    v_ev = {0: 0, 1: 0}
    k = 0
    for jh in range(2):
        for mcol in range(4):
            for cch in range(16):
                par = cch % 2
                if k >= 2:
                    gp.wait_ge(sP, p_cch[k - 2])
                Ga.go(wao16[:, par, :],
                      w_ao[cch, :, mcol * 512:(mcol + 1) * 512], sub=par)
                Ga.wait(te, par)
                if k == 0:
                    te.wait_ge(sP, p_b2pre)
                for jj in range(4):
                    j = jh * 4 + jj
                    te.matmul(jslice4(jj), oT[:, cch, j * 128:(j + 1) * 128],
                              wao16[:, par, :],
                              start=(cch == 0), stop=(cch == 15))
                p_cch[k] = sig(te, "P")
                k += 1
            pj = p_cch[k - 1]
            for jj in range(4):
                j = jh * 4 + jj
                par = jj % 2
                sy.wait_ge(sV, v_ev[par])
                Dxr.go(xres[:, par, :],
                       xb[j * 128:(j + 1) * 128,
                          mcol * 512:(mcol + 1) * 512], sub=par)
                Dxr.wait(ve, par)
                ve.wait_ge(sP, pj)
                ve.tensor_add(x1_sb[:, j, mcol * 512:(mcol + 1) * 512],
                              jslice4(jj), xres[:, par, :])
                v_ev[par] = sig(ve, "V")
            te.wait_ge(sV, v_ev[par])
    v_x1 = n["V"]

    if _DEBUG_STOP == "B2":
        sy.wait_ge(sV, v_x1)
        for j in range(NJ):
            O.go(dbg["x1"][j * 128:(j + 1) * 128, :], x1_sb[:, j, :])
        return finish()

    s_qx = [0] * NJ
    for j in range(NJ):
        ve.wait_ge(sV, v_x1)
        sc.wait_ge(sV, v_x1)
        ln_stats(x1_sb[:, j, :])
        ve.tensor_scalar(lnf, x1_sb[:, j, :], stats[:, 2:3],
                         stats[:, 7:8], op0=ALU.subtract, op1=ALU.mult)
        ve.reduce_max(stats[:, 10:11], lnf, axis=AX.X,
                      apply_absolute_value=True)
        ve.reciprocal(stats[:, 11:12], stats[:, 10:11])
        ve.tensor_scalar_mul(stats[:, 12:13], stats[:, 11:12], 448.0)
        ve.tensor_scalar_mul(stats[:, 13:14], stats[:, 10:11], 1.0 / 448.0)
        v3 = sig(ve, "V")
        sc.wait_ge(sV, v3)
        if _DEBUG_STOP == "C0":
            O.wait(sc, 0)
        sc.activation(sqx, lnf, AF.Copy, scale=stats[:, 12:13])
        s4 = sig(sc, "S")
        ve.wait_ge(sS, s4)
        iv = sqx.bitcast(I32)
        ve.tensor_scalar(iv, iv, RADD, None, op0=ALU.add)
        ve.tensor_scalar(iv, iv, RMASK, None, op0=ALU.bitwise_and)
        ve.tensor_scalar_mul(sqx, sqx, stats[:, 13:14])
        v4 = sig(ve, "V")
        if _DEBUG_STOP == "C0":
            ve.wait_ge(sV, v4)
            sy.wait_ge(sV, v4)
            O.go(dbg["qx"][j * 128:(j + 1) * 128, :], sqx)
        te.wait_ge(sV, v4)
        tr_f32(sqx, M,
               lambda c, src, j=j: ve.tensor_copy(
                   qxT[:, c, j * 128:(j + 1) * 128], src))
        s_qx[j] = n["V"]
        sy.wait_ge(sV, s_qx[j])
        Dx1.go(x1_d[j * 128:(j + 1) * 128, :], x1_sb[:, j, :])

    if _DEBUG_STOP == "C0":
        return finish()

    tr_done = [0] * NIB
    st_done = [0] * NIC
    sil_done = [0] * NIC
    psum_free = {}
    ve.memset(mx_t, -3e38)
    ve.memset(mn_t, 3e38)
    Dx1.wait(te)
    Dx1.wait(sy)
    Dx1.wait(sc)
    Gq.wait_all(sy)
    Gq.wait_all(sc)
    for ib in range(NIB):
        par = ib % 2
        if ib >= 2:
            sy.wait_ge(sP, tr_done[ib - 2])
            sc.wait_ge(sP, tr_done[ib - 2])
        for hc in range(16):
            Dwg.go(wgT[:, par, hc, :],
                   qwg_d[ib * IBLK:(ib + 1) * IBLK, hc * 128:(hc + 1) * 128],
                   sub=par, transpose=True)
            Dwu.go(wuT[:, par, hc, :],
                   qwu_d[ib * IBLK:(ib + 1) * IBLK, hc * 128:(hc + 1) * 128],
                   sub=par, transpose=True)
        Dwg.wait(te, par)
        Dwu.wait(te, par)
        for q4 in range(4):
            ic = ib * 4 + q4
            bufs = (PA, PB, PC)
            gP, uP = bufs[(2 * ic) % 3], bufs[(2 * ic + 1) % 3]
            for X in (gP, uP):
                fr = psum_free.get(id(X))
                if fr is not None:
                    te.wait_ge(sems[fr[0]], fr[1])
            for hc in range(16):
                first, last = hc == 0, hc == 15
                for c0 in (0, 512):
                    te.matmul(gP[:, c0:c0 + 512],
                              wgT[:, par, hc, q4 * 128:(q4 + 1) * 128],
                              qxT[:, hc, c0:c0 + 512],
                              start=first, stop=last)
                    te.matmul(uP[:, c0:c0 + 512],
                              wuT[:, par, hc, q4 * 128:(q4 + 1) * 128],
                              qxT[:, hc, c0:c0 + 512],
                              start=first, stop=last)
            pg = sig(te, "P")
            if q4 == 3:
                tr_done[ib] = pg
            tpar = ic % 2
            sc.wait_ge(sP, pg)
            if ic >= 1:
                sc.wait_ge(sV, sil_done[ic - 1])
            sc.activation(gs_t, gP, AF.Silu, scale=wg_rs[:, ic:ic + 1])
            sg1 = sig(sc, "S")
            psum_free[id(gP)] = ("S", sg1)
            ve.wait_ge(sS, sg1)
            if ic >= 2:
                Dt.wait(ve, tpar)
            ve.tensor_mul(tstg[:, tpar, :], gs_t, uP)
            sil_done[ic] = n["V"] + 1
            ve.tensor_scalar_mul(tstg[:, tpar, :], tstg[:, tpar, :],
                                 wu_rs[:, ic:ic + 1])
            ve.tensor_max(mx_t, mx_t, tstg[:, tpar, :])
            ve.tensor_tensor(mn_t, mn_t, tstg[:, tpar, :], op=ALU.min)
            st_done[ic] = sig(ve, "V")
            psum_free[id(uP)] = ("V", st_done[ic])
            sy.wait_ge(sV, st_done[ic])
            Dt.go(tT_d[ic * 128:(ic + 1) * 128, :], tstg[:, tpar, :],
                  sub=tpar)

    if _DEBUG_STOP == "C1":
        Gq.wait_all(sy)
        Dw.wait_all(sy)
        for ic in range(NIC):
            O.wait(sy, 0)
            Dt.wait_all(sy)
            Ddbg.go(tg[:, 0, :], tT_d[ic * 128:(ic + 1) * 128, :])
            Ddbg.wait(sy)
            O.go(dbg["tT"][ic * 128:(ic + 1) * 128, :], tg[:, 0, :])
        return finish()

    ve.tensor_scalar_mul(mn_t, mn_t, -1.0)
    ve.tensor_max(mx_t, mx_t, mn_t)
    v_mx = sig(ve, "V")
    te.wait_ge(sV, v_mx)
    for jt in range(8):
        te.transpose(PDh[:, 0:128], mx_t[:, jt * 128:(jt + 1) * 128], identf)
        p_mx = pe_fence()
        ve.wait_ge(sP, p_mx)
        ve.reduce_max(rmx8[:, jt:jt + 1], PDh[:, 0:128], axis=AX.X)
        vr = sig(ve, "V")
        te.wait_ge(sV, vr)
    ve.reciprocal(stats[:, 20:28], rmx8)
    ve.nop()
    ve.tensor_scalar_mul(stats[:, 20:28], stats[:, 20:28], 448.0)
    ve.tensor_scalar_mul(rmx8, rmx8, 1.0 / 448.0)
    v_st = sig(ve, "V")
    te.wait_ge(sV, v_st)
    for jt in range(8):
        for which, dstP in ((0, PA), (1, PB)):
            src1 = stats[:, 20 + jt:21 + jt] if which == 0                 else rmx8[:, jt:jt + 1]
            te.transpose(PDh[0:1, 0:128], src1, identf)
            p1 = pe_fence()
            sc.wait_ge(sP, p1)
            sc.copy(row16[0:1, which, :], PDh[0:1, 0:128])
            s1 = sig(sc, "S")
            te.wait_ge(sS, s1)
            te.matmul(dstP[:, jt * 128:(jt + 1) * 128], ones_row,
                      row16[0:1, which, :], start=True, stop=True)
    p_bb = sig(te, "P")
    sc.wait_ge(sP, p_bb)
    sc.copy(st_b, PA)
    sc.copy(str_b, PB)
    s_b = sig(sc, "S")
    te.wait_ge(sS, s_b)

    Dt.wait_all(gp)
    Gq.wait_all(gp)
    Dw.wait_all(gp)
    vq_done = [0] * NIC
    for ic in range(NIC):
        par = ic % 2
        if ic >= 2:
            gp.wait_ge(sV, vq_done[ic - 2])
        Gg.gather(tg[:, par, :], tT_d, pi_sb[:, ic:ic + 1], sub=par)
        Gg.wait(ve, par)
        ve.wait_ge(sS, s_b)
        if ic >= 2:
            Dqw.wait(ve, par)
        ve.tensor_mul(tgq[:, par, :], tg[:, par, :], st_b)
        vq_done[ic] = n["V"] + 1
        iv = tgq[:, par, :].bitcast(I32)
        ve.tensor_scalar(iv, iv, RADD, None, op0=ALU.add)
        ve.tensor_scalar(iv, iv, RMASK, None, op0=ALU.bitwise_and)
        ve.tensor_mul(tgq[:, par, :], tgq[:, par, :], str_b)
        v1 = sig(ve, "V")
        sc.wait_ge(sV, v1)
        sc.copy(qt16[:, par, :], tgq[:, par, :])
        s1 = sig(sc, "S")
        sy.wait_ge(sS, s1)
        Dqw.go(qtT_d[ic * 128:(ic + 1) * 128, :], qt16[:, par, :], sub=par)

    if _DEBUG_STOP == "C3":
        for ic in range(NIC):
            O.wait(sc, 0)
            Dqw.wait_all(sy)
            Ddbg.go(qt16[:, 0, :], qtT_d[ic * 128:(ic + 1) * 128, :])
            Ddbg.wait(sc)
            sc.copy(tgq[:, 0, :], qt16[:, 0, :])
            s1 = sig(sc, "S")
            sy.wait_ge(sS, s1)
            O.go(dbg["qtT"][ic * 128:(ic + 1) * 128, :], tgq[:, 0, :])
        return finish()

    for tt in range(16):
        te.transpose(PDh[0:1, 0:128], wd_rs[:, tt:tt + 1], identf)
        p1 = pe_fence()
        sc.wait_ge(sP, p1)
        sc.copy(row16[0:1, 0, :], PDh[0:1, 0:128])
        s1 = sig(sc, "S")
        te.wait_ge(sS, s1)
        te.matmul((PA, PB)[tt // 8][:, (tt % 8) * 128:(tt % 8 + 1) * 128],
                  ones_row, row16[0:1, 0, :], start=True, stop=True)
    p_cb = sig(te, "P")
    sc.wait_ge(sP, p_cb)
    sc.copy(csc_b[:, 0:1024], PA)
    sc.copy(csc_b[:, 1024:2048], PB)
    s_cb = sig(sc, "S")
    te.wait_ge(sS, s_cb)
    Dqw.wait_all(sy)
    sy.wait_ge(sS, s_cb)
    Gq.wait_all(sc)

    for jh in range(2):
        for mcol in range(4):
            dn_done = [0] * NIC
            v_evprev = n["V"]
            sc.wait_ge(sP, n["P"])
            for ic in range(NIC):
                par = ic % 2
                if ic >= 2:
                    sy.wait_ge(sP, dn_done[ic - 2])
                    sc.wait_ge(sP, dn_done[ic - 2])
                Dqs.go(qts[:, par, 0:512],
                       qtT_d[ic * 128:(ic + 1) * 128,
                             jh * 512:(jh + 1) * 512], sub=par)
                Dwd.go(wdTs[:, par, :],
                       qwd_d[mcol * 512:(mcol + 1) * 512,
                             ic * 128:(ic + 1) * 128],
                       sub=par, transpose=True)
                Dqs.wait(te, par)
                Dwd.wait(te, par)
                if ic == 0:
                    te.wait_ge(sV, v_evprev)
                for jj in range(4):
                    te.matmul(jslice4(jj),
                              qts[:, par, jj * 128:(jj + 1) * 128],
                              wdTs[:, par, :],
                              start=(ic == 0), stop=(ic == NIC - 1))
                dn_done[ic] = sig(te, "P")
            p_dn = dn_done[NIC - 1]
            for jj in range(4):
                j = jh * 4 + jj
                par = jj % 2
                Dxs.go(x1st[:, par, :],
                       x1_d[j * 128:(j + 1) * 128,
                            mcol * 512:(mcol + 1) * 512], sub=par)
                Dxs.wait(ve, par)
                ve.wait_ge(sP, p_dn)
                O.wait(ve, par)
                ve.tensor_mul(yev[:, par, :], jslice4(jj),
                              csc_b[:, mcol * 512:(mcol + 1) * 512])
                ve.tensor_add(yev[:, par, :], yev[:, par, :], x1st[:, par, :])
                v1 = sig(ve, "V")
                sy.wait_ge(sV, v1)
                O.go(out_d[j * 128:(j + 1) * 128,
                           mcol * 512:(mcol + 1) * 512],
                     yev[:, par, :], sub=par)
    return finish()


def kernel(L0_attn_x, L0_attn_w_aq, L0_attn_w_ak, L0_attn_w_av, L0_attn_w_ao,
           L0_ffn_w_gate, L0_ffn_w_up, L0_ffn_w_down,
           L0_ffn_up_reorder_index, L0_ffn_down_reorder_index):
    x = np.ascontiguousarray(np.asarray(L0_attn_x, dtype=np.float32))
    waq = np.ascontiguousarray(np.asarray(L0_attn_w_aq, dtype=np.float32))
    wak = np.ascontiguousarray(np.asarray(L0_attn_w_ak, dtype=np.float32))
    wav = np.ascontiguousarray(np.asarray(L0_attn_w_av, dtype=np.float32))
    wao = np.ascontiguousarray(np.asarray(L0_attn_w_ao, dtype=np.float32))
    wg = np.ascontiguousarray(np.asarray(L0_ffn_w_gate, dtype=np.float32))
    wu = np.ascontiguousarray(np.asarray(L0_ffn_w_up, dtype=np.float32))
    wd = np.ascontiguousarray(np.asarray(L0_ffn_w_down, dtype=np.float32))
    pi = np.asarray(L0_ffn_down_reorder_index).astype(np.int32)
    pi_dev = np.ascontiguousarray(pi.reshape(NIC, 128).T)

    nc, _ = build_program()
    in_maps = []
    perms = {}
    for c in range(8):
        b, p = c // 2, c % 2
        tokperm, cosF, sinFs, mkt, swap, identm = _host_tables(p)
        perms[c] = tokperm
        in_maps.append({
            "xb": np.ascontiguousarray(x[b][tokperm]),
            "w_aq": waq, "w_ak": wak, "w_av": wav, "w_ao": wao,
            "wg": wg, "wu": wu, "wd": wd, "pi": pi_dev,
            "ropecos": np.ascontiguousarray(cosF),
            "ropesin": np.ascontiguousarray(sinFs),
            "mk": np.ascontiguousarray(mkt),
            "ident": identm, "identf": np.eye(128, dtype=np.float32),
            "swapm": swap,
        })
    res = run_bass_kernel_spmd(nc, in_maps, core_ids=list(range(8)))
    out = np.empty((B, T, M), np.float32)
    for c in range(8):
        out[c // 2][perms[c][:NTOK]] = res.results[c]["out"]
    return out


# version 05
# speedup vs baseline: 1.1560x; 1.0524x over previous
import numpy as np
import ml_dtypes
import concourse.bass as bass
import concourse.mybir as mybir
from concourse.bass_utils import run_bass_kernel_spmd
from concourse.bass_core_rust import IndirectOffsetOnAxis

F32 = mybir.dt.float32
BF16 = mybir.dt.bfloat16
I32 = mybir.dt.int32
U8 = mybir.dt.uint8
AF = mybir.ActivationFunctionType
ALU = mybir.AluOpType
AX = mybir.AxisListType

B, T, M, NH, DH = 4, 2048, 2048, 16, 128
II = 5632
KT = T // 128
NJ = 8
NTOK = NJ * 128
NIC = II // 128
IBLK = 512
NIB = II // IBLK
RADD = 0x80000
RMASK = -0x100000

_DEBUG_STOP = None
_DEBUG_HEADS = NH
_DEBUG_TILES = KT
_DEBUG_SUB = 3
_DEBUG_BSUB = 9
_STRICT = [False]
_DEPGUARD = [True]


def _host_tables(p):
    own = [2 * j + p for j in range(NJ)]
    other = [t for t in range(KT) if t not in own]
    tokperm = np.concatenate([np.arange(t * 128, (t + 1) * 128)
                              for t in own + other])
    pos = tokperm.astype(np.float64)
    dims = np.arange(DH // 2, dtype=np.float64)
    ang = np.power(10000.0, -dims / (DH // 2))
    rad = pos[None, :] * ang[:, None]
    cos = np.cos(rad)
    sin = np.sin(rad)
    cosF = np.concatenate([cos, cos], 0).astype(ml_dtypes.bfloat16)
    sinFs = np.concatenate([-sin, sin], 0).astype(ml_dtypes.bfloat16)
    tri = (np.arange(128)[:, None] <= np.arange(128)[None, :]).astype(np.float32)
    mk = np.empty((KT, 128, 128), np.float32)
    for m in range(KT):
        mk[m] = tri if m < NJ else (0.0 if p == 0 else 1.0)
    swap = np.zeros((128, 128), np.float32)
    swap[:64, 64:] = np.eye(64)
    swap[64:, :64] = np.eye(64)
    return (tokperm, cosF, sinFs, mk.astype(ml_dtypes.bfloat16),
            swap.astype(ml_dtypes.bfloat16),
            np.eye(128, dtype=ml_dtypes.bfloat16))


def build_program():
    nc = bass.Bass("TRN2", target_bir_lowering=False, debug=False)

    xb = nc.dram_tensor("xb", [T, M], F32, kind="ExternalInput").ap()
    w_aq = nc.dram_tensor("w_aq", [NH, M, DH], F32, kind="ExternalInput").ap()
    w_ak = nc.dram_tensor("w_ak", [NH, M, DH], F32, kind="ExternalInput").ap()
    w_av = nc.dram_tensor("w_av", [NH, M, DH], F32, kind="ExternalInput").ap()
    w_ao = nc.dram_tensor("w_ao", [NH, DH, M], F32, kind="ExternalInput").ap()
    wg_d = nc.dram_tensor("wg", [II, M], F32, kind="ExternalInput").ap()
    wu_d = nc.dram_tensor("wu", [II, M], F32, kind="ExternalInput").ap()
    wd_d = nc.dram_tensor("wd", [M, II], F32, kind="ExternalInput").ap()
    pi_d = nc.dram_tensor("pi", [128, NIC], I32, kind="ExternalInput").ap()
    cos_d = nc.dram_tensor("ropecos", [128, T], BF16, kind="ExternalInput").ap()
    sin_d = nc.dram_tensor("ropesin", [128, T], BF16, kind="ExternalInput").ap()
    mk_d = nc.dram_tensor("mk", [KT, 128, 128], BF16, kind="ExternalInput").ap()
    id_d = nc.dram_tensor("ident", [128, 128], BF16, kind="ExternalInput").ap()
    sw_d = nc.dram_tensor("swapm", [128, 128], BF16, kind="ExternalInput").ap()
    idf_d = nc.dram_tensor("identf", [128, 128], F32, kind="ExternalInput").ap()

    out_d = nc.dram_tensor("out", [NTOK, M], F32, kind="ExternalOutput").ap()

    qwg_d = nc.dram_tensor("qwg", [II, M], BF16, kind="Internal").ap()
    qwu_d = nc.dram_tensor("qwu", [II, M], BF16, kind="Internal").ap()
    qwd_d = nc.dram_tensor("qwd", [M, II], BF16, kind="Internal").ap()
    tT_d = nc.dram_tensor("tT", [II, NTOK], F32, kind="Internal").ap()
    qtT_d = nc.dram_tensor("qtT", [II, NTOK], BF16, kind="Internal").ap()
    x1_d = nc.dram_tensor("x1s", [NTOK, M], F32, kind="Internal").ap()

    dbg = {}
    if _DEBUG_STOP is not None:
        for nm, shp in [("lnxT", [16, 128, T]), ("kT", [128, T]),
                        ("vTok", [KT, 128, 128]), ("qT", [128, NTOK]),
                        ("oT", [128, NH * NTOK]), ("x1", [NTOK, M]),
                        ("qx", [NTOK, M]), ("tT", [II, NTOK]),
                        ("qtT", [II, NTOK]), ("st", [128, 32]),
                        ("lnA", [128, M])]:
            dbg[nm] = nc.dram_tensor("dbg_" + nm, shp, F32,
                                     kind="ExternalOutput").ap()
        dbg["lnraw"] = nc.dram_tensor("dbg_lnraw", [16, 128, T], BF16,
                                      kind="ExternalOutput").ap()

    def sba(name, shape, dt):
        return nc.alloc_sbuf_tensor(name, list(shape), dt).ap()

    ident = sba("identc", [128, 128], BF16)
    identf = sba("identfc", [128, 128], F32)
    swapm = sba("swapc", [128, 128], BF16)
    mk = sba("mkc", [128, KT, 128], BF16)
    cosF = sba("cosc", [128, T], BF16)
    sinFs = sba("sinc", [128, T], BF16)
    pi_sb = sba("pic", [128, NIC], I32)
    ones_col = sba("onescol", [128, 1], BF16)
    ones_row = sba("onesrow", [1, 128], BF16)
    wg_sc = sba("wgsc", [128, NIC], F32)
    wu_sc = sba("wusc", [128, NIC], F32)
    wd_sc = sba("wdsc", [128, 16], F32)
    wg_rs = sba("wgrs", [128, NIC], F32)
    wu_rs = sba("wurs", [128, NIC], F32)
    wd_rs = sba("wdrs", [128, 16], F32)
    stats = sba("stats", [128, 32], F32)
    rmx8 = sba("rmx8", [128, 8], F32)
    row16 = sba("row16", [1, 2, 128], BF16)
    negc = sba("negc", [128, 1], F32)
    cm16 = sba("cm16", [128, 2], BF16)
    kT = sba("kT", [128, T], BF16)
    kvraw = sba("kvraw", [128, T], BF16)
    kTr, vT = kvraw, kvraw
    qTr = kvraw[:, 0:NTOK]
    vTok2 = sba("vTok2", [128, KT, 256], BF16)
    qT = sba("qT", [128, NTOK], BF16)
    pt = sba("pt", [128, 2, NTOK], BF16)

    def region(name, nbytes):
        return nc.alloc_sbuf_tensor(name, [128, nbytes], U8).ap()

    def view(reg, o, shape, dt):
        esz = {F32: 4, BF16: 2, I32: 4}[dt]
        r = reg.bitcast(dt)
        flat = 1
        for s in shape[1:]:
            flat *= s
        v = r[:, o // esz:o // esz + flat]
        if len(shape) == 2:
            return v
        pat = "p (" + " ".join(f"d{i}" for i in range(1, len(shape))) + ") ->" \
              + " p " + " ".join(f"d{i}" for i in range(1, len(shape)))
        kw = {f"d{i}": shape[i] for i in range(1, len(shape) - 1)}
        return v.rearrange(pat, **kw)

    RG1 = region("RG1", 65536)
    lnxT = view(RG1, 0, [128, 16, T], BF16)
    x1_sb = view(RG1, 0, [128, NJ, M], F32)
    wgT = view(RG1, 0, [128, 2, 16, IBLK], BF16)
    wuT = view(RG1, 32768, [128, 2, 16, IBLK], BF16)
    RG2 = region("RG2", 32768)
    oT = view(RG2, 0, [128, NH, NTOK], BF16)
    qxT = view(RG2, 0, [128, 16, NTOK], BF16)
    RG3 = region("RG3", 36864)
    xst = view(RG3, 0, [128, 2, M], F32)
    lnA = view(RG3, 16384, [128, M], F32)
    wq16 = view(RG3, 0, [128, 2, 16, DH], BF16)
    wk16 = view(RG3, 8192, [128, 2, 16, DH], BF16)
    wv2 = view(RG3, 16384, [128, 16, 2, 128], BF16)
    wv2f = view(RG3, 16384, [128, 16, 256], BF16)
    linv = view(RG3, 24576, [128, NTOK], F32)
    rowB = view(RG3, 28672, [1, 1024], F32)
    rowA16 = view(RG3, 32768, [1, 1024], BF16)
    rowA = view(RG3, 34816, [1, 512], F32)
    wao16 = view(RG3, 0, [128, 2, 512], BF16)
    xres = view(RG3, 2048, [128, 2, 512], F32)
    sqx = view(RG3, 0, [128, M], F32)
    lnf = view(RG3, 8192, [128, M], F32)
    tstg = view(RG3, 16384, [128, 2, NTOK], F32)
    gs_t = view(RG3, 24576, [128, NTOK], F32)
    mx_t = view(RG3, 28672, [128, NTOK], F32)
    mn_t = view(RG3, 32768, [128, NTOK], F32)
    lnsq = view(RG3, 24576, [128, M], F32)
    st_b = view(RG3, 28672, [128, NTOK], F32)
    str_b = view(RG3, 32768, [128, NTOK], F32)
    csc_b = view(RG3, 16384, [128, M], F32)
    RG4 = region("RG4", 28672)
    wst = view(RG4, 0, [128, 2, 2816], F32)
    tg = view(RG4, 0, [128, 2, NTOK], F32)
    tgq = view(RG4, 8192, [128, 2, NTOK], F32)
    qt16 = view(RG4, 16384, [128, 2, NTOK], BF16)
    wdTs = view(RG4, 20480, [128, 2, 512], BF16)
    x1st = view(RG4, 22528, [128, 2, 512], F32)
    qts = view(RG4, 24576, [128, 2, NTOK], BF16)
    yev = view(RG4, 0, [128, 2, 512], F32)

    PA = nc.alloc_psum_tensor("PA", [128, 1024], F32).ap()
    PB = nc.alloc_psum_tensor("PB", [128, 1024], F32).ap()
    PC = nc.alloc_psum_tensor("PC", [128, 1024], F32).ap()
    PDh = nc.alloc_psum_tensor("PDh", [128, 512], F32).ap()
    PF = nc.alloc_psum_tensor("PF", [1, 128], F32).ap()

    _GUARD = {"tensor_copy", "tensor_tensor", "tensor_add", "tensor_sub",
              "tensor_mul", "tensor_max", "tensor_scalar",
              "tensor_single_scalar", "tensor_scalar_add",
              "tensor_scalar_mul", "tensor_scalar_max", "tensor_scalar_min",
              "tensor_reduce", "reduce_max", "reduce_sum", "reciprocal",
              "memset", "activation", "square", "sqrt", "copy", "mul",
              "add", "select"}

    _ESZ = {F32: 4, BF16: 2, I32: 4, U8: 1}

    def _rng(ap):
        try:
            esz = _ESZ.get(ap.dtype, 4)
            span = 1
            for i, (st_, ct) in enumerate(ap.ap):
                if i == 0:
                    continue
                span += (ct - 1) * abs(st_)
            lo = ap.offset * esz
            return (ap.tensor.name, lo, lo + span * esz)
        except Exception:
            try:
                return (ap.tensor.name, 0, 1 << 60)
            except Exception:
                return None

    class _EG:
        def __init__(self, e):
            self._e = e
            self.last = None
            self._hist = []

        def __getattr__(self, name):
            f = getattr(self._e, name)
            if name in _GUARD:
                def g(*a, **k):
                    aps = [x for x in list(a) + list(k.values())
                           if hasattr(x, "tensor") and hasattr(x, "offset")]
                    outr = _rng(aps[0]) if aps else None
                    fs = aps[0].free_size() if aps else 1024
                    need_drain = need_nop = False
                    for idx, (pw, pfs) in enumerate(self._hist):
                        for x in aps[1:]:
                            rr = _rng(x)
                            if rr is None or (pw is not None
                                              and rr[0] == pw[0]
                                              and rr[1] < pw[2]
                                              and pw[1] < rr[2]):
                                if fs < 512 or pfs < 512:
                                    need_drain = True
                                elif idx == 0:
                                    need_nop = True
                    if not _DEPGUARD[0]:
                        need_drain = fs < 512
                        need_nop = True
                    if _STRICT[0] or need_drain:
                        self._e.drain()
                        self._hist.clear()
                    elif need_nop:
                        self._e.nop()
                    r = f(*a, **k)
                    self.last = r
                    self._hist.insert(0, (outr, fs))
                    del self._hist[3:]
                    return r
                return g
            return f

    class _TG:
        def __init__(self, e):
            self._e = e
            self.last = None

        def __getattr__(self, name):
            f = getattr(self._e, name)
            if name in ("transpose", "matmul"):
                def g(*a, **k):
                    r = f(*a, **k)
                    self.last = r
                    return r
                return g
            return f

    sy = nc.sync
    gp = _EG(nc.gpsimd)
    ve = _EG(nc.vector)
    sc = _EG(nc.scalar)
    te = _TG(nc.tensor)
    sems = {k: nc.alloc_semaphore(k) for k in ("P", "V", "S")}
    n = {k: 0 for k in sems}
    sP, sV, sS = (sems[k] for k in ("P", "V", "S"))

    class DStream:
        def __init__(self, name, nsub=1, eng=None):
            self.sems = [nc.alloc_semaphore(name + str(i))
                         for i in range(nsub)]
            self.n = [0] * nsub
            self.eng = eng if eng is not None else sy

        def go(self, out, in_, sub=0, transpose=False):
            if transpose:
                r = self.eng.dma_start_transpose(out, in_)
            else:
                r = self.eng.dma_start(out=out, in_=in_)
            r.then_inc(self.sems[sub], 16)
            self.n[sub] += 16

        def gather(self, out, in_, idx, sub=0):
            self.eng.indirect_dma_start(
                out, None, in_, IndirectOffsetOnAxis(idx, 0)) \
                .then_inc(self.sems[sub], 16)
            self.n[sub] += 16

        def wait(self, eng, sub=0):
            eng.wait_ge(self.sems[sub], self.n[sub])

        def wait_all(self, eng):
            for i, s in enumerate(self.sems):
                eng.wait_ge(s, self.n[i])

    Dc = DStream("Dc")
    Dx = DStream("Dx", 2)
    Dw = DStream("Dw", 2)
    Gq = DStream("Gq", 2, gp)
    Gw = DStream("Gw", 2, gp)
    Gv = DStream("Gv", 1, gp)
    Ga = DStream("Ga", 2, gp)
    Dxr = DStream("Dxr", 2)
    Dx1 = DStream("Dx1")
    Dwg = DStream("Dwg", 2)
    Dwu = DStream("Dwu", 2)
    Dt = DStream("Dt", 2)
    Gg = DStream("Gg", 2, gp)
    Dqw = DStream("Dqw", 2)
    Dqs = DStream("Dqs", 2)
    Dwd = DStream("Dwd", 2)
    Dxs = DStream("Dxs", 2)
    O = DStream("O", 2)
    Ddbg = DStream("Ddbg")
    streams = [Dc, Dx, Dw, Gq, Gw, Gv, Ga, Dxr, Dx1, Dwg, Dwu, Dt, Gg,
               Dqw, Dqs, Dwd, Dxs, O, Ddbg]

    def sig(eng, key):
        eng.last.then_inc(sems[key], 1)
        n[key] += 1
        return n[key]

    def finish():
        for e in (sy, gp, ve, sc, te):
            O.wait_all(e)
        return nc, dbg

    for st_ in streams:
        for s in st_.sems:
            sy.sem_clear(s)
    for s in sems.values():
        sy.sem_clear(s)
    nc.all_engine_barrier()
    Dc.go(ident, id_d)
    Dc.go(swapm, sw_d)
    Dc.go(identf, idf_d)
    Dc.go(mk, mk_d.rearrange("k p q -> p k q"))
    Dc.go(cosF, cos_d)
    Dc.go(sinFs, sin_d)
    Dc.go(pi_sb, pi_d)
    Dc.wait(ve)
    if _DEBUG_STOP is not None:
        for rg_ in (RG1, RG2, RG3, RG4):
            ve.memset(rg_.bitcast(F32), 0.0)
        ve.memset(stats, 0.0)
        ve.memset(rmx8, 0.0)
    ve.memset(ones_col, 1.0)
    ve.memset(ones_row, 1.0)
    v0 = sig(ve, "V")
    for e in (te, sc, gp, sy):
        e.wait_ge(sV, v0)

    def pe_fence():
        return sig(te, "P")

    def tr_f32(src_f32, ncols, write_fn):
        nt = ncols // 128
        done = 0
        while done < nt:
            batch = min(16, nt - done)
            for k in range(batch):
                c = done + k
                P, o = (PA, PB)[k // 8], (k % 8) * 128
                te.transpose(P[:, o:o + 128],
                             src_f32[:, c * 128:(c + 1) * 128], identf)
            p1 = pe_fence()
            ve.wait_ge(sP, p1)
            for k in range(batch):
                c = done + k
                P, o = (PA, PB)[k // 8], (k % 8) * 128
                write_fn(c, P[:, o:o + 128])
            v1 = sig(ve, "V")
            te.wait_ge(sV, v1)
            done += batch

    lnsq_free = [0]
    sqrt_sig = [0]

    def ln_stats(src_ap):
        ve.wait_ge(sS, sqrt_sig[0])
        ve.reduce_sum(stats[:, 0:1], src_ap, axis=AX.X)
        sc.wait_ge(sV, lnsq_free[0])
        sc.square(lnsq, src_ap)
        s1 = sig(sc, "S")
        ve.wait_ge(sS, s1)
        ve.reduce_sum(stats[:, 1:2], lnsq, axis=AX.X)
        lnsq_free[0] = n["V"] + 1
        ve.tensor_scalar_mul(stats[:, 2:3], stats[:, 0:1], 1.0 / M)
        ve.tensor_scalar_mul(stats[:, 3:4], stats[:, 1:2], 1.0 / M)
        ve.tensor_mul(stats[:, 4:5], stats[:, 2:3], stats[:, 2:3])
        ve.tensor_sub(stats[:, 5:6], stats[:, 3:4], stats[:, 4:5])
        ve.tensor_scalar_add(stats[:, 5:6], stats[:, 5:6], 1e-5)
        v1 = sig(ve, "V")
        sc.wait_ge(sV, v1)
        sc.sqrt(stats[:, 6:7], stats[:, 5:6])
        s2 = sig(sc, "S")
        sqrt_sig[0] = s2
        ve.wait_ge(sS, s2)
        ve.reciprocal(stats[:, 7:8], stats[:, 6:7])
        v2 = sig(ve, "V")
        sc.wait_ge(sV, v2)


    s_lnuse = [0] * KT
    for i in range(_DEBUG_TILES):
        par = i % 2
        if i >= 2:
            sy.wait_ge(sV, s_lnuse[i - 2])
        Dx.go(xst[:, par, :], xb[i * 128:(i + 1) * 128, :], sub=par)
        Dx.wait(ve, par)
        Dx.wait(sc, par)
        if _DEBUG_SUB >= 1:
            ln_stats(xst[:, par, :])
        if _DEBUG_SUB >= 2:
            if i >= 1:
                ve.wait_ge(sP, p_lnA[i - 1])
            ve.tensor_scalar(lnA, xst[:, par, :], stats[:, 2:3],
                             stats[:, 7:8], op0=ALU.subtract, op1=ALU.mult)
            s_lnuse[i] = sig(ve, "V")
        if _DEBUG_SUB >= 3:
            te.wait_ge(sV, s_lnuse[i])
            tr_f32(lnA, M,
                   lambda c, src, i=i: ve.tensor_copy(
                       lnxT[:, c, i * 128:(i + 1) * 128], src))
        if i == 0:
            p_lnA = [0] * KT
        p_lnA[i] = n["P"]
    v_endA = n["V"]

    if _DEBUG_STOP == "A":
        sy.wait_ge(sV, v_endA)
        O.go(dbg["st"], stats)
        O.go(dbg["lnA"], lnA)
        for c in range(16):
            O.go(dbg["lnraw"][c], lnxT[:, c, :])
        for c in range(16):
            sc.wait_ge(sV, v_endA)
            O.wait(sc, 0)
            sc.copy(lnf, lnxT[:, c, :])
            s1 = sig(sc, "S")
            sy.wait_ge(sS, s1)
            O.go(dbg["lnxT"][c], lnf)
        return finish()

    w_units = []
    for t in range(II // 128):
        w_units.append(("g", t))
        w_units.append(("u", t))
    for t in range(16):
        w_units.append(("dmax", t))
        for q in range(4):
            w_units.append(("d", t, q))
    wpar = [0]
    w_free = {0: None, 1: None}
    w_dmax_v = {}

    def w_wait_free(par):
        g = w_free[par]
        if g is None:
            return
        if g[0] == "V":
            sy.wait_ge(sV, g[1])
        else:
            Gq.wait(sy, par)

    def w_round_store(par, width, scale_col, dst):
        sc.activation(wst[:, par, 0:width], wst[:, par, 0:width], AF.Copy,
                      scale=scale_col)
        s1 = sig(sc, "S")
        ve.wait_ge(sS, s1)
        iv = wst[:, par, 0:width].bitcast(I32)
        ve.tensor_scalar(iv, iv, RADD, None, op0=ALU.add)
        ve.tensor_scalar(iv, iv, RMASK, None, op0=ALU.bitwise_and)
        v2 = sig(ve, "V")
        gp.wait_ge(sV, v2)
        Gq.go(dst, wst[:, par, 0:width], sub=par)
        w_free[par] = ("Gq", par)

    def emit_w_unit(u):
        if u[0] in ("g", "u"):
            _, t = u
            src, dst = (wg_d, qwg_d) if u[0] == "g" else (wu_d, qwu_d)
            scl = wg_sc if u[0] == "g" else wu_sc
            rsl = wg_rs if u[0] == "g" else wu_rs
            par = wpar[0] % 2
            wpar[0] += 1
            w_wait_free(par)
            Dw.go(wst[:, par, 0:2048], src[t * 128:(t + 1) * 128, :], sub=par)
            Dw.wait(ve, par)
            ve.reduce_max(stats[:, 10:11], wst[:, par, 0:2048], axis=AX.X,
                          apply_absolute_value=True)
            ve.reciprocal(stats[:, 11:12], stats[:, 10:11])
            ve.tensor_scalar_mul(scl[:, t:t + 1], stats[:, 11:12], 448.0)
            ve.tensor_scalar_mul(rsl[:, t:t + 1], stats[:, 10:11], 1 / 448.0)
            v1 = sig(ve, "V")
            sc.wait_ge(sV, v1)
            w_round_store(par, 2048, scl[:, t:t + 1],
                          dst[t * 128:(t + 1) * 128, :])
        elif u[0] == "dmax":
            _, t = u
            for qq in range(4):
                par = wpar[0] % 2
                wpar[0] += 1
                w_wait_free(par)
                Dw.go(wst[:, par, 0:1408],
                      wd_d[t * 128:(t + 1) * 128, qq * 1408:(qq + 1) * 1408],
                      sub=par)
                Dw.wait(ve, par)
                ve.reduce_max(stats[:, 12 + qq:13 + qq], wst[:, par, 0:1408],
                              axis=AX.X, apply_absolute_value=True)
                w_free[par] = ("V", sig(ve, "V"))
            ve.tensor_max(stats[:, 12:13], stats[:, 12:13], stats[:, 13:14])
            ve.tensor_max(stats[:, 14:15], stats[:, 14:15], stats[:, 15:16])
            ve.tensor_max(stats[:, 12:13], stats[:, 12:13], stats[:, 14:15])
            ve.reciprocal(stats[:, 13:14], stats[:, 12:13])
            ve.tensor_scalar_mul(wd_sc[:, t:t + 1], stats[:, 13:14], 448.0)
            ve.tensor_scalar_mul(wd_rs[:, t:t + 1], stats[:, 12:13], 1 / 448.0)
            w_dmax_v[t] = sig(ve, "V")
        else:
            _, t, q = u
            par = wpar[0] % 2
            wpar[0] += 1
            w_wait_free(par)
            Dw.go(wst[:, par, 0:1408],
                  wd_d[t * 128:(t + 1) * 128, q * 1408:(q + 1) * 1408],
                  sub=par)
            Dw.wait(sc, par)
            sc.wait_ge(sV, w_dmax_v[t])
            w_round_store(par, 1408, wd_sc[:, t:t + 1],
                          qwd_d[t * 128:(t + 1) * 128,
                                q * 1408:(q + 1) * 1408])

    p_qproj_prev = [0]

    def fetch_head_w(h):
        hp = h % 2
        if h >= 2:
            gp.wait_ge(sP, p_qproj_prev[0])
        Gw.go(wq16[:, hp, :, :],
              w_aq[h].rearrange("(c p) d -> p c d", p=128), sub=hp)
        Gw.go(wk16[:, hp, :, :],
              w_ak[h].rearrange("(c p) d -> p c d", p=128), sub=hp)

    def fetch_wv2(h):
        Gv.go(wv2[:, :, 0, :], w_av[h].rearrange("(c p) d -> p c d", p=128))
        Gv.go(wv2[:, :, 1, :],
              w_av[h + 1].rearrange("(c p) d -> p c d", p=128))

    gp.wait_ge(sV, v_endA)
    fetch_head_w(0)
    WPH = (len(w_units) + NH - 1) // NH

    def proj(dst16, wsb, hp, psum, ofs):
        for mc in range(16):
            first, last = mc == 0, mc == 15
            for c0 in (0, 512):
                te.matmul(psum[:, c0:c0 + 512], wsb[:, hp, mc, :],
                          lnxT[:, mc, ofs + c0:ofs + c0 + 512],
                          start=first, stop=last)
        p1 = sig(te, "P")
        sc.wait_ge(sP, p1)
        sc.copy(dst16[:, ofs:ofs + 1024], psum[:, 0:1024])
        s1 = sig(sc, "S")
        te.wait_ge(sS, s1)
        return s1

    s_v2 = 0
    p_v2mm = 0
    pv_done_prev = 0
    for h in range(_DEBUG_HEADS):
        hp = h % 2
        if h + 1 < NH:
            fetch_head_w(h + 1)
        Gw.wait(te, hp)
        if hp == 0 and _DEBUG_BSUB >= 1:
            if h >= 2:
                gp.wait_ge(sP, p_v2mm)
            fetch_wv2(h)
            Gv.wait(te)
            te.wait_ge(sS, s_v2)
            if h >= 2:
                te.wait_ge(sP, pv_done_prev)
            sv2_evac = [0] * KT
            for tt in range(KT):
                VP = PDh if tt % 2 == 0 else PC
                if tt >= 2:
                    te.wait_ge(sS, sv2_evac[tt - 2])
                for mc in range(16):
                    te.matmul(VP[:, 0:256],
                              lnxT[:, mc, tt * 128:(tt + 1) * 128],
                              wv2f[:, mc, :],
                              start=(mc == 0), stop=(mc == 15))
                p1 = sig(te, "P")
                sc.wait_ge(sP, p1)
                sc.copy(vTok2[:, tt, :], VP[:, 0:256])
                sv2_evac[tt] = sig(sc, "S")
            s_v2 = sv2_evac[KT - 1]
            p_v2mm = n["P"]
            te.wait_ge(sS, s_v2)
        if _DEBUG_BSUB < 2:
            continue
        proj(kTr, wk16, hp, PA, 0)
        sk1 = proj(kTr, wk16, hp, PB, 1024)
        for c0 in range(0, T, 512):
            te.matmul((PA if c0 < 1024 else PB)[:, c0 % 1024:c0 % 1024 + 512],
                      swapm, kTr[:, c0:c0 + 512], start=True, stop=True)
        pswap = sig(te, "P")
        ve.wait_ge(sS, sk1)
        ve.tensor_mul(kT, kTr, cosF)
        ve.wait_ge(sP, pswap)
        ve.tensor_mul(kTr[:, 0:1024], PA, sinFs[:, 0:1024])
        ve.tensor_mul(kTr[:, 1024:2048], PB, sinFs[:, 1024:2048])
        ve.tensor_add(kT, kT, kTr)
        ve.reduce_max(stats[:, 16:17], kT, axis=AX.X,
                      apply_absolute_value=True)
        vk = sig(ve, "V")
        te.wait_ge(sV, vk)
        if _DEBUG_BSUB < 3:
            continue
        sc.wait_ge(sV, vk)
        sq0 = proj(qTr, wq16, hp, PA, 0)
        p_qproj_prev[0] = n["P"]
        for c0 in (0, 512):
            te.matmul(PA[:, c0:c0 + 512], swapm, qTr[:, c0:c0 + 512],
                      start=True, stop=True)
        pswq = sig(te, "P")
        ve.wait_ge(sS, sq0)
        ve.tensor_mul(qT, qTr, cosF[:, 0:NTOK])
        ve.wait_ge(sP, pswq)
        ve.tensor_mul(qTr, PA, sinFs[:, 0:NTOK])
        ve.tensor_add(qT, qT, qTr)
        ve.reduce_max(stats[:, 17:18], qT, axis=AX.X,
                      apply_absolute_value=True)
        vq = sig(ve, "V")
        te.wait_ge(sV, vq)
        sc.wait_ge(sV, vq)
        sc.copy(cm16[:, 0:1], stats[:, 16:17])
        sc.copy(cm16[:, 1:2], stats[:, 17:18])
        scm = sig(sc, "S")
        te.wait_ge(sS, scm)
        te.matmul(PA[0:1, 0:1], cm16[:, 0:1], cm16[:, 1:2],
                  start=True, stop=True)
        pc1 = sig(te, "P")
        sc.wait_ge(sP, pc1)
        sc.copy(rowA16[0:1, 0:1], PA[0:1, 0:1])
        sc1 = sig(sc, "S")
        te.wait_ge(sS, sc1)
        te.matmul(PA[:, 0:1], ones_row, rowA16[0:1, 0:1],
                  start=True, stop=True)
        pc2 = sig(te, "P")
        sc.wait_ge(sP, pc2)
        sc.activation(negc, PA[:, 0:1], AF.Copy, scale=-1.0 / 128.0)
        snegc = sig(sc, "S")
        te.wait_ge(sS, snegc)

        if _DEBUG_BSUB < 4:
            continue
        exp_done = [0] * KT
        pv_done = [0] * KT
        vm1 = [0] * KT

        def q0_of(m):
            return (m if m < NJ else m - NJ) * 128

        def emit_pv(m):
            q0 = q0_of(m)
            te.wait_ge(sV, vm1[m])
            c0 = q0
            while c0 < NTOK:
                ce = min((c0 // 512 + 1) * 512, NTOK)
                stop = (m == 11) if c0 < 512 else (m == 15)
                te.matmul(PA[0:1, c0:ce], ones_col, pt[:, m % 2, c0:ce],
                          start=(m == 0), stop=stop)
                te.matmul(PB[:, c0:ce],
                          vTok2[:, m, hp * 128:(hp + 1) * 128],
                          pt[:, m % 2, c0:ce],
                          start=(m == 0), stop=stop)
                c0 = ce
            pv_done[m] = sig(te, "P")

        for m in range(KT):
            q0 = q0_of(m)
            SP = PC
            if m >= 1:
                te.wait_ge(sS, exp_done[m - 1])
            c0 = q0
            while c0 < NTOK:
                ce = min((c0 // 512 + 1) * 512, NTOK)
                te.matmul(SP[:, c0:ce], kT[:, m * 128:(m + 1) * 128],
                          qT[:, c0:ce], start=True, stop=True)
                c0 = ce
            ps = sig(te, "P")
            sc.wait_ge(sP, ps)
            if m >= 2:
                sc.wait_ge(sP, pv_done[m - 2])
            sc.activation(pt[:, m % 2, q0:NTOK], SP[:, q0:NTOK], AF.Exp,
                          bias=negc, scale=1.0 / 128.0)
            exp_done[m] = sig(sc, "S")
            ve.wait_ge(sS, exp_done[m])
            ve.tensor_mul(pt[:, m % 2, q0:q0 + 128],
                          pt[:, m % 2, q0:q0 + 128], mk[:, m, :])
            vm1[m] = sig(ve, "V")
            if m >= 1:
                emit_pv(m - 1)
        emit_pv(KT - 1)
        pv_done_prev = pv_done[KT - 1]
        if _DEBUG_BSUB < 5:
            continue
        sc.wait_ge(sP, pv_done[KT - 1])
        sc.copy(rowA[0:1, 0:512], PA[0:1, 0:512])
        sc.copy(rowB[0:1, 0:512], PA[0:1, 512:1024])
        sl1 = sig(sc, "S")
        ve.wait_ge(sS, sl1)
        ve.reciprocal(rowB[0:1, 512:1024], rowB[0:1, 0:512])
        ve.reciprocal(rowA[0:1, 0:512], rowA[0:1, 0:512])
        ve.tensor_copy(rowA16[0:1, 0:512], rowA[0:1, 0:512])
        ve.tensor_copy(rowA16[0:1, 512:1024], rowB[0:1, 512:1024])
        vl1 = sig(ve, "V")
        te.wait_ge(sV, vl1)
        te.matmul(PC[:, 0:512], ones_row, rowA16[0:1, 0:512],
                  start=True, stop=True)
        te.matmul(PC[:, 512:1024], ones_row, rowA16[0:1, 512:1024],
                  start=True, stop=True)
        pl1 = sig(te, "P")
        sc.wait_ge(sP, pl1)
        sc.copy(linv, PC)
        sl2 = sig(sc, "S")
        ve.wait_ge(sS, sl2)
        ve.tensor_mul(oT[:, h, :], PB, linv)
        voT = sig(ve, "V")
        te.wait_ge(sV, voT)
        pv_done_prev = pv_done[KT - 1]
        for u in w_units[h * WPH:(h + 1) * WPH]:
            emit_w_unit(u)

    if _DEBUG_STOP == "B":
        v = n["V"]
        sc.wait_ge(sV, v)
        Gw.wait_all(sc)
        Gv.wait_all(sc)
        sc.copy(sqx[:, 0:T], kT)
        s1 = sig(sc, "S")
        sy.wait_ge(sS, s1)
        O.go(dbg["kT"], sqx[:, 0:T])
        for c in range(KT):
            O.wait(sc, 0)
            sc.copy(lnf[:, 0:128], vTok2[:, c, 0:128] if _DEBUG_HEADS % 2
                    else vTok2[:, c, 128:256])
            s1 = sig(sc, "S")
            sy.wait_ge(sS, s1)
            O.go(dbg["vTok"][c], lnf[:, 0:128])
        O.wait(sc, 0)
        sc.copy(sqx[:, 0:NTOK], qT)
        s1 = sig(sc, "S")
        sy.wait_ge(sS, s1)
        O.go(dbg["qT"], sqx[:, 0:NTOK])
        for hh in range(_DEBUG_HEADS):
            O.wait(sc, 0)
            sc.copy(lnf[:, 0:NTOK], oT[:, hh, :])
            s1 = sig(sc, "S")
            sy.wait_ge(sS, s1)
            O.go(dbg["oT"][:, hh * NTOK:(hh + 1) * NTOK], lnf[:, 0:NTOK])
        return finish()

    def jslice4(jj):
        return (PA, PB)[jj // 2][:, (jj % 2) * 512:(jj % 2) * 512 + 512]

    p_b2pre = n["P"]
    p_cch = {}
    v_ev = {0: 0, 1: 0}
    k = 0
    for jh in range(2):
        for mcol in range(4):
            for cch in range(16):
                par = cch % 2
                if k >= 2:
                    gp.wait_ge(sP, p_cch[k - 2])
                Ga.go(wao16[:, par, :],
                      w_ao[cch, :, mcol * 512:(mcol + 1) * 512], sub=par)
                Ga.wait(te, par)
                if k == 0:
                    te.wait_ge(sP, p_b2pre)
                for jj in range(4):
                    j = jh * 4 + jj
                    te.matmul(jslice4(jj), oT[:, cch, j * 128:(j + 1) * 128],
                              wao16[:, par, :],
                              start=(cch == 0), stop=(cch == 15))
                p_cch[k] = sig(te, "P")
                k += 1
            pj = p_cch[k - 1]
            for jj in range(4):
                j = jh * 4 + jj
                par = jj % 2
                sy.wait_ge(sV, v_ev[par])
                Dxr.go(xres[:, par, :],
                       xb[j * 128:(j + 1) * 128,
                          mcol * 512:(mcol + 1) * 512], sub=par)
                Dxr.wait(ve, par)
                ve.wait_ge(sP, pj)
                ve.tensor_add(x1_sb[:, j, mcol * 512:(mcol + 1) * 512],
                              jslice4(jj), xres[:, par, :])
                v_ev[par] = sig(ve, "V")
            te.wait_ge(sV, v_ev[par])
    v_x1 = n["V"]

    if _DEBUG_STOP == "B2":
        sy.wait_ge(sV, v_x1)
        for j in range(NJ):
            O.go(dbg["x1"][j * 128:(j + 1) * 128, :], x1_sb[:, j, :])
        return finish()

    s_qx = [0] * NJ
    for j in range(NJ):
        ve.wait_ge(sV, v_x1)
        sc.wait_ge(sV, v_x1)
        ln_stats(x1_sb[:, j, :])
        ve.tensor_scalar(lnf, x1_sb[:, j, :], stats[:, 2:3],
                         stats[:, 7:8], op0=ALU.subtract, op1=ALU.mult)
        ve.reduce_max(stats[:, 10:11], lnf, axis=AX.X,
                      apply_absolute_value=True)
        ve.reciprocal(stats[:, 11:12], stats[:, 10:11])
        ve.tensor_scalar_mul(stats[:, 12:13], stats[:, 11:12], 448.0)
        ve.tensor_scalar_mul(stats[:, 13:14], stats[:, 10:11], 1.0 / 448.0)
        v3 = sig(ve, "V")
        sc.wait_ge(sV, v3)
        if _DEBUG_STOP == "C0":
            O.wait(sc, 0)
        sc.activation(sqx, lnf, AF.Copy, scale=stats[:, 12:13])
        s4 = sig(sc, "S")
        ve.wait_ge(sS, s4)
        iv = sqx.bitcast(I32)
        ve.tensor_scalar(iv, iv, RADD, None, op0=ALU.add)
        ve.tensor_scalar(iv, iv, RMASK, None, op0=ALU.bitwise_and)
        ve.tensor_scalar_mul(sqx, sqx, stats[:, 13:14])
        v4 = sig(ve, "V")
        if _DEBUG_STOP == "C0":
            ve.wait_ge(sV, v4)
            sy.wait_ge(sV, v4)
            O.go(dbg["qx"][j * 128:(j + 1) * 128, :], sqx)
        te.wait_ge(sV, v4)
        tr_f32(sqx, M,
               lambda c, src, j=j: ve.tensor_copy(
                   qxT[:, c, j * 128:(j + 1) * 128], src))
        s_qx[j] = n["V"]
        sy.wait_ge(sV, s_qx[j])
        Dx1.go(x1_d[j * 128:(j + 1) * 128, :], x1_sb[:, j, :])

    if _DEBUG_STOP == "C0":
        return finish()

    tr_done = [0] * NIB
    st_done = [0] * NIC
    sil_done = [0] * NIC
    psum_free = {}
    ve.memset(mx_t, -3e38)
    ve.memset(mn_t, 3e38)
    Dx1.wait(te)
    Dx1.wait(sy)
    Dx1.wait(sc)
    Gq.wait_all(sy)
    Gq.wait_all(sc)
    for ib in range(NIB):
        par = ib % 2
        if ib >= 2:
            sy.wait_ge(sP, tr_done[ib - 2])
            sc.wait_ge(sP, tr_done[ib - 2])
        for hc in range(16):
            Dwg.go(wgT[:, par, hc, :],
                   qwg_d[ib * IBLK:(ib + 1) * IBLK, hc * 128:(hc + 1) * 128],
                   sub=par, transpose=True)
            Dwu.go(wuT[:, par, hc, :],
                   qwu_d[ib * IBLK:(ib + 1) * IBLK, hc * 128:(hc + 1) * 128],
                   sub=par, transpose=True)
        Dwg.wait(te, par)
        Dwu.wait(te, par)
        for q4 in range(4):
            ic = ib * 4 + q4
            bufs = (PA, PB, PC)
            gP, uP = bufs[(2 * ic) % 3], bufs[(2 * ic + 1) % 3]
            for X in (gP, uP):
                fr = psum_free.get(id(X))
                if fr is not None:
                    te.wait_ge(sems[fr[0]], fr[1])
            for hc in range(16):
                first, last = hc == 0, hc == 15
                for c0 in (0, 512):
                    te.matmul(gP[:, c0:c0 + 512],
                              wgT[:, par, hc, q4 * 128:(q4 + 1) * 128],
                              qxT[:, hc, c0:c0 + 512],
                              start=first, stop=last)
                    te.matmul(uP[:, c0:c0 + 512],
                              wuT[:, par, hc, q4 * 128:(q4 + 1) * 128],
                              qxT[:, hc, c0:c0 + 512],
                              start=first, stop=last)
            pg = sig(te, "P")
            if q4 == 3:
                tr_done[ib] = pg
            tpar = ic % 2
            sc.wait_ge(sP, pg)
            if ic >= 1:
                sc.wait_ge(sV, sil_done[ic - 1])
            sc.activation(gs_t, gP, AF.Silu, scale=wg_rs[:, ic:ic + 1])
            sg1 = sig(sc, "S")
            psum_free[id(gP)] = ("S", sg1)
            ve.wait_ge(sS, sg1)
            if ic >= 2:
                Dt.wait(ve, tpar)
            ve.tensor_mul(tstg[:, tpar, :], gs_t, uP)
            sil_done[ic] = n["V"] + 1
            ve.tensor_scalar_mul(tstg[:, tpar, :], tstg[:, tpar, :],
                                 wu_rs[:, ic:ic + 1])
            ve.tensor_max(mx_t, mx_t, tstg[:, tpar, :])
            ve.tensor_tensor(mn_t, mn_t, tstg[:, tpar, :], op=ALU.min)
            st_done[ic] = sig(ve, "V")
            psum_free[id(uP)] = ("V", st_done[ic])
            sy.wait_ge(sV, st_done[ic])
            Dt.go(tT_d[ic * 128:(ic + 1) * 128, :], tstg[:, tpar, :],
                  sub=tpar)

    if _DEBUG_STOP == "C1":
        Gq.wait_all(sy)
        Dw.wait_all(sy)
        for ic in range(NIC):
            O.wait(sy, 0)
            Dt.wait_all(sy)
            Ddbg.go(tg[:, 0, :], tT_d[ic * 128:(ic + 1) * 128, :])
            Ddbg.wait(sy)
            O.go(dbg["tT"][ic * 128:(ic + 1) * 128, :], tg[:, 0, :])
        return finish()

    ve.tensor_scalar_mul(mn_t, mn_t, -1.0)
    ve.tensor_max(mx_t, mx_t, mn_t)
    v_mx = sig(ve, "V")
    te.wait_ge(sV, v_mx)
    for jt in range(8):
        te.transpose(PDh[:, 0:128], mx_t[:, jt * 128:(jt + 1) * 128], identf)
        p_mx = pe_fence()
        ve.wait_ge(sP, p_mx)
        ve.reduce_max(rmx8[:, jt:jt + 1], PDh[:, 0:128], axis=AX.X)
        vr = sig(ve, "V")
        te.wait_ge(sV, vr)
    ve.reciprocal(stats[:, 20:28], rmx8)
    ve.nop()
    ve.tensor_scalar_mul(stats[:, 20:28], stats[:, 20:28], 448.0)
    ve.tensor_scalar_mul(rmx8, rmx8, 1.0 / 448.0)
    v_st = sig(ve, "V")
    te.wait_ge(sV, v_st)
    for jt in range(8):
        for which, dstP in ((0, PA), (1, PB)):
            src1 = stats[:, 20 + jt:21 + jt] if which == 0                 else rmx8[:, jt:jt + 1]
            te.transpose(PDh[0:1, 0:128], src1, identf)
            p1 = pe_fence()
            sc.wait_ge(sP, p1)
            sc.copy(row16[0:1, which, :], PDh[0:1, 0:128])
            s1 = sig(sc, "S")
            te.wait_ge(sS, s1)
            te.matmul(dstP[:, jt * 128:(jt + 1) * 128], ones_row,
                      row16[0:1, which, :], start=True, stop=True)
    p_bb = sig(te, "P")
    sc.wait_ge(sP, p_bb)
    sc.copy(st_b, PA)
    sc.copy(str_b, PB)
    s_b = sig(sc, "S")
    te.wait_ge(sS, s_b)

    Dt.wait_all(gp)
    Gq.wait_all(gp)
    Dw.wait_all(gp)
    vq_done = [0] * NIC
    for ic in range(NIC):
        par = ic % 2
        if ic >= 2:
            gp.wait_ge(sV, vq_done[ic - 2])
        Gg.gather(tg[:, par, :], tT_d, pi_sb[:, ic:ic + 1], sub=par)
        Gg.wait(ve, par)
        ve.wait_ge(sS, s_b)
        if ic >= 2:
            Dqw.wait(ve, par)
        ve.tensor_mul(tgq[:, par, :], tg[:, par, :], st_b)
        vq_done[ic] = n["V"] + 1
        iv = tgq[:, par, :].bitcast(I32)
        ve.tensor_scalar(iv, iv, RADD, None, op0=ALU.add)
        ve.tensor_scalar(iv, iv, RMASK, None, op0=ALU.bitwise_and)
        ve.tensor_mul(tgq[:, par, :], tgq[:, par, :], str_b)
        v1 = sig(ve, "V")
        sc.wait_ge(sV, v1)
        sc.copy(qt16[:, par, :], tgq[:, par, :])
        s1 = sig(sc, "S")
        sy.wait_ge(sS, s1)
        Dqw.go(qtT_d[ic * 128:(ic + 1) * 128, :], qt16[:, par, :], sub=par)

    if _DEBUG_STOP == "C3":
        for ic in range(NIC):
            O.wait(sc, 0)
            Dqw.wait_all(sy)
            Ddbg.go(qt16[:, 0, :], qtT_d[ic * 128:(ic + 1) * 128, :])
            Ddbg.wait(sc)
            sc.copy(tgq[:, 0, :], qt16[:, 0, :])
            s1 = sig(sc, "S")
            sy.wait_ge(sS, s1)
            O.go(dbg["qtT"][ic * 128:(ic + 1) * 128, :], tgq[:, 0, :])
        return finish()

    for tt in range(16):
        te.transpose(PDh[0:1, 0:128], wd_rs[:, tt:tt + 1], identf)
        p1 = pe_fence()
        sc.wait_ge(sP, p1)
        sc.copy(row16[0:1, 0, :], PDh[0:1, 0:128])
        s1 = sig(sc, "S")
        te.wait_ge(sS, s1)
        te.matmul((PA, PB)[tt // 8][:, (tt % 8) * 128:(tt % 8 + 1) * 128],
                  ones_row, row16[0:1, 0, :], start=True, stop=True)
    p_cb = sig(te, "P")
    sc.wait_ge(sP, p_cb)
    sc.copy(csc_b[:, 0:1024], PA)
    sc.copy(csc_b[:, 1024:2048], PB)
    s_cb = sig(sc, "S")
    te.wait_ge(sS, s_cb)
    Dqw.wait_all(sy)
    sy.wait_ge(sS, s_cb)
    Gq.wait_all(sc)

    for jh in range(2):
        for mcol in range(4):
            dn_done = [0] * NIC
            v_evprev = n["V"]
            sc.wait_ge(sP, n["P"])
            for ic in range(NIC):
                par = ic % 2
                if ic >= 2:
                    sy.wait_ge(sP, dn_done[ic - 2])
                    sc.wait_ge(sP, dn_done[ic - 2])
                Dqs.go(qts[:, par, 0:512],
                       qtT_d[ic * 128:(ic + 1) * 128,
                             jh * 512:(jh + 1) * 512], sub=par)
                Dwd.go(wdTs[:, par, :],
                       qwd_d[mcol * 512:(mcol + 1) * 512,
                             ic * 128:(ic + 1) * 128],
                       sub=par, transpose=True)
                Dqs.wait(te, par)
                Dwd.wait(te, par)
                if ic == 0:
                    te.wait_ge(sV, v_evprev)
                for jj in range(4):
                    te.matmul(jslice4(jj),
                              qts[:, par, jj * 128:(jj + 1) * 128],
                              wdTs[:, par, :],
                              start=(ic == 0), stop=(ic == NIC - 1))
                dn_done[ic] = sig(te, "P")
            p_dn = dn_done[NIC - 1]
            for jj in range(4):
                j = jh * 4 + jj
                par = jj % 2
                Dxs.go(x1st[:, par, :],
                       x1_d[j * 128:(j + 1) * 128,
                            mcol * 512:(mcol + 1) * 512], sub=par)
                Dxs.wait(ve, par)
                ve.wait_ge(sP, p_dn)
                O.wait(ve, par)
                ve.tensor_mul(yev[:, par, :], jslice4(jj),
                              csc_b[:, mcol * 512:(mcol + 1) * 512])
                ve.tensor_add(yev[:, par, :], yev[:, par, :], x1st[:, par, :])
                v1 = sig(ve, "V")
                sy.wait_ge(sV, v1)
                O.go(out_d[j * 128:(j + 1) * 128,
                           mcol * 512:(mcol + 1) * 512],
                     yev[:, par, :], sub=par)
    return finish()


def kernel(L0_attn_x, L0_attn_w_aq, L0_attn_w_ak, L0_attn_w_av, L0_attn_w_ao,
           L0_ffn_w_gate, L0_ffn_w_up, L0_ffn_w_down,
           L0_ffn_up_reorder_index, L0_ffn_down_reorder_index):
    x = np.ascontiguousarray(np.asarray(L0_attn_x, dtype=np.float32))
    waq = np.ascontiguousarray(np.asarray(L0_attn_w_aq, dtype=np.float32))
    wak = np.ascontiguousarray(np.asarray(L0_attn_w_ak, dtype=np.float32))
    wav = np.ascontiguousarray(np.asarray(L0_attn_w_av, dtype=np.float32))
    wao = np.ascontiguousarray(np.asarray(L0_attn_w_ao, dtype=np.float32))
    wg = np.ascontiguousarray(np.asarray(L0_ffn_w_gate, dtype=np.float32))
    wu = np.ascontiguousarray(np.asarray(L0_ffn_w_up, dtype=np.float32))
    wd = np.ascontiguousarray(np.asarray(L0_ffn_w_down, dtype=np.float32))
    pi = np.asarray(L0_ffn_down_reorder_index).astype(np.int32)
    pi_dev = np.ascontiguousarray(pi.reshape(NIC, 128).T)

    nc, _ = build_program()
    in_maps = []
    perms = {}
    for c in range(8):
        b, p = c // 2, c % 2
        tokperm, cosF, sinFs, mkt, swap, identm = _host_tables(p)
        perms[c] = tokperm
        in_maps.append({
            "xb": np.ascontiguousarray(x[b][tokperm]),
            "w_aq": waq, "w_ak": wak, "w_av": wav, "w_ao": wao,
            "wg": wg, "wu": wu, "wd": wd, "pi": pi_dev,
            "ropecos": np.ascontiguousarray(cosF),
            "ropesin": np.ascontiguousarray(sinFs),
            "mk": np.ascontiguousarray(mkt),
            "ident": identm, "identf": np.eye(128, dtype=np.float32),
            "swapm": swap,
        })
    res = run_bass_kernel_spmd(nc, in_maps, core_ids=list(range(8)))
    out = np.empty((B, T, M), np.float32)
    for c in range(8):
        out[c // 2][perms[c][:NTOK]] = res.results[c]["out"]
    return out


# version 06
# speedup vs baseline: 1.1806x; 1.0213x over previous
import numpy as np
import ml_dtypes
import concourse.bass as bass
import concourse.mybir as mybir
from concourse.bass_utils import run_bass_kernel_spmd
from concourse.bass_core_rust import IndirectOffsetOnAxis

F32 = mybir.dt.float32
BF16 = mybir.dt.bfloat16
I32 = mybir.dt.int32
U8 = mybir.dt.uint8
AF = mybir.ActivationFunctionType
ALU = mybir.AluOpType
AX = mybir.AxisListType

B, T, M, NH, DH = 4, 2048, 2048, 16, 128
II = 5632
KT = T // 128
NJ = 8
NTOK = NJ * 128
NIC = II // 128
IBLK = 512
NIB = II // IBLK
RADD = 0x80000
RMASK = -0x100000

_DEBUG_STOP = None
_DEBUG_HEADS = NH
_DEBUG_TILES = KT
_DEBUG_SUB = 3
_DEBUG_BSUB = 9
_STRICT = [False]
_DEPGUARD = [True]


def _host_tables(p):
    own = [2 * j + p for j in range(NJ)]
    other = [t for t in range(KT) if t not in own]
    tokperm = np.concatenate([np.arange(t * 128, (t + 1) * 128)
                              for t in own + other])
    pos = tokperm.astype(np.float64)
    dims = np.arange(DH // 2, dtype=np.float64)
    ang = np.power(10000.0, -dims / (DH // 2))
    rad = pos[None, :] * ang[:, None]
    cos = np.cos(rad)
    sin = np.sin(rad)
    cosF = np.concatenate([cos, cos], 0).astype(ml_dtypes.bfloat16)
    sinFs = np.concatenate([-sin, sin], 0).astype(ml_dtypes.bfloat16)
    tri = (np.arange(128)[:, None] <= np.arange(128)[None, :]).astype(np.float32)
    mk = np.empty((KT, 128, 128), np.float32)
    for m in range(KT):
        mk[m] = tri if m < NJ else (0.0 if p == 0 else 1.0)
    swap = np.zeros((128, 128), np.float32)
    swap[:64, 64:] = np.eye(64)
    swap[64:, :64] = np.eye(64)
    return (tokperm, cosF, sinFs, mk.astype(ml_dtypes.bfloat16),
            swap.astype(ml_dtypes.bfloat16),
            np.eye(128, dtype=ml_dtypes.bfloat16))


def build_program():
    nc = bass.Bass("TRN2", target_bir_lowering=False, debug=False)

    xb = nc.dram_tensor("xb", [T, M], F32, kind="ExternalInput").ap()
    w_aq = nc.dram_tensor("w_aq", [NH, M, DH], F32, kind="ExternalInput").ap()
    w_ak = nc.dram_tensor("w_ak", [NH, M, DH], F32, kind="ExternalInput").ap()
    w_av = nc.dram_tensor("w_av", [NH, M, DH], F32, kind="ExternalInput").ap()
    w_ao = nc.dram_tensor("w_ao", [NH, DH, M], F32, kind="ExternalInput").ap()
    wg_d = nc.dram_tensor("wg", [II, M], F32, kind="ExternalInput").ap()
    wu_d = nc.dram_tensor("wu", [II, M], F32, kind="ExternalInput").ap()
    wd_d = nc.dram_tensor("wd", [M, II], F32, kind="ExternalInput").ap()
    pi_d = nc.dram_tensor("pi", [128, NIC], I32, kind="ExternalInput").ap()
    cos_d = nc.dram_tensor("ropecos", [128, T], BF16, kind="ExternalInput").ap()
    sin_d = nc.dram_tensor("ropesin", [128, T], BF16, kind="ExternalInput").ap()
    mk_d = nc.dram_tensor("mk", [KT, 128, 128], BF16, kind="ExternalInput").ap()
    id_d = nc.dram_tensor("ident", [128, 128], BF16, kind="ExternalInput").ap()
    sw_d = nc.dram_tensor("swapm", [128, 128], BF16, kind="ExternalInput").ap()
    idf_d = nc.dram_tensor("identf", [128, 128], F32, kind="ExternalInput").ap()

    out_d = nc.dram_tensor("out", [NTOK, M], F32, kind="ExternalOutput").ap()

    qwg_d = nc.dram_tensor("qwg", [II, M], BF16, kind="Internal").ap()
    qwu_d = nc.dram_tensor("qwu", [II, M], BF16, kind="Internal").ap()
    qwd_d = nc.dram_tensor("qwd", [M, II], BF16, kind="Internal").ap()
    tT_d = nc.dram_tensor("tT", [II, NTOK], F32, kind="Internal").ap()
    qtT_d = nc.dram_tensor("qtT", [II, NTOK], BF16, kind="Internal").ap()
    x1_d = nc.dram_tensor("x1s", [NTOK, M], F32, kind="Internal").ap()

    dbg = {}
    if _DEBUG_STOP is not None:
        for nm, shp in [("lnxT", [16, 128, T]), ("kT", [128, T]),
                        ("vTok", [KT, 128, 128]), ("qT", [128, NTOK]),
                        ("oT", [128, NH * NTOK]), ("x1", [NTOK, M]),
                        ("qx", [NTOK, M]), ("tT", [II, NTOK]),
                        ("qtT", [II, NTOK]), ("st", [128, 32]),
                        ("lnA", [128, M])]:
            dbg[nm] = nc.dram_tensor("dbg_" + nm, shp, F32,
                                     kind="ExternalOutput").ap()
        dbg["lnraw"] = nc.dram_tensor("dbg_lnraw", [16, 128, T], BF16,
                                      kind="ExternalOutput").ap()

    def sba(name, shape, dt):
        return nc.alloc_sbuf_tensor(name, list(shape), dt).ap()

    ident = sba("identc", [128, 128], BF16)
    identf = sba("identfc", [128, 128], F32)
    swapm = sba("swapc", [128, 128], BF16)
    mk = sba("mkc", [128, KT, 128], BF16)
    cosF = sba("cosc", [128, T], BF16)
    sinFs = sba("sinc", [128, T], BF16)
    pi_sb = sba("pic", [128, NIC], I32)
    ones_col = sba("onescol", [128, 1], BF16)
    ones_row = sba("onesrow", [1, 128], BF16)
    wg_sc = sba("wgsc", [128, NIC], F32)
    wu_sc = sba("wusc", [128, NIC], F32)
    wd_sc = sba("wdsc", [128, 16], F32)
    wg_rs = sba("wgrs", [128, NIC], F32)
    wu_rs = sba("wurs", [128, NIC], F32)
    wd_rs = sba("wdrs", [128, 16], F32)
    stats = sba("stats", [128, 32], F32)
    rmx8 = sba("rmx8", [128, 8], F32)
    row16 = sba("row16", [1, 2, 128], BF16)
    negc = sba("negc", [128, 1], F32)
    cm16 = sba("cm16", [128, 2], BF16)
    kT = sba("kT", [128, T], BF16)
    kvraw = sba("kvraw", [128, T], BF16)
    kTr, vT = kvraw, kvraw
    qTr = kvraw[:, 0:NTOK]
    vTok2 = sba("vTok2", [128, KT, 256], BF16)
    qT = sba("qT", [128, NTOK], BF16)
    pt = sba("pt", [128, 2, NTOK], BF16)

    def region(name, nbytes):
        return nc.alloc_sbuf_tensor(name, [128, nbytes], U8).ap()

    def view(reg, o, shape, dt):
        esz = {F32: 4, BF16: 2, I32: 4}[dt]
        r = reg.bitcast(dt)
        flat = 1
        for s in shape[1:]:
            flat *= s
        v = r[:, o // esz:o // esz + flat]
        if len(shape) == 2:
            return v
        pat = "p (" + " ".join(f"d{i}" for i in range(1, len(shape))) + ") ->" \
              + " p " + " ".join(f"d{i}" for i in range(1, len(shape)))
        kw = {f"d{i}": shape[i] for i in range(1, len(shape) - 1)}
        return v.rearrange(pat, **kw)

    RG1 = region("RG1", 65536)
    lnxT = view(RG1, 0, [128, 16, T], BF16)
    x1_sb = view(RG1, 0, [128, NJ, M], F32)
    wgT = view(RG1, 0, [128, 2, 16, IBLK], BF16)
    wuT = view(RG1, 32768, [128, 2, 16, IBLK], BF16)
    RG2 = region("RG2", 32768)
    oT = view(RG2, 0, [128, NH, NTOK], BF16)
    qxT = view(RG2, 0, [128, 16, NTOK], BF16)
    RG3 = region("RG3", 36864)
    xst = view(RG3, 0, [128, 2, M], F32)
    lnA = view(RG3, 16384, [128, M], F32)
    wq16 = view(RG3, 0, [128, 2, 16, DH], BF16)
    wk16 = view(RG3, 8192, [128, 2, 16, DH], BF16)
    wv2 = view(RG3, 16384, [128, 16, 2, 128], BF16)
    wv2f = view(RG3, 16384, [128, 16, 256], BF16)
    linv = view(RG3, 24576, [128, NTOK], F32)
    rowB = view(RG3, 28672, [1, 1024], F32)
    rowA16 = view(RG3, 32768, [1, 1024], BF16)
    rowA = view(RG3, 34816, [1, 512], F32)
    wao16 = view(RG3, 0, [128, 2, 512], BF16)
    xres = view(RG3, 2048, [128, 2, 512], F32)
    sqx = view(RG3, 0, [128, M], F32)
    lnf = view(RG3, 8192, [128, M], F32)
    tstg = view(RG3, 16384, [128, 2, NTOK], F32)
    gs_t = view(RG3, 24576, [128, NTOK], F32)
    mx_t = view(RG3, 28672, [128, NTOK], F32)
    mn_t = view(RG3, 32768, [128, NTOK], F32)
    lnsq = view(RG3, 24576, [128, M], F32)
    st_b = view(RG3, 28672, [128, NTOK], F32)
    str_b = view(RG3, 32768, [128, NTOK], F32)
    csc_b = view(RG3, 16384, [128, M], F32)
    RG4 = region("RG4", 28672)
    wst = view(RG4, 0, [128, 2, 2816], F32)
    tg = view(RG4, 0, [128, 2, NTOK], F32)
    tgq = view(RG4, 8192, [128, 2, NTOK], F32)
    qt16 = view(RG4, 16384, [128, 2, NTOK], BF16)
    wdTs = view(RG4, 20480, [128, 2, 512], BF16)
    x1st = view(RG4, 22528, [128, 2, 512], F32)
    qts = view(RG4, 24576, [128, 2, NTOK], BF16)
    yev = view(RG4, 0, [128, 2, 512], F32)

    PA = nc.alloc_psum_tensor("PA", [128, 1024], F32).ap()
    PB = nc.alloc_psum_tensor("PB", [128, 1024], F32).ap()
    PC = nc.alloc_psum_tensor("PC", [128, 1024], F32).ap()
    PDh = nc.alloc_psum_tensor("PDh", [128, 512], F32).ap()
    PF = nc.alloc_psum_tensor("PF", [1, 128], F32).ap()

    _GUARD = {"tensor_copy", "tensor_tensor", "tensor_add", "tensor_sub",
              "tensor_mul", "tensor_max", "tensor_scalar",
              "tensor_single_scalar", "tensor_scalar_add",
              "tensor_scalar_mul", "tensor_scalar_max", "tensor_scalar_min",
              "tensor_reduce", "reduce_max", "reduce_sum", "reciprocal",
              "memset", "activation", "square", "sqrt", "copy", "mul",
              "add", "select"}

    _ESZ = {F32: 4, BF16: 2, I32: 4, U8: 1}

    def _rng(ap):
        try:
            esz = _ESZ.get(ap.dtype, 4)
            span = 1
            for i, (st_, ct) in enumerate(ap.ap):
                if i == 0:
                    continue
                span += (ct - 1) * abs(st_)
            lo = ap.offset * esz
            return (ap.tensor.name, lo, lo + span * esz)
        except Exception:
            try:
                return (ap.tensor.name, 0, 1 << 60)
            except Exception:
                return None

    class _EG:
        def __init__(self, e):
            self._e = e
            self.last = None
            self._hist = []

        def __getattr__(self, name):
            f = getattr(self._e, name)
            if name in _GUARD:
                def g(*a, **k):
                    aps = [x for x in list(a) + list(k.values())
                           if hasattr(x, "tensor") and hasattr(x, "offset")]
                    outr = _rng(aps[0]) if aps else None
                    fs = aps[0].free_size() if aps else 1024
                    need_drain = need_nop = False
                    for idx, (pw, pfs) in enumerate(self._hist):
                        for x in aps[1:]:
                            rr = _rng(x)
                            if rr is None or (pw is not None
                                              and rr[0] == pw[0]
                                              and rr[1] < pw[2]
                                              and pw[1] < rr[2]):
                                if fs < 512 or pfs < 512:
                                    need_drain = True
                                elif idx == 0:
                                    need_nop = True
                    if not _DEPGUARD[0]:
                        need_drain = fs < 512
                        need_nop = True
                    if _STRICT[0] or need_drain:
                        self._e.drain()
                        self._hist.clear()
                    elif need_nop:
                        self._e.nop()
                    r = f(*a, **k)
                    self.last = r
                    self._hist.insert(0, (outr, fs))
                    del self._hist[3:]
                    return r
                return g
            return f

    class _TG:
        def __init__(self, e):
            self._e = e
            self.last = None

        def __getattr__(self, name):
            f = getattr(self._e, name)
            if name in ("transpose", "matmul"):
                def g(*a, **k):
                    r = f(*a, **k)
                    self.last = r
                    return r
                return g
            return f

    sy = nc.sync
    gp = _EG(nc.gpsimd)
    ve = _EG(nc.vector)
    sc = _EG(nc.scalar)
    te = _TG(nc.tensor)
    sems = {k: nc.alloc_semaphore(k) for k in ("P", "V", "S")}
    n = {k: 0 for k in sems}
    sP, sV, sS = (sems[k] for k in ("P", "V", "S"))

    class DStream:
        def __init__(self, name, nsub=1, eng=None):
            self.sems = [nc.alloc_semaphore(name + str(i))
                         for i in range(nsub)]
            self.n = [0] * nsub
            self.eng = eng if eng is not None else sy

        def go(self, out, in_, sub=0, transpose=False):
            if transpose:
                r = self.eng.dma_start_transpose(out, in_)
            else:
                r = self.eng.dma_start(out=out, in_=in_)
            r.then_inc(self.sems[sub], 16)
            self.n[sub] += 16

        def gather(self, out, in_, idx, sub=0):
            self.eng.indirect_dma_start(
                out, None, in_, IndirectOffsetOnAxis(idx, 0)) \
                .then_inc(self.sems[sub], 16)
            self.n[sub] += 16

        def wait(self, eng, sub=0):
            eng.wait_ge(self.sems[sub], self.n[sub])

        def wait_all(self, eng):
            for i, s in enumerate(self.sems):
                eng.wait_ge(s, self.n[i])

    Dc = DStream("Dc")
    Dx = DStream("Dx", 2)
    Dw = DStream("Dw", 2)
    Gq = DStream("Gq", 2, gp)
    Gw = DStream("Gw", 2, gp)
    Gv = DStream("Gv", 1, gp)
    Ga = DStream("Ga", 2, gp)
    Dxr = DStream("Dxr", 2)
    Dx1 = DStream("Dx1")
    Dwg = DStream("Dwg", 2)
    Dwu = DStream("Dwu", 2)
    Dt = DStream("Dt", 2)
    Gg = DStream("Gg", 2, gp)
    Dqw = DStream("Dqw", 2)
    Dqs = DStream("Dqs", 2)
    Dwd = DStream("Dwd", 2)
    Dxs = DStream("Dxs", 2)
    O = DStream("O", 2)
    Ddbg = DStream("Ddbg")
    streams = [Dc, Dx, Dw, Gq, Gw, Gv, Ga, Dxr, Dx1, Dwg, Dwu, Dt, Gg,
               Dqw, Dqs, Dwd, Dxs, O, Ddbg]

    def sig(eng, key):
        eng.last.then_inc(sems[key], 1)
        n[key] += 1
        return n[key]

    def finish():
        for e in (sy, gp, ve, sc, te):
            O.wait_all(e)
        return nc, dbg

    for st_ in streams:
        for s in st_.sems:
            sy.sem_clear(s)
    for s in sems.values():
        sy.sem_clear(s)
    nc.all_engine_barrier()
    Dc.go(ident, id_d)
    Dc.go(swapm, sw_d)
    Dc.go(identf, idf_d)
    Dc.go(mk, mk_d.rearrange("k p q -> p k q"))
    Dc.go(cosF, cos_d)
    Dc.go(sinFs, sin_d)
    Dc.go(pi_sb, pi_d)
    Dc.wait(ve)
    if _DEBUG_STOP is not None:
        for rg_ in (RG1, RG2, RG3, RG4):
            ve.memset(rg_.bitcast(F32), 0.0)
        ve.memset(stats, 0.0)
        ve.memset(rmx8, 0.0)
    ve.memset(ones_col, 1.0)
    ve.memset(ones_row, 1.0)
    v0 = sig(ve, "V")
    for e in (te, sc, gp, sy):
        e.wait_ge(sV, v0)

    def pe_fence():
        return sig(te, "P")

    def tr_f32(src_f32, ncols, write_fn):
        nt = ncols // 128
        done = 0
        while done < nt:
            batch = min(16, nt - done)
            for k in range(batch):
                c = done + k
                P, o = (PA, PB)[k // 8], (k % 8) * 128
                te.transpose(P[:, o:o + 128],
                             src_f32[:, c * 128:(c + 1) * 128], identf)
            p1 = pe_fence()
            ve.wait_ge(sP, p1)
            for k in range(batch):
                c = done + k
                P, o = (PA, PB)[k // 8], (k % 8) * 128
                write_fn(c, P[:, o:o + 128])
            v1 = sig(ve, "V")
            te.wait_ge(sV, v1)
            done += batch

    lnsq_free = [0]
    sqrt_sig = [0]

    def ln_stats(src_ap):
        ve.wait_ge(sS, sqrt_sig[0])
        ve.reduce_sum(stats[:, 0:1], src_ap, axis=AX.X)
        sc.wait_ge(sV, lnsq_free[0])
        sc.square(lnsq, src_ap)
        s1 = sig(sc, "S")
        ve.wait_ge(sS, s1)
        ve.reduce_sum(stats[:, 1:2], lnsq, axis=AX.X)
        lnsq_free[0] = n["V"] + 1
        ve.tensor_scalar_mul(stats[:, 2:3], stats[:, 0:1], 1.0 / M)
        ve.tensor_scalar_mul(stats[:, 3:4], stats[:, 1:2], 1.0 / M)
        ve.tensor_mul(stats[:, 4:5], stats[:, 2:3], stats[:, 2:3])
        ve.tensor_sub(stats[:, 5:6], stats[:, 3:4], stats[:, 4:5])
        ve.tensor_scalar_add(stats[:, 5:6], stats[:, 5:6], 1e-5)
        v1 = sig(ve, "V")
        sc.wait_ge(sV, v1)
        sc.sqrt(stats[:, 6:7], stats[:, 5:6])
        s2 = sig(sc, "S")
        sqrt_sig[0] = s2
        ve.wait_ge(sS, s2)
        ve.reciprocal(stats[:, 7:8], stats[:, 6:7])
        v2 = sig(ve, "V")
        sc.wait_ge(sV, v2)


    s_lnuse = [0] * KT
    for i in range(_DEBUG_TILES):
        par = i % 2
        if i >= 2:
            sy.wait_ge(sV, s_lnuse[i - 2])
        Dx.go(xst[:, par, :], xb[i * 128:(i + 1) * 128, :], sub=par)
        Dx.wait(ve, par)
        Dx.wait(sc, par)
        if _DEBUG_SUB >= 1:
            ln_stats(xst[:, par, :])
        if _DEBUG_SUB >= 2:
            if i >= 1:
                ve.wait_ge(sP, p_lnA[i - 1])
            ve.tensor_scalar(lnA, xst[:, par, :], stats[:, 2:3],
                             stats[:, 7:8], op0=ALU.subtract, op1=ALU.mult)
            s_lnuse[i] = sig(ve, "V")
        if _DEBUG_SUB >= 3:
            te.wait_ge(sV, s_lnuse[i])
            tr_f32(lnA, M,
                   lambda c, src, i=i: ve.tensor_copy(
                       lnxT[:, c, i * 128:(i + 1) * 128], src))
        if i == 0:
            p_lnA = [0] * KT
        p_lnA[i] = n["P"]
    v_endA = n["V"]

    if _DEBUG_STOP == "A":
        sy.wait_ge(sV, v_endA)
        O.go(dbg["st"], stats)
        O.go(dbg["lnA"], lnA)
        for c in range(16):
            O.go(dbg["lnraw"][c], lnxT[:, c, :])
        for c in range(16):
            sc.wait_ge(sV, v_endA)
            O.wait(sc, 0)
            sc.copy(lnf, lnxT[:, c, :])
            s1 = sig(sc, "S")
            sy.wait_ge(sS, s1)
            O.go(dbg["lnxT"][c], lnf)
        return finish()

    w_units = []
    for t in range(II // 128):
        w_units.append(("g", t))
        w_units.append(("u", t))
    for t in range(16):
        w_units.append(("dmax", t))
        for q in range(4):
            w_units.append(("d", t, q))
    wpar = [0]
    w_free = {0: None, 1: None}
    w_dmax_v = {}

    def w_wait_free(par):
        g = w_free[par]
        if g is None:
            return
        if g[0] == "V":
            sy.wait_ge(sV, g[1])
        else:
            Gq.wait(sy, par)

    def w_round_store(par, width, scale_col, dst):
        sc.activation(wst[:, par, 0:width], wst[:, par, 0:width], AF.Copy,
                      scale=scale_col)
        s1 = sig(sc, "S")
        ve.wait_ge(sS, s1)
        iv = wst[:, par, 0:width].bitcast(I32)
        ve.tensor_scalar(iv, iv, RADD, None, op0=ALU.add)
        ve.tensor_scalar(iv, iv, RMASK, None, op0=ALU.bitwise_and)
        v2 = sig(ve, "V")
        gp.wait_ge(sV, v2)
        Gq.go(dst, wst[:, par, 0:width], sub=par)
        w_free[par] = ("Gq", par)

    def emit_w_unit(u):
        if u[0] in ("g", "u"):
            _, t = u
            src, dst = (wg_d, qwg_d) if u[0] == "g" else (wu_d, qwu_d)
            scl = wg_sc if u[0] == "g" else wu_sc
            rsl = wg_rs if u[0] == "g" else wu_rs
            par = wpar[0] % 2
            wpar[0] += 1
            w_wait_free(par)
            Dw.go(wst[:, par, 0:2048], src[t * 128:(t + 1) * 128, :], sub=par)
            Dw.wait(ve, par)
            ve.reduce_max(stats[:, 10:11], wst[:, par, 0:2048], axis=AX.X,
                          apply_absolute_value=True)
            ve.reciprocal(stats[:, 11:12], stats[:, 10:11])
            ve.tensor_scalar_mul(scl[:, t:t + 1], stats[:, 11:12], 448.0)
            ve.tensor_scalar_mul(rsl[:, t:t + 1], stats[:, 10:11], 1 / 448.0)
            v1 = sig(ve, "V")
            sc.wait_ge(sV, v1)
            w_round_store(par, 2048, scl[:, t:t + 1],
                          dst[t * 128:(t + 1) * 128, :])
        elif u[0] == "dmax":
            _, t = u
            for qq in range(4):
                par = wpar[0] % 2
                wpar[0] += 1
                w_wait_free(par)
                Dw.go(wst[:, par, 0:1408],
                      wd_d[t * 128:(t + 1) * 128, qq * 1408:(qq + 1) * 1408],
                      sub=par)
                Dw.wait(ve, par)
                ve.reduce_max(stats[:, 12 + qq:13 + qq], wst[:, par, 0:1408],
                              axis=AX.X, apply_absolute_value=True)
                w_free[par] = ("V", sig(ve, "V"))
            ve.tensor_max(stats[:, 12:13], stats[:, 12:13], stats[:, 13:14])
            ve.tensor_max(stats[:, 14:15], stats[:, 14:15], stats[:, 15:16])
            ve.tensor_max(stats[:, 12:13], stats[:, 12:13], stats[:, 14:15])
            ve.reciprocal(stats[:, 13:14], stats[:, 12:13])
            ve.tensor_scalar_mul(wd_sc[:, t:t + 1], stats[:, 13:14], 448.0)
            ve.tensor_scalar_mul(wd_rs[:, t:t + 1], stats[:, 12:13], 1 / 448.0)
            w_dmax_v[t] = sig(ve, "V")
        else:
            _, t, q = u
            par = wpar[0] % 2
            wpar[0] += 1
            w_wait_free(par)
            Dw.go(wst[:, par, 0:1408],
                  wd_d[t * 128:(t + 1) * 128, q * 1408:(q + 1) * 1408],
                  sub=par)
            Dw.wait(sc, par)
            sc.wait_ge(sV, w_dmax_v[t])
            w_round_store(par, 1408, wd_sc[:, t:t + 1],
                          qwd_d[t * 128:(t + 1) * 128,
                                q * 1408:(q + 1) * 1408])

    p_qproj_prev = [0]

    def fetch_head_w(h):
        hp = h % 2
        if h >= 2:
            gp.wait_ge(sP, p_qproj_prev[0])
        Gw.go(wq16[:, hp, :, :],
              w_aq[h].rearrange("(c p) d -> p c d", p=128), sub=hp)
        Gw.go(wk16[:, hp, :, :],
              w_ak[h].rearrange("(c p) d -> p c d", p=128), sub=hp)

    def fetch_wv2(h):
        Gv.go(wv2[:, :, 0, :], w_av[h].rearrange("(c p) d -> p c d", p=128))
        Gv.go(wv2[:, :, 1, :],
              w_av[h + 1].rearrange("(c p) d -> p c d", p=128))

    gp.wait_ge(sV, v_endA)
    fetch_head_w(0)
    WPH = (len(w_units) + NH - 1) // NH

    def proj(dst16, wsb, hp, psum, ofs):
        for mc in range(16):
            first, last = mc == 0, mc == 15
            for c0 in (0, 512):
                te.matmul(psum[:, c0:c0 + 512], wsb[:, hp, mc, :],
                          lnxT[:, mc, ofs + c0:ofs + c0 + 512],
                          start=first, stop=last)
        p1 = sig(te, "P")
        sc.wait_ge(sP, p1)
        sc.copy(dst16[:, ofs:ofs + 1024], psum[:, 0:1024])
        s1 = sig(sc, "S")
        te.wait_ge(sS, s1)
        return s1

    s_v2 = 0
    p_v2mm = 0
    pv_done_prev = 0
    for h in range(_DEBUG_HEADS):
        hp = h % 2
        if h + 1 < NH:
            fetch_head_w(h + 1)
        Gw.wait(te, hp)
        if hp == 0 and _DEBUG_BSUB >= 1:
            if h >= 2:
                gp.wait_ge(sP, p_v2mm)
            fetch_wv2(h)
            Gv.wait(te)
            te.wait_ge(sS, s_v2)
            if h >= 2:
                te.wait_ge(sP, pv_done_prev)
            sv2_evac = [0] * KT
            for tt in range(KT):
                VP = PDh if tt % 2 == 0 else PC
                if tt >= 2:
                    te.wait_ge(sS, sv2_evac[tt - 2])
                for mc in range(16):
                    te.matmul(VP[:, 0:256],
                              lnxT[:, mc, tt * 128:(tt + 1) * 128],
                              wv2f[:, mc, :],
                              start=(mc == 0), stop=(mc == 15))
                p1 = sig(te, "P")
                sc.wait_ge(sP, p1)
                sc.copy(vTok2[:, tt, :], VP[:, 0:256])
                sv2_evac[tt] = sig(sc, "S")
            s_v2 = sv2_evac[KT - 1]
            p_v2mm = n["P"]
            te.wait_ge(sS, s_v2)
        if _DEBUG_BSUB < 2:
            continue
        proj(kTr, wk16, hp, PA, 0)
        sk1 = proj(kTr, wk16, hp, PB, 1024)
        for c0 in range(0, T, 512):
            te.matmul((PA if c0 < 1024 else PB)[:, c0 % 1024:c0 % 1024 + 512],
                      swapm, kTr[:, c0:c0 + 512], start=True, stop=True)
        pswap = sig(te, "P")
        ve.wait_ge(sS, sk1)
        ve.tensor_mul(kT, kTr, cosF)
        ve.wait_ge(sP, pswap)
        ve.tensor_mul(kTr[:, 0:1024], PA, sinFs[:, 0:1024])
        ve.tensor_mul(kTr[:, 1024:2048], PB, sinFs[:, 1024:2048])
        ve.tensor_add(kT, kT, kTr)
        ve.reduce_max(stats[:, 16:17], kT, axis=AX.X,
                      apply_absolute_value=True)
        vk = sig(ve, "V")
        te.wait_ge(sV, vk)
        if _DEBUG_BSUB < 3:
            continue
        sc.wait_ge(sV, vk)
        sq0 = proj(qTr, wq16, hp, PA, 0)
        p_qproj_prev[0] = n["P"]
        for c0 in (0, 512):
            te.matmul(PA[:, c0:c0 + 512], swapm, qTr[:, c0:c0 + 512],
                      start=True, stop=True)
        pswq = sig(te, "P")
        ve.wait_ge(sS, sq0)
        ve.tensor_mul(qT, qTr, cosF[:, 0:NTOK])
        ve.wait_ge(sP, pswq)
        ve.tensor_mul(qTr, PA, sinFs[:, 0:NTOK])
        ve.tensor_add(qT, qT, qTr)
        ve.reduce_max(stats[:, 17:18], qT, axis=AX.X,
                      apply_absolute_value=True)
        vq = sig(ve, "V")
        te.wait_ge(sV, vq)
        sc.wait_ge(sV, vq)
        sc.copy(cm16[:, 0:1], stats[:, 16:17])
        sc.copy(cm16[:, 1:2], stats[:, 17:18])
        scm = sig(sc, "S")
        te.wait_ge(sS, scm)
        te.matmul(PA[0:1, 0:1], cm16[:, 0:1], cm16[:, 1:2],
                  start=True, stop=True)
        pc1 = sig(te, "P")
        sc.wait_ge(sP, pc1)
        sc.copy(rowA16[0:1, 0:1], PA[0:1, 0:1])
        sc1 = sig(sc, "S")
        te.wait_ge(sS, sc1)
        te.matmul(PA[:, 0:1], ones_row, rowA16[0:1, 0:1],
                  start=True, stop=True)
        pc2 = sig(te, "P")
        sc.wait_ge(sP, pc2)
        sc.activation(negc, PA[:, 0:1], AF.Copy, scale=-1.0 / 128.0)
        snegc = sig(sc, "S")
        te.wait_ge(sS, snegc)

        if _DEBUG_BSUB < 4:
            continue
        exp_done = [0] * KT
        pv_done = [0] * KT
        vm1 = [0] * KT

        def q0_of(m):
            return (m if m < NJ else m - NJ) * 128

        def emit_pv(m):
            q0 = q0_of(m)
            te.wait_ge(sV, vm1[m])
            c0 = q0
            while c0 < NTOK:
                ce = min((c0 // 512 + 1) * 512, NTOK)
                stop = (m == 11) if c0 < 512 else (m == 15)
                te.matmul(PA[0:1, c0:ce], ones_col, pt[:, m % 2, c0:ce],
                          start=(m == 0), stop=stop)
                te.matmul(PB[:, c0:ce],
                          vTok2[:, m, hp * 128:(hp + 1) * 128],
                          pt[:, m % 2, c0:ce],
                          start=(m == 0), stop=stop)
                c0 = ce
            pv_done[m] = sig(te, "P")

        for m in range(KT):
            q0 = q0_of(m)
            SP = PC
            if m >= 1:
                te.wait_ge(sS, exp_done[m - 1])
            c0 = q0
            while c0 < NTOK:
                ce = min((c0 // 512 + 1) * 512, NTOK)
                te.matmul(SP[:, c0:ce], kT[:, m * 128:(m + 1) * 128],
                          qT[:, c0:ce], start=True, stop=True)
                c0 = ce
            ps = sig(te, "P")
            sc.wait_ge(sP, ps)
            if m >= 2:
                sc.wait_ge(sP, pv_done[m - 2])
            sc.activation(pt[:, m % 2, q0:NTOK], SP[:, q0:NTOK], AF.Exp,
                          bias=negc, scale=1.0 / 128.0)
            exp_done[m] = sig(sc, "S")
            ve.wait_ge(sS, exp_done[m])
            ve.tensor_mul(pt[:, m % 2, q0:q0 + 128],
                          pt[:, m % 2, q0:q0 + 128], mk[:, m, :])
            vm1[m] = sig(ve, "V")
            if m >= 1:
                emit_pv(m - 1)
        emit_pv(KT - 1)
        pv_done_prev = pv_done[KT - 1]
        if _DEBUG_BSUB < 5:
            continue
        sc.wait_ge(sP, pv_done[KT - 1])
        sc.copy(rowA[0:1, 0:512], PA[0:1, 0:512])
        sc.copy(rowB[0:1, 0:512], PA[0:1, 512:1024])
        sl1 = sig(sc, "S")
        ve.wait_ge(sS, sl1)
        ve.reciprocal(rowB[0:1, 512:1024], rowB[0:1, 0:512])
        ve.reciprocal(rowA[0:1, 0:512], rowA[0:1, 0:512])
        ve.tensor_copy(rowA16[0:1, 0:512], rowA[0:1, 0:512])
        ve.tensor_copy(rowA16[0:1, 512:1024], rowB[0:1, 512:1024])
        vl1 = sig(ve, "V")
        te.wait_ge(sV, vl1)
        te.matmul(PC[:, 0:512], ones_row, rowA16[0:1, 0:512],
                  start=True, stop=True)
        te.matmul(PC[:, 512:1024], ones_row, rowA16[0:1, 512:1024],
                  start=True, stop=True)
        pl1 = sig(te, "P")
        sc.wait_ge(sP, pl1)
        sc.copy(linv, PC)
        sl2 = sig(sc, "S")
        ve.wait_ge(sS, sl2)
        ve.tensor_mul(oT[:, h, :], PB, linv)
        voT = sig(ve, "V")
        te.wait_ge(sV, voT)
        pv_done_prev = pv_done[KT - 1]
        for u in w_units[h * WPH:(h + 1) * WPH]:
            emit_w_unit(u)

    if _DEBUG_STOP == "B":
        v = n["V"]
        sc.wait_ge(sV, v)
        Gw.wait_all(sc)
        Gv.wait_all(sc)
        sc.copy(sqx[:, 0:T], kT)
        s1 = sig(sc, "S")
        sy.wait_ge(sS, s1)
        O.go(dbg["kT"], sqx[:, 0:T])
        for c in range(KT):
            O.wait(sc, 0)
            sc.copy(lnf[:, 0:128], vTok2[:, c, 0:128] if _DEBUG_HEADS % 2
                    else vTok2[:, c, 128:256])
            s1 = sig(sc, "S")
            sy.wait_ge(sS, s1)
            O.go(dbg["vTok"][c], lnf[:, 0:128])
        O.wait(sc, 0)
        sc.copy(sqx[:, 0:NTOK], qT)
        s1 = sig(sc, "S")
        sy.wait_ge(sS, s1)
        O.go(dbg["qT"], sqx[:, 0:NTOK])
        for hh in range(_DEBUG_HEADS):
            O.wait(sc, 0)
            sc.copy(lnf[:, 0:NTOK], oT[:, hh, :])
            s1 = sig(sc, "S")
            sy.wait_ge(sS, s1)
            O.go(dbg["oT"][:, hh * NTOK:(hh + 1) * NTOK], lnf[:, 0:NTOK])
        return finish()

    def jslice4(jj):
        return (PA, PB)[jj // 2][:, (jj % 2) * 512:(jj % 2) * 512 + 512]

    p_b2pre = n["P"]
    waoall = (view(RG3, 0, [128, 16, 512], BF16),
              view(RG3, 20480, [128, 16, 512], BF16))
    xresB = view(RG3, 16384, [128, 2, 512], F32)
    v_ev = {0: 0, 1: 0}
    p_blk = [p_b2pre, p_b2pre]
    gp.wait_ge(sP, p_b2pre)
    sy.wait_ge(sP, p_b2pre)
    te.wait_ge(sP, p_b2pre)
    for mcol in range(4):
        par = mcol % 2
        buf = waoall[par]
        if mcol >= 2:
            gp.wait_ge(sP, p_blk[par])
        for cch in range(16):
            Ga.go(buf[:, cch, :],
                  w_ao[cch, :, mcol * 512:(mcol + 1) * 512], sub=par)
        Ga.wait(te, par)
        for jh in range(2):
            for cch in range(16):
                for jj in range(4):
                    j = jh * 4 + jj
                    te.matmul(jslice4(jj), oT[:, cch, j * 128:(j + 1) * 128],
                              buf[:, cch, :],
                              start=(cch == 0), stop=(cch == 15))
            pj = sig(te, "P")
            for jj in range(4):
                j = jh * 4 + jj
                rpar = jj % 2
                sy.wait_ge(sV, v_ev[rpar])
                Dxr.go(xresB[:, rpar, :],
                       xb[j * 128:(j + 1) * 128,
                          mcol * 512:(mcol + 1) * 512], sub=rpar)
                Dxr.wait(ve, rpar)
                ve.wait_ge(sP, pj)
                ve.tensor_add(x1_sb[:, j, mcol * 512:(mcol + 1) * 512],
                              jslice4(jj), xresB[:, rpar, :])
                v_ev[rpar] = sig(ve, "V")
            te.wait_ge(sV, v_ev[rpar])
        p_blk[par] = n["P"]
    v_x1 = n["V"]

    if _DEBUG_STOP == "B2":
        sy.wait_ge(sV, v_x1)
        for j in range(NJ):
            O.go(dbg["x1"][j * 128:(j + 1) * 128, :], x1_sb[:, j, :])
        return finish()

    s_qx = [0] * NJ
    for j in range(NJ):
        ve.wait_ge(sV, v_x1)
        sc.wait_ge(sV, v_x1)
        ln_stats(x1_sb[:, j, :])
        ve.tensor_scalar(lnf, x1_sb[:, j, :], stats[:, 2:3],
                         stats[:, 7:8], op0=ALU.subtract, op1=ALU.mult)
        ve.reduce_max(stats[:, 10:11], lnf, axis=AX.X,
                      apply_absolute_value=True)
        ve.reciprocal(stats[:, 11:12], stats[:, 10:11])
        ve.tensor_scalar_mul(stats[:, 12:13], stats[:, 11:12], 448.0)
        ve.tensor_scalar_mul(stats[:, 13:14], stats[:, 10:11], 1.0 / 448.0)
        v3 = sig(ve, "V")
        sc.wait_ge(sV, v3)
        if _DEBUG_STOP == "C0":
            O.wait(sc, 0)
        sc.activation(sqx, lnf, AF.Copy, scale=stats[:, 12:13])
        s4 = sig(sc, "S")
        ve.wait_ge(sS, s4)
        iv = sqx.bitcast(I32)
        ve.tensor_scalar(iv, iv, RADD, None, op0=ALU.add)
        ve.tensor_scalar(iv, iv, RMASK, None, op0=ALU.bitwise_and)
        ve.tensor_scalar_mul(sqx, sqx, stats[:, 13:14])
        v4 = sig(ve, "V")
        if _DEBUG_STOP == "C0":
            ve.wait_ge(sV, v4)
            sy.wait_ge(sV, v4)
            O.go(dbg["qx"][j * 128:(j + 1) * 128, :], sqx)
        te.wait_ge(sV, v4)
        tr_f32(sqx, M,
               lambda c, src, j=j: ve.tensor_copy(
                   qxT[:, c, j * 128:(j + 1) * 128], src))
        s_qx[j] = n["V"]
        sy.wait_ge(sV, s_qx[j])
        Dx1.go(x1_d[j * 128:(j + 1) * 128, :], x1_sb[:, j, :])

    if _DEBUG_STOP == "C0":
        return finish()

    tr_done = [0] * NIB
    st_done = [0] * NIC
    sil_done = [0] * NIC
    psum_free = {}
    ve.memset(mx_t, -3e38)
    ve.memset(mn_t, 3e38)
    Dx1.wait(te)
    Dx1.wait(sy)
    Dx1.wait(sc)
    Gq.wait_all(sy)
    Gq.wait_all(sc)
    for ib in range(NIB):
        par = ib % 2
        if ib >= 2:
            sy.wait_ge(sP, tr_done[ib - 2])
            sc.wait_ge(sP, tr_done[ib - 2])
        for hc in range(16):
            Dwg.go(wgT[:, par, hc, :],
                   qwg_d[ib * IBLK:(ib + 1) * IBLK, hc * 128:(hc + 1) * 128],
                   sub=par, transpose=True)
            Dwu.go(wuT[:, par, hc, :],
                   qwu_d[ib * IBLK:(ib + 1) * IBLK, hc * 128:(hc + 1) * 128],
                   sub=par, transpose=True)
        Dwg.wait(te, par)
        Dwu.wait(te, par)
        for q4 in range(4):
            ic = ib * 4 + q4
            bufs = (PA, PB, PC)
            gP, uP = bufs[(2 * ic) % 3], bufs[(2 * ic + 1) % 3]
            for X in (gP, uP):
                fr = psum_free.get(id(X))
                if fr is not None:
                    te.wait_ge(sems[fr[0]], fr[1])
            for hc in range(16):
                first, last = hc == 0, hc == 15
                for c0 in (0, 512):
                    te.matmul(gP[:, c0:c0 + 512],
                              wgT[:, par, hc, q4 * 128:(q4 + 1) * 128],
                              qxT[:, hc, c0:c0 + 512],
                              start=first, stop=last)
                    te.matmul(uP[:, c0:c0 + 512],
                              wuT[:, par, hc, q4 * 128:(q4 + 1) * 128],
                              qxT[:, hc, c0:c0 + 512],
                              start=first, stop=last)
            pg = sig(te, "P")
            if q4 == 3:
                tr_done[ib] = pg
            tpar = ic % 2
            sc.wait_ge(sP, pg)
            if ic >= 1:
                sc.wait_ge(sV, sil_done[ic - 1])
            sc.activation(gs_t, gP, AF.Silu, scale=wg_rs[:, ic:ic + 1])
            sg1 = sig(sc, "S")
            psum_free[id(gP)] = ("S", sg1)
            ve.wait_ge(sS, sg1)
            if ic >= 2:
                Dt.wait(ve, tpar)
            ve.tensor_mul(tstg[:, tpar, :], gs_t, uP)
            sil_done[ic] = n["V"] + 1
            ve.tensor_scalar_mul(tstg[:, tpar, :], tstg[:, tpar, :],
                                 wu_rs[:, ic:ic + 1])
            ve.tensor_max(mx_t, mx_t, tstg[:, tpar, :])
            ve.tensor_tensor(mn_t, mn_t, tstg[:, tpar, :], op=ALU.min)
            st_done[ic] = sig(ve, "V")
            psum_free[id(uP)] = ("V", st_done[ic])
            sy.wait_ge(sV, st_done[ic])
            Dt.go(tT_d[ic * 128:(ic + 1) * 128, :], tstg[:, tpar, :],
                  sub=tpar)

    if _DEBUG_STOP == "C1":
        Gq.wait_all(sy)
        Dw.wait_all(sy)
        for ic in range(NIC):
            O.wait(sy, 0)
            Dt.wait_all(sy)
            Ddbg.go(tg[:, 0, :], tT_d[ic * 128:(ic + 1) * 128, :])
            Ddbg.wait(sy)
            O.go(dbg["tT"][ic * 128:(ic + 1) * 128, :], tg[:, 0, :])
        return finish()

    ve.tensor_scalar_mul(mn_t, mn_t, -1.0)
    ve.tensor_max(mx_t, mx_t, mn_t)
    v_mx = sig(ve, "V")
    te.wait_ge(sV, v_mx)
    for jt in range(8):
        te.transpose(PDh[:, 0:128], mx_t[:, jt * 128:(jt + 1) * 128], identf)
        p_mx = pe_fence()
        ve.wait_ge(sP, p_mx)
        ve.reduce_max(rmx8[:, jt:jt + 1], PDh[:, 0:128], axis=AX.X)
        vr = sig(ve, "V")
        te.wait_ge(sV, vr)
    ve.reciprocal(stats[:, 20:28], rmx8)
    ve.nop()
    ve.tensor_scalar_mul(stats[:, 20:28], stats[:, 20:28], 448.0)
    ve.tensor_scalar_mul(rmx8, rmx8, 1.0 / 448.0)
    v_st = sig(ve, "V")
    te.wait_ge(sV, v_st)
    for jt in range(8):
        for which, dstP in ((0, PA), (1, PB)):
            src1 = stats[:, 20 + jt:21 + jt] if which == 0                 else rmx8[:, jt:jt + 1]
            te.transpose(PDh[0:1, 0:128], src1, identf)
            p1 = pe_fence()
            sc.wait_ge(sP, p1)
            sc.copy(row16[0:1, which, :], PDh[0:1, 0:128])
            s1 = sig(sc, "S")
            te.wait_ge(sS, s1)
            te.matmul(dstP[:, jt * 128:(jt + 1) * 128], ones_row,
                      row16[0:1, which, :], start=True, stop=True)
    p_bb = sig(te, "P")
    sc.wait_ge(sP, p_bb)
    sc.copy(st_b, PA)
    sc.copy(str_b, PB)
    s_b = sig(sc, "S")
    te.wait_ge(sS, s_b)

    Dt.wait_all(gp)
    Gq.wait_all(gp)
    Dw.wait_all(gp)
    vq_done = [0] * NIC
    for ic in range(NIC):
        par = ic % 2
        if ic >= 2:
            gp.wait_ge(sV, vq_done[ic - 2])
        Gg.gather(tg[:, par, :], tT_d, pi_sb[:, ic:ic + 1], sub=par)
        Gg.wait(ve, par)
        ve.wait_ge(sS, s_b)
        if ic >= 2:
            Dqw.wait(ve, par)
        ve.tensor_mul(tgq[:, par, :], tg[:, par, :], st_b)
        vq_done[ic] = n["V"] + 1
        iv = tgq[:, par, :].bitcast(I32)
        ve.tensor_scalar(iv, iv, RADD, None, op0=ALU.add)
        ve.tensor_scalar(iv, iv, RMASK, None, op0=ALU.bitwise_and)
        ve.tensor_mul(tgq[:, par, :], tgq[:, par, :], str_b)
        v1 = sig(ve, "V")
        sc.wait_ge(sV, v1)
        sc.copy(qt16[:, par, :], tgq[:, par, :])
        s1 = sig(sc, "S")
        sy.wait_ge(sS, s1)
        Dqw.go(qtT_d[ic * 128:(ic + 1) * 128, :], qt16[:, par, :], sub=par)

    if _DEBUG_STOP == "C3":
        for ic in range(NIC):
            O.wait(sc, 0)
            Dqw.wait_all(sy)
            Ddbg.go(qt16[:, 0, :], qtT_d[ic * 128:(ic + 1) * 128, :])
            Ddbg.wait(sc)
            sc.copy(tgq[:, 0, :], qt16[:, 0, :])
            s1 = sig(sc, "S")
            sy.wait_ge(sS, s1)
            O.go(dbg["qtT"][ic * 128:(ic + 1) * 128, :], tgq[:, 0, :])
        return finish()

    for tt in range(16):
        te.transpose(PDh[0:1, 0:128], wd_rs[:, tt:tt + 1], identf)
        p1 = pe_fence()
        sc.wait_ge(sP, p1)
        sc.copy(row16[0:1, 0, :], PDh[0:1, 0:128])
        s1 = sig(sc, "S")
        te.wait_ge(sS, s1)
        te.matmul((PA, PB)[tt // 8][:, (tt % 8) * 128:(tt % 8 + 1) * 128],
                  ones_row, row16[0:1, 0, :], start=True, stop=True)
    p_cb = sig(te, "P")
    sc.wait_ge(sP, p_cb)
    sc.copy(csc_b[:, 0:1024], PA)
    sc.copy(csc_b[:, 1024:2048], PB)
    s_cb = sig(sc, "S")
    te.wait_ge(sS, s_cb)
    Dqw.wait_all(sy)
    sy.wait_ge(sS, s_cb)
    Gq.wait_all(sc)

    for jh in range(2):
        for mcol in range(4):
            dn_done = [0] * NIC
            v_evprev = n["V"]
            sc.wait_ge(sP, n["P"])
            for ic in range(NIC):
                par = ic % 2
                if ic >= 2:
                    sy.wait_ge(sP, dn_done[ic - 2])
                    sc.wait_ge(sP, dn_done[ic - 2])
                Dqs.go(qts[:, par, 0:512],
                       qtT_d[ic * 128:(ic + 1) * 128,
                             jh * 512:(jh + 1) * 512], sub=par)
                Dwd.go(wdTs[:, par, :],
                       qwd_d[mcol * 512:(mcol + 1) * 512,
                             ic * 128:(ic + 1) * 128],
                       sub=par, transpose=True)
                Dqs.wait(te, par)
                Dwd.wait(te, par)
                if ic == 0:
                    te.wait_ge(sV, v_evprev)
                for jj in range(4):
                    te.matmul(jslice4(jj),
                              qts[:, par, jj * 128:(jj + 1) * 128],
                              wdTs[:, par, :],
                              start=(ic == 0), stop=(ic == NIC - 1))
                dn_done[ic] = sig(te, "P")
            p_dn = dn_done[NIC - 1]
            for jj in range(4):
                j = jh * 4 + jj
                par = jj % 2
                Dxs.go(x1st[:, par, :],
                       x1_d[j * 128:(j + 1) * 128,
                            mcol * 512:(mcol + 1) * 512], sub=par)
                Dxs.wait(ve, par)
                ve.wait_ge(sP, p_dn)
                O.wait(ve, par)
                ve.tensor_mul(yev[:, par, :], jslice4(jj),
                              csc_b[:, mcol * 512:(mcol + 1) * 512])
                ve.tensor_add(yev[:, par, :], yev[:, par, :], x1st[:, par, :])
                v1 = sig(ve, "V")
                sy.wait_ge(sV, v1)
                O.go(out_d[j * 128:(j + 1) * 128,
                           mcol * 512:(mcol + 1) * 512],
                     yev[:, par, :], sub=par)
    return finish()


def kernel(L0_attn_x, L0_attn_w_aq, L0_attn_w_ak, L0_attn_w_av, L0_attn_w_ao,
           L0_ffn_w_gate, L0_ffn_w_up, L0_ffn_w_down,
           L0_ffn_up_reorder_index, L0_ffn_down_reorder_index):
    x = np.ascontiguousarray(np.asarray(L0_attn_x, dtype=np.float32))
    waq = np.ascontiguousarray(np.asarray(L0_attn_w_aq, dtype=np.float32))
    wak = np.ascontiguousarray(np.asarray(L0_attn_w_ak, dtype=np.float32))
    wav = np.ascontiguousarray(np.asarray(L0_attn_w_av, dtype=np.float32))
    wao = np.ascontiguousarray(np.asarray(L0_attn_w_ao, dtype=np.float32))
    wg = np.ascontiguousarray(np.asarray(L0_ffn_w_gate, dtype=np.float32))
    wu = np.ascontiguousarray(np.asarray(L0_ffn_w_up, dtype=np.float32))
    wd = np.ascontiguousarray(np.asarray(L0_ffn_w_down, dtype=np.float32))
    pi = np.asarray(L0_ffn_down_reorder_index).astype(np.int32)
    pi_dev = np.ascontiguousarray(pi.reshape(NIC, 128).T)

    nc, _ = build_program()
    in_maps = []
    perms = {}
    for c in range(8):
        b, p = c // 2, c % 2
        tokperm, cosF, sinFs, mkt, swap, identm = _host_tables(p)
        perms[c] = tokperm
        in_maps.append({
            "xb": np.ascontiguousarray(x[b][tokperm]),
            "w_aq": waq, "w_ak": wak, "w_av": wav, "w_ao": wao,
            "wg": wg, "wu": wu, "wd": wd, "pi": pi_dev,
            "ropecos": np.ascontiguousarray(cosF),
            "ropesin": np.ascontiguousarray(sinFs),
            "mk": np.ascontiguousarray(mkt),
            "ident": identm, "identf": np.eye(128, dtype=np.float32),
            "swapm": swap,
        })
    res = run_bass_kernel_spmd(nc, in_maps, core_ids=list(range(8)))
    out = np.empty((B, T, M), np.float32)
    for c in range(8):
        out[c // 2][perms[c][:NTOK]] = res.results[c]["out"]
    return out


# version 07
# speedup vs baseline: 1.2009x; 1.0172x over previous
import numpy as np
import ml_dtypes
import concourse.bass as bass
import concourse.mybir as mybir
from concourse.bass_utils import run_bass_kernel_spmd
from concourse.bass_core_rust import IndirectOffsetOnAxis

F32 = mybir.dt.float32
BF16 = mybir.dt.bfloat16
I32 = mybir.dt.int32
U8 = mybir.dt.uint8
AF = mybir.ActivationFunctionType
ALU = mybir.AluOpType
AX = mybir.AxisListType

B, T, M, NH, DH = 4, 2048, 2048, 16, 128
II = 5632
KT = T // 128
NJ = 8
NTOK = NJ * 128
NIC = II // 128
IBLK = 512
NIB = II // IBLK
RADD = 0x80000
RMASK = -0x100000

_DEBUG_STOP = None
_DEBUG_HEADS = NH
_DEBUG_TILES = KT
_DEBUG_SUB = 3
_DEBUG_BSUB = 9
_STRICT = [False]
_DEPGUARD = [True]


def _host_tables(p):
    own = [2 * j + p for j in range(NJ)]
    other = [t for t in range(KT) if t not in own]
    tokperm = np.concatenate([np.arange(t * 128, (t + 1) * 128)
                              for t in own + other])
    pos = tokperm.astype(np.float64)
    dims = np.arange(DH // 2, dtype=np.float64)
    ang = np.power(10000.0, -dims / (DH // 2))
    rad = pos[None, :] * ang[:, None]
    cos = np.cos(rad)
    sin = np.sin(rad)
    cosF = np.concatenate([cos, cos], 0).astype(ml_dtypes.bfloat16)
    sinFs = np.concatenate([-sin, sin], 0).astype(ml_dtypes.bfloat16)
    tri = (np.arange(128)[:, None] <= np.arange(128)[None, :]).astype(np.float32)
    mk = np.empty((KT, 128, 128), np.float32)
    for m in range(KT):
        mk[m] = tri if m < NJ else (0.0 if p == 0 else 1.0)
    swap = np.zeros((128, 128), np.float32)
    swap[:64, 64:] = np.eye(64)
    swap[64:, :64] = np.eye(64)
    return (tokperm, cosF, sinFs, mk.astype(ml_dtypes.bfloat16),
            swap.astype(ml_dtypes.bfloat16),
            np.eye(128, dtype=ml_dtypes.bfloat16))


def build_program():
    nc = bass.Bass("TRN2", target_bir_lowering=False, debug=False)

    xb = nc.dram_tensor("xb", [T, M], F32, kind="ExternalInput").ap()
    w_aq = nc.dram_tensor("w_aq", [NH, M, DH], F32, kind="ExternalInput").ap()
    w_ak = nc.dram_tensor("w_ak", [NH, M, DH], F32, kind="ExternalInput").ap()
    w_av = nc.dram_tensor("w_av", [NH, M, DH], F32, kind="ExternalInput").ap()
    w_ao = nc.dram_tensor("w_ao", [NH, DH, M], F32, kind="ExternalInput").ap()
    wg_d = nc.dram_tensor("wg", [II, M], F32, kind="ExternalInput").ap()
    wu_d = nc.dram_tensor("wu", [II, M], F32, kind="ExternalInput").ap()
    wd_d = nc.dram_tensor("wd", [M, II], F32, kind="ExternalInput").ap()
    pi_d = nc.dram_tensor("pi", [128, NIC], I32, kind="ExternalInput").ap()
    cos_d = nc.dram_tensor("ropecos", [128, T], BF16, kind="ExternalInput").ap()
    sin_d = nc.dram_tensor("ropesin", [128, T], BF16, kind="ExternalInput").ap()
    mk_d = nc.dram_tensor("mk", [KT, 128, 128], BF16, kind="ExternalInput").ap()
    id_d = nc.dram_tensor("ident", [128, 128], BF16, kind="ExternalInput").ap()
    sw_d = nc.dram_tensor("swapm", [128, 128], BF16, kind="ExternalInput").ap()
    idf_d = nc.dram_tensor("identf", [128, 128], F32, kind="ExternalInput").ap()

    out_d = nc.dram_tensor("out", [NTOK, M], F32, kind="ExternalOutput").ap()

    qwg_d = nc.dram_tensor("qwg", [II, M], BF16, kind="Internal").ap()
    qwu_d = nc.dram_tensor("qwu", [II, M], BF16, kind="Internal").ap()
    qwd_d = nc.dram_tensor("qwd", [M, II], BF16, kind="Internal").ap()
    tT_d = nc.dram_tensor("tT", [II, NTOK], F32, kind="Internal").ap()
    qtT_d = nc.dram_tensor("qtT", [II, NTOK], BF16, kind="Internal").ap()
    x1_d = nc.dram_tensor("x1s", [NTOK, M], F32, kind="Internal").ap()

    dbg = {}
    if _DEBUG_STOP is not None:
        for nm, shp in [("lnxT", [16, 128, T]), ("kT", [128, T]),
                        ("vTok", [KT, 128, 128]), ("qT", [128, NTOK]),
                        ("oT", [128, NH * NTOK]), ("x1", [NTOK, M]),
                        ("qx", [NTOK, M]), ("tT", [II, NTOK]),
                        ("qtT", [II, NTOK]), ("st", [128, 32]),
                        ("lnA", [128, M])]:
            dbg[nm] = nc.dram_tensor("dbg_" + nm, shp, F32,
                                     kind="ExternalOutput").ap()
        dbg["lnraw"] = nc.dram_tensor("dbg_lnraw", [16, 128, T], BF16,
                                      kind="ExternalOutput").ap()

    def sba(name, shape, dt):
        return nc.alloc_sbuf_tensor(name, list(shape), dt).ap()

    ident = sba("identc", [128, 128], BF16)
    identf = sba("identfc", [128, 128], F32)
    swapm = sba("swapc", [128, 128], BF16)
    mk = sba("mkc", [128, KT, 128], BF16)
    cosF = sba("cosc", [128, T], BF16)
    sinFs = sba("sinc", [128, T], BF16)
    pi_sb = sba("pic", [128, NIC], I32)
    ones_col = sba("onescol", [128, 1], BF16)
    ones_row = sba("onesrow", [1, 128], BF16)
    wg_sc = sba("wgsc", [128, NIC], F32)
    wu_sc = sba("wusc", [128, NIC], F32)
    wd_sc = sba("wdsc", [128, 16], F32)
    wg_rs = sba("wgrs", [128, NIC], F32)
    wu_rs = sba("wurs", [128, NIC], F32)
    wd_rs = sba("wdrs", [128, 16], F32)
    stats = sba("stats", [128, 32], F32)
    rmx8 = sba("rmx8", [128, 8], F32)
    row16 = sba("row16", [1, 2, 128], BF16)
    negc = sba("negc", [128, 1], F32)
    cm16 = sba("cm16", [128, 2], BF16)
    kT = sba("kT", [128, T], BF16)
    kvraw = sba("kvraw", [128, T], BF16)
    kTr, vT = kvraw, kvraw
    qTr = kvraw[:, 0:NTOK]
    vTok2 = sba("vTok2", [128, KT, 256], BF16)
    qT = sba("qT", [128, NTOK], BF16)
    pt = sba("pt", [128, 2, NTOK], BF16)

    def region(name, nbytes):
        return nc.alloc_sbuf_tensor(name, [128, nbytes], U8).ap()

    def view(reg, o, shape, dt):
        esz = {F32: 4, BF16: 2, I32: 4}[dt]
        r = reg.bitcast(dt)
        flat = 1
        for s in shape[1:]:
            flat *= s
        v = r[:, o // esz:o // esz + flat]
        if len(shape) == 2:
            return v
        pat = "p (" + " ".join(f"d{i}" for i in range(1, len(shape))) + ") ->" \
              + " p " + " ".join(f"d{i}" for i in range(1, len(shape)))
        kw = {f"d{i}": shape[i] for i in range(1, len(shape) - 1)}
        return v.rearrange(pat, **kw)

    RG1 = region("RG1", 65536)
    lnxT = view(RG1, 0, [128, 16, T], BF16)
    x1_sb = view(RG1, 0, [128, NJ, M], F32)
    wgT = view(RG1, 0, [128, 2, 16, IBLK], BF16)
    wuT = view(RG1, 32768, [128, 2, 16, IBLK], BF16)
    RG2 = region("RG2", 32768)
    oT = view(RG2, 0, [128, NH, NTOK], BF16)
    qxT = view(RG2, 0, [128, 16, NTOK], BF16)
    RG3 = region("RG3", 36864)
    xst = view(RG3, 0, [128, 2, M], F32)
    lnA = view(RG3, 16384, [128, M], F32)
    wq16 = view(RG3, 0, [128, 2, 16, DH], BF16)
    wk16 = view(RG3, 8192, [128, 2, 16, DH], BF16)
    wv2 = view(RG3, 16384, [128, 16, 2, 128], BF16)
    wv2f = view(RG3, 16384, [128, 16, 256], BF16)
    linv = view(RG3, 24576, [128, NTOK], F32)
    rowB = view(RG3, 28672, [1, 1024], F32)
    rowA16 = view(RG3, 32768, [1, 1024], BF16)
    rowA = view(RG3, 34816, [1, 512], F32)
    wao16 = view(RG3, 0, [128, 2, 512], BF16)
    xres = view(RG3, 2048, [128, 2, 512], F32)
    sqx = view(RG3, 0, [128, M], F32)
    lnf = view(RG3, 8192, [128, M], F32)
    tstg = view(RG3, 16384, [128, 2, NTOK], F32)
    gs_t = view(RG3, 24576, [128, NTOK], F32)
    mx_t = view(RG3, 28672, [128, NTOK], F32)
    mn_t = view(RG3, 32768, [128, NTOK], F32)
    lnsq = view(RG3, 24576, [128, M], F32)
    st_b = view(RG3, 28672, [128, NTOK], F32)
    str_b = view(RG3, 32768, [128, NTOK], F32)
    csc_b = view(RG3, 16384, [128, M], F32)
    RG4 = region("RG4", 28672)
    wst = view(RG4, 0, [128, 2, 2816], F32)
    tg = view(RG4, 0, [128, 2, NTOK], F32)
    tgq = view(RG4, 8192, [128, 2, NTOK], F32)
    qt16 = view(RG4, 16384, [128, 2, NTOK], BF16)
    wdTs = view(RG4, 20480, [128, 2, 512], BF16)
    x1st = view(RG4, 22528, [128, 2, 512], F32)
    qts = view(RG4, 24576, [128, 2, NTOK], BF16)
    yev = view(RG4, 0, [128, 2, 512], F32)

    PA = nc.alloc_psum_tensor("PA", [128, 1024], F32).ap()
    PB = nc.alloc_psum_tensor("PB", [128, 1024], F32).ap()
    PC = nc.alloc_psum_tensor("PC", [128, 1024], F32).ap()
    PDh = nc.alloc_psum_tensor("PDh", [128, 512], F32).ap()
    PF = nc.alloc_psum_tensor("PF", [1, 128], F32).ap()

    _GUARD = {"tensor_copy", "tensor_tensor", "tensor_add", "tensor_sub",
              "tensor_mul", "tensor_max", "tensor_scalar",
              "tensor_single_scalar", "tensor_scalar_add",
              "tensor_scalar_mul", "tensor_scalar_max", "tensor_scalar_min",
              "tensor_reduce", "reduce_max", "reduce_sum", "reciprocal",
              "memset", "activation", "square", "sqrt", "copy", "mul",
              "add", "select"}

    _ESZ = {F32: 4, BF16: 2, I32: 4, U8: 1}

    def _rng(ap):
        try:
            esz = _ESZ.get(ap.dtype, 4)
            span = 1
            for i, (st_, ct) in enumerate(ap.ap):
                if i == 0:
                    continue
                span += (ct - 1) * abs(st_)
            lo = ap.offset * esz
            return (ap.tensor.name, lo, lo + span * esz)
        except Exception:
            try:
                return (ap.tensor.name, 0, 1 << 60)
            except Exception:
                return None

    class _EG:
        def __init__(self, e):
            self._e = e
            self.last = None
            self._hist = []

        def __getattr__(self, name):
            f = getattr(self._e, name)
            if name in _GUARD:
                def g(*a, **k):
                    aps = [x for x in list(a) + list(k.values())
                           if hasattr(x, "tensor") and hasattr(x, "offset")]
                    outr = _rng(aps[0]) if aps else None
                    fs = aps[0].free_size() if aps else 1024
                    need_drain = need_nop = False
                    for idx, (pw, pfs) in enumerate(self._hist):
                        for x in aps[1:]:
                            rr = _rng(x)
                            if rr is None or (pw is not None
                                              and rr[0] == pw[0]
                                              and rr[1] < pw[2]
                                              and pw[1] < rr[2]):
                                if fs < 512 or pfs < 512:
                                    need_drain = True
                                elif idx == 0:
                                    need_nop = True
                    if not _DEPGUARD[0]:
                        need_drain = fs < 512
                        need_nop = True
                    if _STRICT[0] or need_drain:
                        self._e.drain()
                        self._hist.clear()
                    elif need_nop:
                        self._e.nop()
                    r = f(*a, **k)
                    self.last = r
                    self._hist.insert(0, (outr, fs))
                    del self._hist[3:]
                    return r
                return g
            return f

    class _TG:
        def __init__(self, e):
            self._e = e
            self.last = None

        def __getattr__(self, name):
            f = getattr(self._e, name)
            if name in ("transpose", "matmul"):
                def g(*a, **k):
                    r = f(*a, **k)
                    self.last = r
                    return r
                return g
            return f

    sy = nc.sync
    gp = _EG(nc.gpsimd)
    ve = _EG(nc.vector)
    sc = _EG(nc.scalar)
    te = _TG(nc.tensor)
    sems = {k: nc.alloc_semaphore(k) for k in ("P", "V", "S")}
    n = {k: 0 for k in sems}
    sP, sV, sS = (sems[k] for k in ("P", "V", "S"))

    class DStream:
        def __init__(self, name, nsub=1, eng=None):
            self.sems = [nc.alloc_semaphore(name + str(i))
                         for i in range(nsub)]
            self.n = [0] * nsub
            self.eng = eng if eng is not None else sy

        def go(self, out, in_, sub=0, transpose=False):
            if transpose:
                r = self.eng.dma_start_transpose(out, in_)
            else:
                r = self.eng.dma_start(out=out, in_=in_)
            r.then_inc(self.sems[sub], 16)
            self.n[sub] += 16

        def gather(self, out, in_, idx, sub=0):
            self.eng.indirect_dma_start(
                out, None, in_, IndirectOffsetOnAxis(idx, 0)) \
                .then_inc(self.sems[sub], 16)
            self.n[sub] += 16

        def wait(self, eng, sub=0):
            eng.wait_ge(self.sems[sub], self.n[sub])

        def wait_all(self, eng):
            for i, s in enumerate(self.sems):
                eng.wait_ge(s, self.n[i])

    Dc = DStream("Dc")
    Dx = DStream("Dx", 2)
    Dw = DStream("Dw", 2)
    Gq = DStream("Gq", 2, gp)
    Gw = DStream("Gw", 2, gp)
    Gv = DStream("Gv", 1, gp)
    Ga = DStream("Ga", 2, gp)
    Dxr = DStream("Dxr", 2)
    Dx1 = DStream("Dx1")
    Dwg = DStream("Dwg", 2)
    Dwu = DStream("Dwu", 2)
    Dt = DStream("Dt", 2)
    Gg = DStream("Gg", 2, gp)
    Dqw = DStream("Dqw", 2)
    Dqs = DStream("Dqs", 2)
    Dwd = DStream("Dwd", 2)
    Dxs = DStream("Dxs", 2)
    O = DStream("O", 2)
    Ddbg = DStream("Ddbg")
    streams = [Dc, Dx, Dw, Gq, Gw, Gv, Ga, Dxr, Dx1, Dwg, Dwu, Dt, Gg,
               Dqw, Dqs, Dwd, Dxs, O, Ddbg]

    def sig(eng, key):
        eng.last.then_inc(sems[key], 1)
        n[key] += 1
        return n[key]

    def finish():
        for e in (sy, gp, ve, sc, te):
            O.wait_all(e)
        return nc, dbg

    for st_ in streams:
        for s in st_.sems:
            sy.sem_clear(s)
    for s in sems.values():
        sy.sem_clear(s)
    nc.all_engine_barrier()
    Dc.go(ident, id_d)
    Dc.go(swapm, sw_d)
    Dc.go(identf, idf_d)
    Dc.go(mk, mk_d.rearrange("k p q -> p k q"))
    Dc.go(cosF, cos_d)
    Dc.go(sinFs, sin_d)
    Dc.go(pi_sb, pi_d)
    Dc.wait(ve)
    if _DEBUG_STOP is not None:
        for rg_ in (RG1, RG2, RG3, RG4):
            ve.memset(rg_.bitcast(F32), 0.0)
        ve.memset(stats, 0.0)
        ve.memset(rmx8, 0.0)
    ve.memset(ones_col, 1.0)
    ve.memset(ones_row, 1.0)
    v0 = sig(ve, "V")
    for e in (te, sc, gp, sy):
        e.wait_ge(sV, v0)

    def pe_fence():
        return sig(te, "P")

    def tr_f32(src_f32, ncols, write_fn):
        nt = ncols // 128
        done = 0
        while done < nt:
            batch = min(16, nt - done)
            for k in range(batch):
                c = done + k
                P, o = (PA, PB)[k // 8], (k % 8) * 128
                te.transpose(P[:, o:o + 128],
                             src_f32[:, c * 128:(c + 1) * 128], identf)
            p1 = pe_fence()
            ve.wait_ge(sP, p1)
            for k in range(batch):
                c = done + k
                P, o = (PA, PB)[k // 8], (k % 8) * 128
                write_fn(c, P[:, o:o + 128])
            v1 = sig(ve, "V")
            te.wait_ge(sV, v1)
            done += batch

    lnsq_free = [0]
    sqrt_sig = [0]

    def ln_stats(src_ap):
        ve.wait_ge(sS, sqrt_sig[0])
        ve.reduce_sum(stats[:, 0:1], src_ap, axis=AX.X)
        sc.wait_ge(sV, lnsq_free[0])
        sc.square(lnsq, src_ap)
        s1 = sig(sc, "S")
        ve.wait_ge(sS, s1)
        ve.reduce_sum(stats[:, 1:2], lnsq, axis=AX.X)
        lnsq_free[0] = n["V"] + 1
        ve.tensor_scalar_mul(stats[:, 2:3], stats[:, 0:1], 1.0 / M)
        ve.tensor_scalar_mul(stats[:, 3:4], stats[:, 1:2], 1.0 / M)
        ve.tensor_mul(stats[:, 4:5], stats[:, 2:3], stats[:, 2:3])
        ve.tensor_sub(stats[:, 5:6], stats[:, 3:4], stats[:, 4:5])
        ve.tensor_scalar_add(stats[:, 5:6], stats[:, 5:6], 1e-5)
        v1 = sig(ve, "V")
        sc.wait_ge(sV, v1)
        sc.sqrt(stats[:, 6:7], stats[:, 5:6])
        s2 = sig(sc, "S")
        sqrt_sig[0] = s2
        ve.wait_ge(sS, s2)
        ve.reciprocal(stats[:, 7:8], stats[:, 6:7])
        v2 = sig(ve, "V")
        sc.wait_ge(sV, v2)


    s_lnuse = [0] * KT
    for i in range(_DEBUG_TILES):
        par = i % 2
        if i >= 2:
            sy.wait_ge(sV, s_lnuse[i - 2])
        Dx.go(xst[:, par, :], xb[i * 128:(i + 1) * 128, :], sub=par)
        Dx.wait(ve, par)
        Dx.wait(sc, par)
        if _DEBUG_SUB >= 1:
            ln_stats(xst[:, par, :])
        if _DEBUG_SUB >= 2:
            if i >= 1:
                ve.wait_ge(sP, p_lnA[i - 1])
            ve.tensor_scalar(lnA, xst[:, par, :], stats[:, 2:3],
                             stats[:, 7:8], op0=ALU.subtract, op1=ALU.mult)
            s_lnuse[i] = sig(ve, "V")
        if _DEBUG_SUB >= 3:
            te.wait_ge(sV, s_lnuse[i])
            tr_f32(lnA, M,
                   lambda c, src, i=i: ve.tensor_copy(
                       lnxT[:, c, i * 128:(i + 1) * 128], src))
        if i == 0:
            p_lnA = [0] * KT
        p_lnA[i] = n["P"]
    v_endA = n["V"]

    if _DEBUG_STOP == "A":
        sy.wait_ge(sV, v_endA)
        O.go(dbg["st"], stats)
        O.go(dbg["lnA"], lnA)
        for c in range(16):
            O.go(dbg["lnraw"][c], lnxT[:, c, :])
        for c in range(16):
            sc.wait_ge(sV, v_endA)
            O.wait(sc, 0)
            sc.copy(lnf, lnxT[:, c, :])
            s1 = sig(sc, "S")
            sy.wait_ge(sS, s1)
            O.go(dbg["lnxT"][c], lnf)
        return finish()

    w_units = []
    for t in range(II // 128):
        w_units.append(("g", t))
        w_units.append(("u", t))
    for t in range(16):
        w_units.append(("dmax", t))
        for q in range(4):
            w_units.append(("d", t, q))
    wpar = [0]
    w_free = {0: None, 1: None}
    w_dmax_v = {}

    def w_wait_free(par):
        g = w_free[par]
        if g is None:
            return
        if g[0] == "V":
            sy.wait_ge(sV, g[1])
        else:
            Gq.wait(sy, par)

    def w_round_store(par, width, scale_col, dst):
        sc.activation(wst[:, par, 0:width], wst[:, par, 0:width], AF.Copy,
                      scale=scale_col)
        s1 = sig(sc, "S")
        ve.wait_ge(sS, s1)
        iv = wst[:, par, 0:width].bitcast(I32)
        ve.tensor_scalar(iv, iv, RADD, None, op0=ALU.add)
        ve.tensor_scalar(iv, iv, RMASK, None, op0=ALU.bitwise_and)
        v2 = sig(ve, "V")
        gp.wait_ge(sV, v2)
        Gq.go(dst, wst[:, par, 0:width], sub=par)
        w_free[par] = ("Gq", par)

    def emit_w_unit(u):
        if u[0] in ("g", "u"):
            _, t = u
            src, dst = (wg_d, qwg_d) if u[0] == "g" else (wu_d, qwu_d)
            scl = wg_sc if u[0] == "g" else wu_sc
            rsl = wg_rs if u[0] == "g" else wu_rs
            par = wpar[0] % 2
            wpar[0] += 1
            w_wait_free(par)
            Dw.go(wst[:, par, 0:2048], src[t * 128:(t + 1) * 128, :], sub=par)
            Dw.wait(ve, par)
            ve.reduce_max(stats[:, 10:11], wst[:, par, 0:2048], axis=AX.X,
                          apply_absolute_value=True)
            ve.reciprocal(stats[:, 11:12], stats[:, 10:11])
            ve.tensor_scalar_mul(scl[:, t:t + 1], stats[:, 11:12], 448.0)
            ve.tensor_scalar_mul(rsl[:, t:t + 1], stats[:, 10:11], 1 / 448.0)
            v1 = sig(ve, "V")
            sc.wait_ge(sV, v1)
            w_round_store(par, 2048, scl[:, t:t + 1],
                          dst[t * 128:(t + 1) * 128, :])
        elif u[0] == "dmax":
            _, t = u
            for qq in range(4):
                par = wpar[0] % 2
                wpar[0] += 1
                w_wait_free(par)
                Dw.go(wst[:, par, 0:1408],
                      wd_d[t * 128:(t + 1) * 128, qq * 1408:(qq + 1) * 1408],
                      sub=par)
                Dw.wait(ve, par)
                ve.reduce_max(stats[:, 12 + qq:13 + qq], wst[:, par, 0:1408],
                              axis=AX.X, apply_absolute_value=True)
                w_free[par] = ("V", sig(ve, "V"))
            ve.tensor_max(stats[:, 12:13], stats[:, 12:13], stats[:, 13:14])
            ve.tensor_max(stats[:, 14:15], stats[:, 14:15], stats[:, 15:16])
            ve.tensor_max(stats[:, 12:13], stats[:, 12:13], stats[:, 14:15])
            ve.reciprocal(stats[:, 13:14], stats[:, 12:13])
            ve.tensor_scalar_mul(wd_sc[:, t:t + 1], stats[:, 13:14], 448.0)
            ve.tensor_scalar_mul(wd_rs[:, t:t + 1], stats[:, 12:13], 1 / 448.0)
            w_dmax_v[t] = sig(ve, "V")
        else:
            _, t, q = u
            par = wpar[0] % 2
            wpar[0] += 1
            w_wait_free(par)
            Dw.go(wst[:, par, 0:1408],
                  wd_d[t * 128:(t + 1) * 128, q * 1408:(q + 1) * 1408],
                  sub=par)
            Dw.wait(sc, par)
            sc.wait_ge(sV, w_dmax_v[t])
            w_round_store(par, 1408, wd_sc[:, t:t + 1],
                          qwd_d[t * 128:(t + 1) * 128,
                                q * 1408:(q + 1) * 1408])

    p_qproj_prev = [0]

    def fetch_head_w(h):
        hp = h % 2
        if h >= 2:
            gp.wait_ge(sP, p_qproj_prev[0])
        Gw.go(wq16[:, hp, :, :],
              w_aq[h].rearrange("(c p) d -> p c d", p=128), sub=hp)
        Gw.go(wk16[:, hp, :, :],
              w_ak[h].rearrange("(c p) d -> p c d", p=128), sub=hp)

    def fetch_wv2(h):
        Gv.go(wv2[:, :, 0, :], w_av[h].rearrange("(c p) d -> p c d", p=128))
        Gv.go(wv2[:, :, 1, :],
              w_av[h + 1].rearrange("(c p) d -> p c d", p=128))

    gp.wait_ge(sV, v_endA)
    fetch_head_w(0)
    WPH = (len(w_units) + NH - 1) // NH

    def proj(dst16, wsb, hp, psum, ofs):
        for mc in range(16):
            first, last = mc == 0, mc == 15
            for c0 in (0, 512):
                te.matmul(psum[:, c0:c0 + 512], wsb[:, hp, mc, :],
                          lnxT[:, mc, ofs + c0:ofs + c0 + 512],
                          start=first, stop=last)
        p1 = sig(te, "P")
        sc.wait_ge(sP, p1)
        sc.copy(dst16[:, ofs:ofs + 1024], psum[:, 0:1024])
        s1 = sig(sc, "S")
        te.wait_ge(sS, s1)
        return s1

    s_v2 = 0
    p_v2mm = 0
    pv_done_prev = 0
    for h in range(_DEBUG_HEADS):
        hp = h % 2
        if h + 1 < NH:
            fetch_head_w(h + 1)
        Gw.wait(te, hp)
        if hp == 0 and _DEBUG_BSUB >= 1:
            if h >= 2:
                gp.wait_ge(sP, p_v2mm)
            fetch_wv2(h)
            Gv.wait(te)
            te.wait_ge(sS, s_v2)
            if h >= 2:
                te.wait_ge(sP, pv_done_prev)
            sv2_evac = [0] * KT
            for tt in range(KT):
                VP = PDh if tt % 2 == 0 else PC
                if tt >= 2:
                    te.wait_ge(sS, sv2_evac[tt - 2])
                for mc in range(16):
                    te.matmul(VP[:, 0:256],
                              lnxT[:, mc, tt * 128:(tt + 1) * 128],
                              wv2f[:, mc, :],
                              start=(mc == 0), stop=(mc == 15))
                p1 = sig(te, "P")
                sc.wait_ge(sP, p1)
                sc.copy(vTok2[:, tt, :], VP[:, 0:256])
                sv2_evac[tt] = sig(sc, "S")
            s_v2 = sv2_evac[KT - 1]
            p_v2mm = n["P"]
            te.wait_ge(sS, s_v2)
        if _DEBUG_BSUB < 2:
            continue
        proj(kTr, wk16, hp, PA, 0)
        sk1 = proj(kTr, wk16, hp, PB, 1024)
        for c0 in range(0, T, 512):
            te.matmul((PA if c0 < 1024 else PB)[:, c0 % 1024:c0 % 1024 + 512],
                      swapm, kTr[:, c0:c0 + 512], start=True, stop=True)
        pswap = sig(te, "P")
        ve.wait_ge(sS, sk1)
        ve.tensor_mul(kT, kTr, cosF)
        ve.wait_ge(sP, pswap)
        ve.tensor_mul(kTr[:, 0:1024], PA, sinFs[:, 0:1024])
        ve.tensor_mul(kTr[:, 1024:2048], PB, sinFs[:, 1024:2048])
        ve.tensor_add(kT, kT, kTr)
        ve.reduce_max(stats[:, 16:17], kT, axis=AX.X,
                      apply_absolute_value=True)
        vk = sig(ve, "V")
        te.wait_ge(sV, vk)
        if _DEBUG_BSUB < 3:
            continue
        sc.wait_ge(sV, vk)
        sq0 = proj(qTr, wq16, hp, PA, 0)
        p_qproj_prev[0] = n["P"]
        for c0 in (0, 512):
            te.matmul(PA[:, c0:c0 + 512], swapm, qTr[:, c0:c0 + 512],
                      start=True, stop=True)
        pswq = sig(te, "P")
        ve.wait_ge(sS, sq0)
        ve.tensor_mul(qT, qTr, cosF[:, 0:NTOK])
        ve.wait_ge(sP, pswq)
        ve.tensor_mul(qTr, PA, sinFs[:, 0:NTOK])
        ve.tensor_add(qT, qT, qTr)
        ve.reduce_max(stats[:, 17:18], qT, axis=AX.X,
                      apply_absolute_value=True)
        vq = sig(ve, "V")
        te.wait_ge(sV, vq)
        sc.wait_ge(sV, vq)
        sc.copy(cm16[:, 0:1], stats[:, 16:17])
        sc.copy(cm16[:, 1:2], stats[:, 17:18])
        scm = sig(sc, "S")
        te.wait_ge(sS, scm)
        te.matmul(PA[0:1, 0:1], cm16[:, 0:1], cm16[:, 1:2],
                  start=True, stop=True)
        pc1 = sig(te, "P")
        sc.wait_ge(sP, pc1)
        sc.copy(rowA16[0:1, 0:1], PA[0:1, 0:1])
        sc1 = sig(sc, "S")
        te.wait_ge(sS, sc1)
        te.matmul(PA[:, 0:1], ones_row, rowA16[0:1, 0:1],
                  start=True, stop=True)
        pc2 = sig(te, "P")
        sc.wait_ge(sP, pc2)
        sc.activation(negc, PA[:, 0:1], AF.Copy, scale=-1.0 / 128.0)
        snegc = sig(sc, "S")
        te.wait_ge(sS, snegc)

        if _DEBUG_BSUB < 4:
            continue
        exp_done = [0] * KT
        pv_done = [0] * KT
        vm1 = [0] * KT

        def q0_of(m):
            return (m if m < NJ else m - NJ) * 128

        def emit_pv(m):
            q0 = q0_of(m)
            te.wait_ge(sV, vm1[m])
            c0 = q0
            while c0 < NTOK:
                ce = min((c0 // 512 + 1) * 512, NTOK)
                stop = (m == 11) if c0 < 512 else (m == 15)
                te.matmul(PA[0:1, c0:ce], ones_col, pt[:, m % 2, c0:ce],
                          start=(m == 0), stop=stop)
                te.matmul(PB[:, c0:ce],
                          vTok2[:, m, hp * 128:(hp + 1) * 128],
                          pt[:, m % 2, c0:ce],
                          start=(m == 0), stop=stop)
                c0 = ce
            pv_done[m] = sig(te, "P")

        for m in range(KT):
            q0 = q0_of(m)
            SP = PC
            if m >= 1:
                te.wait_ge(sS, exp_done[m - 1])
            c0 = q0
            while c0 < NTOK:
                ce = min((c0 // 512 + 1) * 512, NTOK)
                te.matmul(SP[:, c0:ce], kT[:, m * 128:(m + 1) * 128],
                          qT[:, c0:ce], start=True, stop=True)
                c0 = ce
            ps = sig(te, "P")
            sc.wait_ge(sP, ps)
            if m >= 2:
                sc.wait_ge(sP, pv_done[m - 2])
            sc.activation(pt[:, m % 2, q0:NTOK], SP[:, q0:NTOK], AF.Exp,
                          bias=negc, scale=1.0 / 128.0)
            exp_done[m] = sig(sc, "S")
            ve.wait_ge(sS, exp_done[m])
            ve.tensor_mul(pt[:, m % 2, q0:q0 + 128],
                          pt[:, m % 2, q0:q0 + 128], mk[:, m, :])
            vm1[m] = sig(ve, "V")
            if m >= 1:
                emit_pv(m - 1)
        emit_pv(KT - 1)
        pv_done_prev = pv_done[KT - 1]
        if _DEBUG_BSUB < 5:
            continue
        sc.wait_ge(sP, pv_done[KT - 1])
        sc.copy(rowA[0:1, 0:512], PA[0:1, 0:512])
        sc.copy(rowB[0:1, 0:512], PA[0:1, 512:1024])
        sl1 = sig(sc, "S")
        ve.wait_ge(sS, sl1)
        ve.reciprocal(rowB[0:1, 512:1024], rowB[0:1, 0:512])
        ve.reciprocal(rowA[0:1, 0:512], rowA[0:1, 0:512])
        ve.tensor_copy(rowA16[0:1, 0:512], rowA[0:1, 0:512])
        ve.tensor_copy(rowA16[0:1, 512:1024], rowB[0:1, 512:1024])
        vl1 = sig(ve, "V")
        te.wait_ge(sV, vl1)
        te.matmul(PC[:, 0:512], ones_row, rowA16[0:1, 0:512],
                  start=True, stop=True)
        te.matmul(PC[:, 512:1024], ones_row, rowA16[0:1, 512:1024],
                  start=True, stop=True)
        pl1 = sig(te, "P")
        sc.wait_ge(sP, pl1)
        sc.copy(linv, PC)
        sl2 = sig(sc, "S")
        ve.wait_ge(sS, sl2)
        ve.tensor_mul(oT[:, h, :], PB, linv)
        voT = sig(ve, "V")
        te.wait_ge(sV, voT)
        pv_done_prev = pv_done[KT - 1]
        for u in w_units[h * WPH:(h + 1) * WPH]:
            emit_w_unit(u)

    if _DEBUG_STOP == "B":
        v = n["V"]
        sc.wait_ge(sV, v)
        Gw.wait_all(sc)
        Gv.wait_all(sc)
        sc.copy(sqx[:, 0:T], kT)
        s1 = sig(sc, "S")
        sy.wait_ge(sS, s1)
        O.go(dbg["kT"], sqx[:, 0:T])
        for c in range(KT):
            O.wait(sc, 0)
            sc.copy(lnf[:, 0:128], vTok2[:, c, 0:128] if _DEBUG_HEADS % 2
                    else vTok2[:, c, 128:256])
            s1 = sig(sc, "S")
            sy.wait_ge(sS, s1)
            O.go(dbg["vTok"][c], lnf[:, 0:128])
        O.wait(sc, 0)
        sc.copy(sqx[:, 0:NTOK], qT)
        s1 = sig(sc, "S")
        sy.wait_ge(sS, s1)
        O.go(dbg["qT"], sqx[:, 0:NTOK])
        for hh in range(_DEBUG_HEADS):
            O.wait(sc, 0)
            sc.copy(lnf[:, 0:NTOK], oT[:, hh, :])
            s1 = sig(sc, "S")
            sy.wait_ge(sS, s1)
            O.go(dbg["oT"][:, hh * NTOK:(hh + 1) * NTOK], lnf[:, 0:NTOK])
        return finish()

    def jslice4(jj):
        return (PA, PB)[jj // 2][:, (jj % 2) * 512:(jj % 2) * 512 + 512]

    p_b2pre = n["P"]
    waoall = (view(RG3, 0, [128, 16, 512], BF16),
              view(RG3, 20480, [128, 16, 512], BF16))
    xresB = view(RG3, 16384, [128, 2, 512], F32)
    v_ev = {0: 0, 1: 0}
    p_blk = [p_b2pre, p_b2pre]
    gp.wait_ge(sP, p_b2pre)
    sy.wait_ge(sP, p_b2pre)
    te.wait_ge(sP, p_b2pre)
    for mcol in range(4):
        par = mcol % 2
        buf = waoall[par]
        if mcol >= 2:
            gp.wait_ge(sP, p_blk[par])
        for cch in range(16):
            Ga.go(buf[:, cch, :],
                  w_ao[cch, :, mcol * 512:(mcol + 1) * 512], sub=par)
        Ga.wait(te, par)
        for jh in range(2):
            for cch in range(16):
                for jj in range(4):
                    j = jh * 4 + jj
                    te.matmul(jslice4(jj), oT[:, cch, j * 128:(j + 1) * 128],
                              buf[:, cch, :],
                              start=(cch == 0), stop=(cch == 15))
            pj = sig(te, "P")
            for jj in range(4):
                j = jh * 4 + jj
                rpar = jj % 2
                sy.wait_ge(sV, v_ev[rpar])
                Dxr.go(xresB[:, rpar, :],
                       xb[j * 128:(j + 1) * 128,
                          mcol * 512:(mcol + 1) * 512], sub=rpar)
                Dxr.wait(ve, rpar)
                ve.wait_ge(sP, pj)
                ve.tensor_add(x1_sb[:, j, mcol * 512:(mcol + 1) * 512],
                              jslice4(jj), xresB[:, rpar, :])
                v_ev[rpar] = sig(ve, "V")
            te.wait_ge(sV, v_ev[rpar])
        p_blk[par] = n["P"]
    v_x1 = n["V"]

    if _DEBUG_STOP == "B2":
        sy.wait_ge(sV, v_x1)
        for j in range(NJ):
            O.go(dbg["x1"][j * 128:(j + 1) * 128, :], x1_sb[:, j, :])
        return finish()

    s_qx = [0] * NJ
    for j in range(NJ):
        ve.wait_ge(sV, v_x1)
        sc.wait_ge(sV, v_x1)
        ln_stats(x1_sb[:, j, :])
        ve.tensor_scalar(lnf, x1_sb[:, j, :], stats[:, 2:3],
                         stats[:, 7:8], op0=ALU.subtract, op1=ALU.mult)
        ve.reduce_max(stats[:, 10:11], lnf, axis=AX.X,
                      apply_absolute_value=True)
        ve.reciprocal(stats[:, 11:12], stats[:, 10:11])
        ve.tensor_scalar_mul(stats[:, 12:13], stats[:, 11:12], 448.0)
        ve.tensor_scalar_mul(stats[:, 13:14], stats[:, 10:11], 1.0 / 448.0)
        v3 = sig(ve, "V")
        sc.wait_ge(sV, v3)
        if _DEBUG_STOP == "C0":
            O.wait(sc, 0)
        sc.activation(sqx, lnf, AF.Copy, scale=stats[:, 12:13])
        s4 = sig(sc, "S")
        ve.wait_ge(sS, s4)
        iv = sqx.bitcast(I32)
        ve.tensor_scalar(iv, iv, RADD, None, op0=ALU.add)
        ve.tensor_scalar(iv, iv, RMASK, None, op0=ALU.bitwise_and)
        ve.tensor_scalar_mul(sqx, sqx, stats[:, 13:14])
        v4 = sig(ve, "V")
        if _DEBUG_STOP == "C0":
            ve.wait_ge(sV, v4)
            sy.wait_ge(sV, v4)
            O.go(dbg["qx"][j * 128:(j + 1) * 128, :], sqx)
        te.wait_ge(sV, v4)
        tr_f32(sqx, M,
               lambda c, src, j=j: ve.tensor_copy(
                   qxT[:, c, j * 128:(j + 1) * 128], src))
        s_qx[j] = n["V"]
        sy.wait_ge(sV, s_qx[j])
        Dx1.go(x1_d[j * 128:(j + 1) * 128, :], x1_sb[:, j, :])

    if _DEBUG_STOP == "C0":
        return finish()

    tr_done = [0] * NIB
    st_done = [0] * NIC
    sil_done = [0] * NIC
    psum_free = {}
    ve.memset(mx_t, -3e38)
    ve.memset(mn_t, 3e38)
    Dx1.wait(te)
    Dx1.wait(sy)
    Dx1.wait(sc)
    Gq.wait_all(sy)
    Gq.wait_all(sc)
    for ib in range(NIB):
        par = ib % 2
        if ib >= 2:
            sy.wait_ge(sP, tr_done[ib - 2])
            sc.wait_ge(sP, tr_done[ib - 2])
        for hc in range(16):
            Dwg.go(wgT[:, par, hc, :],
                   qwg_d[ib * IBLK:(ib + 1) * IBLK, hc * 128:(hc + 1) * 128],
                   sub=par, transpose=True)
            Dwu.go(wuT[:, par, hc, :],
                   qwu_d[ib * IBLK:(ib + 1) * IBLK, hc * 128:(hc + 1) * 128],
                   sub=par, transpose=True)
        Dwg.wait(te, par)
        Dwu.wait(te, par)
        for q4 in range(4):
            ic = ib * 4 + q4
            bufs = (PA, PB, PC)
            gP, uP = bufs[(2 * ic) % 3], bufs[(2 * ic + 1) % 3]
            for X in (gP, uP):
                fr = psum_free.get(id(X))
                if fr is not None:
                    te.wait_ge(sems[fr[0]], fr[1])
            for hc in range(16):
                first, last = hc == 0, hc == 15
                for c0 in (0, 512):
                    te.matmul(gP[:, c0:c0 + 512],
                              wgT[:, par, hc, q4 * 128:(q4 + 1) * 128],
                              qxT[:, hc, c0:c0 + 512],
                              start=first, stop=last)
                    te.matmul(uP[:, c0:c0 + 512],
                              wuT[:, par, hc, q4 * 128:(q4 + 1) * 128],
                              qxT[:, hc, c0:c0 + 512],
                              start=first, stop=last)
            pg = sig(te, "P")
            if q4 == 3:
                tr_done[ib] = pg
            tpar = ic % 2
            sc.wait_ge(sP, pg)
            if ic >= 1:
                sc.wait_ge(sV, sil_done[ic - 1])
            sc.activation(gs_t, gP, AF.Silu, scale=wg_rs[:, ic:ic + 1])
            sg1 = sig(sc, "S")
            psum_free[id(gP)] = ("S", sg1)
            ve.wait_ge(sS, sg1)
            if ic >= 2:
                Dt.wait(ve, tpar)
            ve.tensor_mul(tstg[:, tpar, :], gs_t, uP)
            sil_done[ic] = n["V"] + 1
            ve.tensor_scalar_mul(tstg[:, tpar, :], tstg[:, tpar, :],
                                 wu_rs[:, ic:ic + 1])
            ve.tensor_max(mx_t, mx_t, tstg[:, tpar, :])
            ve.tensor_tensor(mn_t, mn_t, tstg[:, tpar, :], op=ALU.min)
            st_done[ic] = sig(ve, "V")
            psum_free[id(uP)] = ("V", st_done[ic])
            sy.wait_ge(sV, st_done[ic])
            Dt.go(tT_d[ic * 128:(ic + 1) * 128, :], tstg[:, tpar, :],
                  sub=tpar)

    if _DEBUG_STOP == "C1":
        Gq.wait_all(sy)
        Dw.wait_all(sy)
        for ic in range(NIC):
            O.wait(sy, 0)
            Dt.wait_all(sy)
            Ddbg.go(tg[:, 0, :], tT_d[ic * 128:(ic + 1) * 128, :])
            Ddbg.wait(sy)
            O.go(dbg["tT"][ic * 128:(ic + 1) * 128, :], tg[:, 0, :])
        return finish()

    ve.tensor_scalar_mul(mn_t, mn_t, -1.0)
    ve.tensor_max(mx_t, mx_t, mn_t)
    v_mx = sig(ve, "V")
    te.wait_ge(sV, v_mx)
    for jt in range(8):
        te.transpose(PDh[:, 0:128], mx_t[:, jt * 128:(jt + 1) * 128], identf)
        p_mx = pe_fence()
        ve.wait_ge(sP, p_mx)
        ve.reduce_max(rmx8[:, jt:jt + 1], PDh[:, 0:128], axis=AX.X)
        vr = sig(ve, "V")
        te.wait_ge(sV, vr)
    ve.reciprocal(stats[:, 20:28], rmx8)
    ve.nop()
    ve.tensor_scalar_mul(stats[:, 20:28], stats[:, 20:28], 448.0)
    ve.tensor_scalar_mul(rmx8, rmx8, 1.0 / 448.0)
    v_st = sig(ve, "V")
    te.wait_ge(sV, v_st)
    for jt in range(8):
        te.transpose(PDh[0:1, 0:128], stats[:, 20 + jt:21 + jt], identf)
        p1 = pe_fence()
        sc.wait_ge(sP, p1)
        sc.copy(row16[0:1, 0, :], PDh[0:1, 0:128])
        s1 = sig(sc, "S")
        te.wait_ge(sS, s1)
        te.matmul(PA[:, jt * 128:(jt + 1) * 128], ones_row,
                  row16[0:1, 0, :], start=True, stop=True)
    p_bb = sig(te, "P")
    sc.wait_ge(sP, p_bb)
    sc.copy(st_b, PA)
    s_b = sig(sc, "S")
    te.wait_ge(sS, s_b)

    Dt.wait_all(sc)
    for tt in range(16):
        te.transpose(PDh[0:1, 0:128], wd_rs[:, tt:tt + 1], identf)
        p1 = pe_fence()
        sc.wait_ge(sP, p1)
        sc.copy(row16[0:1, 0, :], PDh[0:1, 0:128])
        s1 = sig(sc, "S")
        te.wait_ge(sS, s1)
        te.matmul((PA, PB)[tt // 8][:, (tt % 8) * 128:(tt % 8 + 1) * 128],
                  ones_row, row16[0:1, 0, :], start=True, stop=True)
    p_cb = sig(te, "P")
    sc.wait_ge(sP, p_cb)
    sc.copy(csc_b[:, 0:1024], PA)
    sc.copy(csc_b[:, 1024:2048], PB)
    s_cb = sig(sc, "S")
    te.wait_ge(sS, s_cb)

    Dt.wait_all(gp)
    Gq.wait_all(gp)
    Dw.wait_all(gp)
    vq_done = [0] * NIC
    wlist = [0] * NIC
    for ic in range(NIC):
        par = ic % 2
        if ic >= 2:
            gp.wait_ge(sV, vq_done[ic - 2])
        Gg.gather(tg[:, par, :], tT_d, pi_sb[:, ic:ic + 1], sub=par)
        Gg.wait(ve, par)
        ve.wait_ge(sS, s_b)
        if ic >= 2:
            Dqw.wait(ve, par)
        ve.tensor_mul(tgq[:, par, :], tg[:, par, :], st_b)
        vq_done[ic] = n["V"] + 1
        iv = tgq[:, par, :].bitcast(I32)
        ve.tensor_scalar(iv, iv, RADD, None, op0=ALU.add)
        ve.tensor_scalar(iv, iv, RMASK, None, op0=ALU.bitwise_and)
        v1 = sig(ve, "V")
        sc.wait_ge(sV, v1)
        sc.copy(qt16[:, par, :], tgq[:, par, :])
        s1 = sig(sc, "S")
        sy.wait_ge(sS, s1)
        Dqw.go(qtT_d[ic * 128:(ic + 1) * 128, :], qt16[:, par, :], sub=par)
        wlist[ic] = Dqw.n[par]

    if _DEBUG_STOP == "C3":
        for ic in range(NIC):
            O.wait(sc, 0)
            Dqw.wait_all(sy)
            Ddbg.go(qt16[:, 0, :], qtT_d[ic * 128:(ic + 1) * 128, :])
            Ddbg.wait(sc)
            sc.copy(tgq[:, 0, :], qt16[:, 0, :])
            s1 = sig(sc, "S")
            sy.wait_ge(sS, s1)
            O.go(dbg["qtT"][ic * 128:(ic + 1) * 128, :], tgq[:, 0, :])
        return finish()

    yev2 = view(RG4, 4096, [128, 2, 512], F32)
    sy.wait_ge(sS, s_cb)
    Gq.wait_all(sy)
    yev2_use = {0: 0, 1: 0}
    for jh in range(2):
        for mcol in range(4):
            first_blk = jh == 0 and mcol == 0
            dn_done = [0] * NIC
            s_evprev = n["S"]
            for ic in range(NIC):
                par = ic % 2
                if ic >= 2:
                    sy.wait_ge(sP, dn_done[ic - 2])
                if first_blk:
                    sy.wait_ge(Dqw.sems[par], wlist[ic])
                Dqs.go(qts[:, par, 0:512],
                       qtT_d[ic * 128:(ic + 1) * 128,
                             jh * 512:(jh + 1) * 512], sub=par)
                Dwd.go(wdTs[:, par, :],
                       qwd_d[mcol * 512:(mcol + 1) * 512,
                             ic * 128:(ic + 1) * 128],
                       sub=par, transpose=True)
                Dqs.wait(te, par)
                Dwd.wait(te, par)
                if ic == 0:
                    te.wait_ge(sS, s_evprev)
                for jj in range(4):
                    te.matmul(jslice4(jj),
                              qts[:, par, jj * 128:(jj + 1) * 128],
                              wdTs[:, par, :],
                              start=(ic == 0), stop=(ic == NIC - 1))
                dn_done[ic] = sig(te, "P")
            p_dn = dn_done[NIC - 1]
            for jj in range(4):
                j = jh * 4 + jj
                par = jj % 2
                Dxs.go(x1st[:, par, :],
                       x1_d[j * 128:(j + 1) * 128,
                            mcol * 512:(mcol + 1) * 512], sub=par)
                sc.wait_ge(sP, p_dn)
                sc.wait_ge(sV, yev2_use[par])
                sc.activation(yev2[:, par, :], jslice4(jj), AF.Copy,
                              scale=rmx8[:, j:j + 1])
                s_y = sig(sc, "S")
                Dxs.wait(ve, par)
                ve.wait_ge(sS, s_y)
                O.wait(ve, par)
                ve.tensor_mul(yev[:, par, :], yev2[:, par, :],
                              csc_b[:, mcol * 512:(mcol + 1) * 512])
                ve.tensor_add(yev[:, par, :], yev[:, par, :], x1st[:, par, :])
                v1 = sig(ve, "V")
                yev2_use[par] = v1
                sy.wait_ge(sV, v1)
                O.go(out_d[j * 128:(j + 1) * 128,
                           mcol * 512:(mcol + 1) * 512],
                     yev[:, par, :], sub=par)
    return finish()


def kernel(L0_attn_x, L0_attn_w_aq, L0_attn_w_ak, L0_attn_w_av, L0_attn_w_ao,
           L0_ffn_w_gate, L0_ffn_w_up, L0_ffn_w_down,
           L0_ffn_up_reorder_index, L0_ffn_down_reorder_index):
    x = np.ascontiguousarray(np.asarray(L0_attn_x, dtype=np.float32))
    waq = np.ascontiguousarray(np.asarray(L0_attn_w_aq, dtype=np.float32))
    wak = np.ascontiguousarray(np.asarray(L0_attn_w_ak, dtype=np.float32))
    wav = np.ascontiguousarray(np.asarray(L0_attn_w_av, dtype=np.float32))
    wao = np.ascontiguousarray(np.asarray(L0_attn_w_ao, dtype=np.float32))
    wg = np.ascontiguousarray(np.asarray(L0_ffn_w_gate, dtype=np.float32))
    wu = np.ascontiguousarray(np.asarray(L0_ffn_w_up, dtype=np.float32))
    wd = np.ascontiguousarray(np.asarray(L0_ffn_w_down, dtype=np.float32))
    pi = np.asarray(L0_ffn_down_reorder_index).astype(np.int32)
    pi_dev = np.ascontiguousarray(pi.reshape(NIC, 128).T)

    nc, _ = build_program()
    in_maps = []
    perms = {}
    for c in range(8):
        b, p = c // 2, c % 2
        tokperm, cosF, sinFs, mkt, swap, identm = _host_tables(p)
        perms[c] = tokperm
        in_maps.append({
            "xb": np.ascontiguousarray(x[b][tokperm]),
            "w_aq": waq, "w_ak": wak, "w_av": wav, "w_ao": wao,
            "wg": wg, "wu": wu, "wd": wd, "pi": pi_dev,
            "ropecos": np.ascontiguousarray(cosF),
            "ropesin": np.ascontiguousarray(sinFs),
            "mk": np.ascontiguousarray(mkt),
            "ident": identm, "identf": np.eye(128, dtype=np.float32),
            "swapm": swap,
        })
    res = run_bass_kernel_spmd(nc, in_maps, core_ids=list(range(8)))
    out = np.empty((B, T, M), np.float32)
    for c in range(8):
        out[c // 2][perms[c][:NTOK]] = res.results[c]["out"]
    return out
